# version 09
# baseline (speedup 1.0000x reference)
import math
import contextlib
import numpy as np
import ml_dtypes
import concourse.bass as bass
import concourse.mybir as mybir
from concourse.bass_utils import run_bass_kernel_spmd

F32 = mybir.dt.float32
BF16 = mybir.dt.bfloat16
AF = mybir.ActivationFunctionType
ALU = mybir.AluOpType
AX = mybir.AxisListType

P = 128
D = 2048
T = 4096
OWN = 2048
DFF = 1408
NRE = 16
NSH = 2
NEXP = NRE + NSH
NPAIR = 16
HD = 64
VD = 128
LAMBDA_INIT = 0.8 - 0.6 * math.exp(-0.3 * 12)
EPS = 1e-5
ESLOT = [12, 16, 28, 32]
NMASK = 12
NCHIP = 8

FC_W1A = 6
W1A = FC_W1A * P
W1B = DFF - W1A
FC_W1B = 5


class Ctx:

    NAMES = ["T", "V", "S", "G"]

    def __init__(self, nc, es):
        self.nc = nc
        self.es = es
        self.sem = {n: es.enter_context(nc.semaphore("sem_" + n)) for n in self.NAMES}
        self.cnt = {n: 0 for n in self.NAMES}

    def inc(self, instr, name, n=1):
        instr.then_inc(self.sem[name], n)
        self.cnt[name] += n
        return self.cnt[name]

    def incd(self, instr, key):
        if key not in self.sem:
            self.sem[key] = self.es.enter_context(self.nc.semaphore("d_" + key))
            self.cnt[key] = 0
        return self.inc(instr, key, 16)

    def wait(self, eng, name, val):
        if val > 0 and name in self.sem:
            eng.wait_ge(self.sem[name], val)

    def now(self, name):
        return self.cnt.get(name, 0)


def build_program(debug=False):
    nc = bass.Bass("TRN2", target_bir_lowering=False, debug=False)

    xseq = nc.dram_tensor("xseq", [T, D], F32, kind="ExternalInput").ap()
    xown = nc.dram_tensor("xown", [OWN, D], F32, kind="ExternalInput").ap()
    wqT = nc.dram_tensor("wqT", [D, D], F32, kind="ExternalInput").ap()
    wkT = nc.dram_tensor("wkT", [D, 512], F32, kind="ExternalInput").ap()
    wvT = nc.dram_tensor("wvT", [D, 512], F32, kind="ExternalInput").ap()
    woT = nc.dram_tensor("woT", [D, D], F32, kind="ExternalInput").ap()
    centT = nc.dram_tensor("centT", [D, NRE], F32, kind="ExternalInput").ap()
    rw1 = nc.dram_tensor("rw1", [NRE, D, DFF], F32, kind="ExternalInput").ap()
    rw2 = nc.dram_tensor("rw2", [NRE, DFF, D], F32, kind="ExternalInput").ap()
    sw1 = nc.dram_tensor("sw1", [NSH, D, DFF], F32, kind="ExternalInput").ap()
    sw2 = nc.dram_tensor("sw2", [NSH, DFF, D], F32, kind="ExternalInput").ap()
    b1all = nc.dram_tensor("b1all", [P, NEXP * 11], F32, kind="ExternalInput").ap()
    b2r = nc.dram_tensor("b2r", [NRE, D], F32, kind="ExternalInput").ap()
    sb2 = nc.dram_tensor("sb2", [NSH, D], F32, kind="ExternalInput").ap()
    biasbc = nc.dram_tensor("biasbc", [P, NRE], F32, kind="ExternalInput").ap()
    lnwbc = nc.dram_tensor("lnwbc", [P, VD], F32, kind="ExternalInput").ap()
    lamvec = nc.dram_tensor("lamvec", [1, 256], F32, kind="ExternalInput").ap()
    ident_in = nc.dram_tensor("ident", [P, P], BF16, kind="ExternalInput").ap()
    eoh_in = nc.dram_tensor("eoh", [NRE, NRE * P], BF16, kind="ExternalInput").ap()
    mask01 = nc.dram_tensor("mask01", [4, NMASK, P, 512], BF16, kind="ExternalInput").ap()

    yout = nc.dram_tensor("yout", [OWN, D], F32, kind="ExternalOutput").ap()

    q_scr = nc.dram_tensor("q_scr", [NPAIR, P, OWN], BF16, kind="Internal").ap()
    k_scr = nc.dram_tensor("k_scr", [4, P, T], BF16, kind="Internal").ap()
    v_scr = nc.dram_tensor("v_scr", [32, P, 4 * 129], BF16, kind="Internal").ap()
    xmid_scr = nc.dram_tensor("xmid_scr", [16, P, D], F32, kind="Internal").ap()
    if debug:
        dbg_xmid = nc.dram_tensor("dbg_xmid", [16, P, D], F32, kind="ExternalOutput").ap()
        dbg_q = nc.dram_tensor("dbg_q", [NPAIR, P, OWN], F32, kind="ExternalOutput").ap()
        dbg_k = nc.dram_tensor("dbg_k", [4, P, T], F32, kind="ExternalOutput").ap()
        dbg_v = nc.dram_tensor("dbg_v", [32, P, 4 * 129], F32, kind="ExternalOutput").ap()

    es_glob = contextlib.ExitStack()
    C = Ctx(nc, es_glob)
    sb = lambda name, shape, dt=F32: es_glob.enter_context(nc.sbuf_tensor(name, shape, dt))

    ident = sb("g_ident", [P, P], BF16)
    identf = sb("g_identf", [P, P], F32)
    lamv = sb("g_lamv", [1, 256], F32)
    lprod = sb("g_lprod", [1, 64], F32)
    lsc = sb("g_lsc", [1, 4], F32)
    lambc = sb("g_lambc", [P, 1], F32)
    lnw = sb("g_lnw", [P, VD], F32)
    epsap = sb("g_epsap", [P, 1], F32)
    zrow = sb("g_zrow", [1, P], BF16)
    orow = sb("g_orow", [1, 512], BF16)
    onesf = sb("g_onesf", [1, P], F32)
    ones2 = sb("g_ones2", [NSH, 512], BF16)

    with nc.Block("ph0"):
        pass

    with nc.Block("ph_const") as blk:
        def _(eng):
            tok = C.incd(eng.dma_start(out=ident[:], in_=ident_in), "c0")
            C.incd(eng.dma_start(out=lamv[:], in_=lamvec), "c0")
            C.incd(eng.dma_start(out=lnw[:], in_=lnwbc), "c0")
            return tok
        blk.sync(_)

        def _(eng):
            eng.memset(epsap[:], EPS)
            eng.memset(zrow[:], 0.0)
            eng.memset(orow[:], 1.0)
            eng.memset(onesf[:], 1.0)
            C.inc(eng.memset(ones2[:], 1.0), "G")
        blk.gpsimd(_)

        def _(eng):
            C.wait(eng, "c0", 48)
            C.inc(eng.tensor_copy(out=identf[:], in_=ident[:]), "V")
            t1 = C.inc(eng.tensor_tensor(out=lprod[:], in0=lamv[:, 0:64],
                                         in1=lamv[:, 64:128], op=ALU.mult), "V")
            C.wait(eng, "V", t1)
            t2 = C.inc(eng.reduce_sum(out=lsc[:, 0:1], in_=lprod[:], axis=AX.X), "V")
            C.wait(eng, "V", t2)
            t3 = C.inc(eng.tensor_tensor(out=lprod[:], in0=lamv[:, 128:192],
                                         in1=lamv[:, 192:256], op=ALU.mult), "V")
            C.wait(eng, "V", t3)
            C.inc(eng.reduce_sum(out=lsc[:, 1:2], in_=lprod[:], axis=AX.X), "V")
            C.inc(eng.tensor_scalar(out=lnw[:], in0=lnw[:], scalar1=1.0 - LAMBDA_INIT,
                                    scalar2=None, op0=ALU.mult), "V")
        blk.vector(_)

        def _(eng):
            C.wait(eng, "V", C.now("V"))
            t = C.inc(eng.activation(out=lsc[:, 2:3], in_=lsc[:, 0:1], func=AF.Exp), "S")
            C.wait(eng, "S", t)
            t = C.inc(eng.activation(out=lsc[:, 3:4], in_=lsc[:, 1:2], func=AF.Exp), "S")
            C.wait(eng, "S", t)
        blk.scalar(_)

        def _(eng):
            C.wait(eng, "S", C.now("S"))
            t = C.inc(eng.tensor_scalar(out=lsc[:, 0:1], in0=lsc[:, 2:3],
                                        scalar1=lsc[:, 3:4], scalar2=LAMBDA_INIT,
                                        op0=ALU.subtract, op1=ALU.add), "V")
            C.wait(eng, "V", t)
            t = C.inc(eng.tensor_copy(out=lsc[:, 1:2], in_=lsc[:, 0:1]), "V")
            C.wait(eng, "V", t)
        blk.vector(_)

        with nc.psum_tensor("ph0_bc", [P, 2], F32) as bc_ps:
            def _(eng):
                C.wait(eng, "V", C.now("V"))
                C.wait(eng, "G", C.now("G"))
                C.inc(eng.matmul(out=bc_ps[:], lhsT=onesf[:], rhs=lsc[:, 0:2],
                                 start=True, stop=True), "T")
            blk.tensor(_)

            def _(eng):
                C.wait(eng, "T", C.now("T"))
                C.inc(eng.tensor_copy(out=lambc[:], in_=bc_ps[:, 0:1]), "V")
            blk.vector(_)

    es1 = contextlib.ExitStack()
    s1 = lambda name, shape, dt: es1.enter_context(nc.sbuf_tensor(name, shape, dt))
    p1t = lambda name, shape, dt=F32: es1.enter_context(nc.psum_tensor(name, shape, dt))

    wq_sb = s1("p1_wq", [P, 16 * D], BF16)
    wk_sb = s1("p1_wk", [P, 16 * 512], BF16)
    wv_sb = s1("p1_wv", [P, 16 * 512], BF16)
    xt = [s1(f"p1_xt{i}", [P, D], F32) for i in range(4)]
    xlnbf = [s1(f"p1_xlnbf{i}", [P, D], BF16) for i in range(2)]
    xfm = s1("p1_xfm", [P, 16 * 512], BF16)
    bn6 = [s1(f"p1_bn6{i}", [P, 24], F32) for i in range(4)]
    mvg = s1("p1_mvg", [P, 8], F32)
    rstd4 = s1("p1_rstd4", [P, 4], F32)
    lnscr = s1("p1_lnscr", [P, 4], F32)
    kstage = [s1(f"p1_kst{i}", [P, 512], BF16) for i in range(2)]
    vstage = [s1(f"p1_vst{i}", [P, 4 * 129], BF16) for i in range(2)]
    qstage = [s1(f"p1_qst{i}", [P, 512], BF16) for i in range(2)]

    psT = [p1t(f"p1_psT{i}", [P, 1024], BF16) for i in range(2)]
    q_ps = [p1t(f"p1_qps{i}", [P, 512]) for i in range(2)]
    k_ps = q_ps
    v_ps = [p1t(f"p1_vps{i}", [P, 512]) for i in range(2)]

    with nc.Block("ph1") as blk:
        def _(eng):
            C.incd(eng.dma_start(out=wq_sb.ap().rearrange("p (c f) -> p c f", c=16),
                                 in_=wqT.rearrange("(c p) f -> p c f", p=P)), "w1x")
            C.incd(eng.dma_start(out=wk_sb.ap().rearrange("p (c f) -> p c f", c=16),
                                 in_=wkT.rearrange("(c p) f -> p c f", p=P)), "w1x")
            C.incd(eng.dma_start(out=wv_sb.ap().rearrange("p (c f) -> p c f", c=16),
                                 in_=wvT.rearrange("(c p) f -> p c f", p=P)), "w1x")
            for i in range(2):
                for kv in range(4):
                    ins = eng.memset(vstage[i][:, kv * 129 + 128:kv * 129 + 129], 1.0)
            C.inc(ins, "G")
        blk.gpsimd(_)
        tok_w = C.now("w1x")
        tok_vst = C.now("G")

        xt_free = [0, 0, 0, 0]
        xln_free = [0, 0]
        psT_free = [0, 0]
        kst_free = [0, 0]
        vst_free = [0, 0]
        qst_free = [0, 0]
        qkv_ps_free = [0, 0]
        xfm_free = 0

        def ln_group(src_tensor, g):
            stat_toks = []
            for t in range(4):
                tile = g * 4 + t
                buf = t

                def _(eng, tile=tile, buf=buf):
                    C.wait(eng, "V", xt_free[buf])
                    C.incd(eng.dma_start(out=xt[buf][:],
                                         in_=src_tensor[tile * P:(tile + 1) * P, :]),
                           f"xt{buf}")
                blk.sync(_)
                tok_x = C.now(f"xt{buf}")

                def _(eng, t=t, buf=buf, tok_x=tok_x):
                    C.wait(eng, f"xt{buf}", tok_x)
                    for i in range(4):
                        ins = eng.bn_stats(out=bn6[buf][:, 6 * i:6 * i + 6],
                                           in_=xt[buf][:, 512 * i:512 * (i + 1)])
                    tv = C.inc(ins, "V")
                    C.wait(eng, "V", tv)
                    C.inc(eng.bn_aggr(out=mvg[:, t * 2:t * 2 + 2], in_=bn6[buf][:]), "V")
                blk.vector(_)
                stat_toks.append(C.now("V"))

            def _(eng, tok=stat_toks[-1]):
                C.wait(eng, "V", tok)
                vr = mvg.ap().rearrange("p (t two) -> p t two", two=2)[:, :, 1:2]
                ts_ = C.inc(eng.activation(out=lnscr[:, 0:4], in_=vr,
                                           func=AF.Ln, bias=epsap[:]), "S")
                C.wait(eng, "S", ts_)
                C.inc(eng.activation(out=rstd4[:, 0:4], in_=lnscr[:, 0:4],
                                     func=AF.Exp, scale=-0.5), "S")
            blk.scalar(_)
            tok_rs = C.now("S")

            for t in range(4):
                buf = t

                def _(eng, t=t, buf=buf, tok_rs=tok_rs):
                    C.wait(eng, "S", tok_rs)
                    C.wait(eng, "T", xln_free[buf % 2])
                    C.inc(eng.tensor_scalar(out=xlnbf[buf % 2][:], in0=xt[buf][:],
                                            scalar1=mvg[:, t * 2:t * 2 + 1],
                                            scalar2=rstd4[:, t:t + 1],
                                            op0=ALU.subtract, op1=ALU.mult), "V")
                blk.vector(_)
                xt_free[buf] = C.now("V")
                tok_xln = C.now("V")

                def _(eng, buf=buf, tok_xln=tok_xln):
                    C.wait(eng, "V", tok_xln)
                    C.wait(eng, "V", psT_free[0])
                    for c in range(8):
                        ins = eng.transpose(out=psT[0][:, c * 128:(c + 1) * 128],
                                            in_=xlnbf[buf % 2][:, c * 128:(c + 1) * 128],
                                            identity=ident[:])
                    C.inc(ins, "T")
                    C.wait(eng, "V", psT_free[1])
                    for c in range(8):
                        ins = eng.transpose(out=psT[1][:, c * 128:(c + 1) * 128],
                                            in_=xlnbf[buf % 2][:, (8 + c) * 128:(9 + c) * 128],
                                            identity=ident[:])
                    C.inc(ins, "T")
                blk.tensor(_)
                xln_free[buf % 2] = C.now("T")
                tok_tr = C.now("T")

                def _(eng, t=t, tok_tr=tok_tr):
                    xfm3d = xfm.ap().rearrange("p (c f) -> p c f", f=512)
                    C.wait(eng, "T", tok_tr - 1)
                    C.wait(eng, "T", xfm_free)
                    dst0 = xfm3d[:, 0:8, t * 128:(t + 1) * 128]
                    C.inc(eng.tensor_copy(out=dst0, in_=psT[0].ap().rearrange(
                        "p (c f) -> p c f", f=128)), "V")
                    psT_free[0] = C.now("V")
                    C.wait(eng, "T", tok_tr)
                    dst1 = xfm3d[:, 8:16, t * 128:(t + 1) * 128]
                    C.inc(eng.tensor_copy(out=dst1, in_=psT[1].ap().rearrange(
                        "p (c f) -> p c f", f=128)), "V")
                    psT_free[1] = C.now("V")
                blk.vector(_)
            return C.now("V")

        for g in range(8):
            tok_xfm = ln_group(xseq, g)

            for fb in range(4):
                pb = fb % 2

                def _(eng, fb=fb, pb=pb):
                    C.wait(eng, "V", tok_xfm)
                    C.wait(eng, "w1x", tok_w)
                    C.wait(eng, "S", qkv_ps_free[pb])
                    for dc in range(16):
                        ins = eng.matmul(out=k_ps[pb][:],
                                         lhsT=wk_sb[:, dc * 512 + fb * 128:dc * 512 + (fb + 1) * 128],
                                         rhs=xfm[:, dc * 512:(dc + 1) * 512],
                                         start=(dc == 0), stop=(dc == 15))
                    C.inc(ins, "T")
                blk.tensor(_)
                tok_mm = C.now("T")

                def _(eng, fb=fb, pb=pb, tok_mm=tok_mm):
                    C.wait(eng, "T", tok_mm)
                    C.wait(eng, f"kst{pb}", kst_free[pb])
                    C.inc(eng.copy(out=kstage[pb][:], in_=k_ps[pb][:]), "S")
                blk.scalar(_)
                qkv_ps_free[pb] = C.now("S")
                tok_st = C.now("S")

                def _(eng, fb=fb, pb=pb, g=g, tok_st=tok_st):
                    C.wait(eng, "S", tok_st)
                    C.incd(eng.dma_start(out=k_scr[fb, :, g * 512:(g + 1) * 512],
                                         in_=kstage[pb][:]), f"kst{pb}")
                blk.sync(_)
                kst_free[pb] = C.now(f"kst{pb}")

            for t in range(4):
                pb = t % 2

                def _(eng, t=t, pb=pb):
                    C.wait(eng, "V", tok_xfm)
                    C.wait(eng, "S", qkv_ps_free[pb])
                    for dc in range(16):
                        ins = eng.matmul(out=v_ps[pb][:],
                                         lhsT=xfm[:, dc * 512 + t * 128:dc * 512 + (t + 1) * 128],
                                         rhs=wv_sb[:, dc * 512:(dc + 1) * 512],
                                         start=(dc == 0), stop=(dc == 15))
                    C.inc(ins, "T")
                blk.tensor(_)
                tok_mm = C.now("T")

                def _(eng, t=t, pb=pb, tok_mm=tok_mm):
                    C.wait(eng, "T", tok_mm)
                    C.wait(eng, f"vst{pb}", vst_free[pb])
                    C.wait(eng, "G", tok_vst)
                    for kv in range(4):
                        ins = eng.copy(out=vstage[pb][:, kv * 129:kv * 129 + 128],
                                       in_=v_ps[pb][:, kv * 128:(kv + 1) * 128])
                    C.inc(ins, "S")
                blk.scalar(_)
                qkv_ps_free[pb] = C.now("S")
                tok_st = C.now("S")

                def _(eng, t=t, pb=pb, g=g, tok_st=tok_st):
                    C.wait(eng, "S", tok_st)
                    C.incd(eng.dma_start(out=v_scr[g * 4 + t, :, :], in_=vstage[pb][:]), f"vst{pb}")
                blk.sync(_)
                vst_free[pb] = C.now(f"vst{pb}")

            xfm_free_local = C.now("T")
            xfm_free = xfm_free_local

        for g in range(4):
            tok_xfm = ln_group(xown, g)

            for fc in range(16):
                pb = fc % 2

                def _(eng, fc=fc, pb=pb):
                    C.wait(eng, "V", tok_xfm)
                    C.wait(eng, "S", qkv_ps_free[pb])
                    for dc in range(16):
                        ins = eng.matmul(out=q_ps[pb][:],
                                         lhsT=wq_sb[:, dc * D + fc * 128:dc * D + (fc + 1) * 128],
                                         rhs=xfm[:, dc * 512:(dc + 1) * 512],
                                         start=(dc == 0), stop=(dc == 15))
                    C.inc(ins, "T")
                blk.tensor(_)
                tok_mm = C.now("T")

                def _(eng, fc=fc, pb=pb, tok_mm=tok_mm):
                    C.wait(eng, "T", tok_mm)
                    C.wait(eng, f"qst{pb}", qst_free[pb])
                    C.inc(eng.copy(out=qstage[pb][:], in_=q_ps[pb][:]), "S")
                blk.scalar(_)
                qkv_ps_free[pb] = C.now("S")
                tok_st = C.now("S")

                def _(eng, fc=fc, pb=pb, g=g, tok_st=tok_st):
                    C.wait(eng, "S", tok_st)
                    C.incd(eng.dma_start(out=q_scr[fc, :, g * 512:(g + 1) * 512],
                                         in_=qstage[pb][:]), f"qst{pb}")
                blk.sync(_)
                qst_free[pb] = C.now(f"qst{pb}")

            xfm_free = C.now("T")

    es1.close()

    es2 = contextlib.ExitStack()
    s2 = lambda name, shape, dt: es2.enter_context(nc.sbuf_tensor(name, shape, dt))
    p2t = lambda name, shape, dt=F32: es2.enter_context(nc.psum_tensor(name, shape, dt))

    k_sb = s2("p2_k", [P, 4 * T], BF16)
    v_sb = s2("p2_v", [P, 4 * 32 * 129], BF16)
    wo_sb = s2("p2_wo", [P, 16 * D], BF16)
    afm = s2("p2_afm", [P, NPAIR * 512], BF16)
    mask_sb = s2("p2_mask", [P, NMASK * 512], BF16)
    q1b = [s2(f"p2_q1{i}", [P, 512], BF16) for i in range(4)]
    q2b = [s2(f"p2_q2{i}", [P, 512], BF16) for i in range(4)]
    pbuf = [s2(f"p2_p{i}", [P, 1024], BF16) for i in range(2)]
    cmb = s2("p2_cmb", [P, 1040], F32)
    abf4 = [s2(f"p2_abf4{i}", [P, 512], BF16) for i in range(2)]
    ysb = s2("p2_ysb", [P, D], F32)
    xo = s2("p2_xo", [P, D], F32)
    xm = s2("p2_xm", [P, D], F32)
    bn6y = s2("p2_bn6y", [P, 24], F32)
    mvy = s2("p2_mvy", [P, 2], F32)
    rsy = s2("p2_rsy", [P, 2], F32)

    s3_ps = p2t("p2_s3", [P, 1024])
    shps2 = p2t("p2_sh2", [P, 512])
    pv4 = p2t("p2_pv4", [P, 2048])
    shps = p2t("p2_sh", [P, 512])
    trview = shps.ap().bitcast(BF16)

    with nc.Block("ph2") as blk:
        def _(eng):
            for key in ("kst0", "kst1", "vst0", "vst1", "qst0", "qst1"):
                C.wait(eng, key, C.now(key))
            C.incd(eng.dma_start(out=wo_sb.ap().rearrange("p (c f) -> p c f", c=16),
                                 in_=woT.rearrange("(c p) f -> p c f", p=P)), "wo2")
        blk.gpsimd(_)
        tok_wo = C.now("wo2")

        def _(eng):
            for key in ("kst0", "kst1", "vst0", "vst1", "qst0", "qst1"):
                C.wait(eng, key, C.now(key))
            for fb in range(4):
                C.incd(eng.dma_start(out=k_sb[:, fb * T:(fb + 1) * T], in_=k_scr[fb]), "kv2")
            for kv in range(4):
                C.incd(eng.dma_start(
                    out=v_sb.ap()[:, kv * 32 * 129:(kv + 1) * 32 * 129].rearrange(
                        "p (tc c) -> p tc c", c=129),
                    in_=v_scr.rearrange("tc p c -> p tc c")[:, :, kv * 129:(kv + 1) * 129]), "kv2")
        blk.sync(_)
        tok_kv = C.now("kv2")

        def _(eng):
            for key in ("kst0", "kst1", "vst0", "vst1", "qst0", "qst1"):
                C.wait(eng, key, C.now(key))
            for b in q1b + q2b:
                C.inc(eng.memset(b[:], 0.0), "V")
        blk.vector(_)
        tok_qz = C.now("V")

        q_free = [0, 0, 0, 0]
        pend_tr = []
        abf_freeT = [0, 0]
        sbank_last = 0
        pmask_last = 0
        pv_cons = ("V", 0)
        trv_consumed = 0
        afm_free = 0
        mask_free = 0
        yps_free = 0
        yps_free2 = [0, 0]
        xm_freeV = 0
        xm_freeDs = 0

        for g in range(4):
            Eg = ESLOT[g]
            nm0 = Eg - NMASK

            def _(eng, g=g):
                C.wait(eng, "V", mask_free)
                C.incd(eng.dma_start(
                    out=mask_sb.ap().rearrange("p (s c) -> p s c", c=512),
                    in_=mask01[g].rearrange("s p c -> p s c")), "msk")
            blk.sync(_)
            tok_mask = C.now("msk")

            def emit_qdma(h):
                qb = h % 4
                kt = h // 2
                base = (kt % 2) * 64

                def _(eng, h=h, qb=qb, base=base, g=g):
                    C.wait(eng, "T", q_free[qb])
                    if g == 0 and h < 4:
                        C.wait(eng, "V", tok_qz)
                    C.incd(eng.dma_start(out=q1b[qb][base:base + 64, :],
                                         in_=q_scr[h, 0:64, g * 512:(g + 1) * 512]), f"q1_{qb}")
                    C.incd(eng.dma_start(out=q2b[qb][base:base + 64, :],
                                         in_=q_scr[h, 64:128, g * 512:(g + 1) * 512]), f"q2_{qb}")
                blk.sync(_)
                return (C.now(f"q1_{qb}"), C.now(f"q2_{qb}"), qb)

            tok_q_next = emit_qdma(0)
            for h in range(NPAIR):
                kt = h // 2
                base = (kt % 2) * 64
                ktile = kt // 2
                vh = h // 4
                qb = h % 4

                tok_q = tok_q_next
                if h + 1 < NPAIR:
                    tok_q_next = emit_qdma(h + 1)

                if len(pend_tr) > 1:
                    etr, eev = pend_tr.pop(0)
                    blk.tensor(etr)
                    _tok_trp = C.now("T")
                    blk.vector(lambda eng, eev=eev, _tok_trp=_tok_trp: eev(
                        eng, tok_trp=_tok_trp))
                    trv_consumed = C.now("V")

                base_T, base_S, base_V = C.now("T"), C.now("S"), C.now("V")
                tok_mm = {}
                tok_pv = {}

                def _(eng, h=h, qb=qb, base=base, ktile=ktile, vh=vh, Eg=Eg,
                      nm0=nm0, tok_q=tok_q, base_S=base_S, base_V=base_V,
                      sb_prev=sbank_last, pv_cons=pv_cons):
                    C.wait(eng, f"q1_{tok_q[2]}", tok_q[0])
                    C.wait(eng, f"q2_{tok_q[2]}", tok_q[1])
                    C.wait(eng, "kv2", tok_kv)

                    def emit_pv(s):
                        masked = s >= nm0
                        if masked:
                            C.wait(eng, "V", base_V + 2 * (s - nm0) + 2)
                        else:
                            C.wait(eng, "S", base_S + s + 1)
                        pq = pbuf[s % 2]
                        vcol = vh * 32 * 129 + s * 129
                        for tt in range(4):
                            ins = eng.matmul(out=pv4[:, tt * 512:tt * 512 + 129],
                                             lhsT=pq[:, tt * 128:(tt + 1) * 128],
                                             rhs=v_sb[:, vcol:vcol + 129],
                                             start=False, stop=False, skip_group_check=True)
                            ins = eng.matmul(out=pv4[:, tt * 512 + 129:tt * 512 + 258],
                                             lhsT=pq[:, 512 + tt * 128:512 + (tt + 1) * 128],
                                             rhs=v_sb[:, vcol:vcol + 129],
                                             start=False, stop=False, skip_group_check=True)
                        tok_pv[s] = C.inc(ins, "T")

                    for s in range(Eg):
                        if s == 1:
                            C.wait(eng, pv_cons[0], pv_cons[1])
                            for tt in range(4):
                                ins = eng.matmul(
                                    out=pv4[:, tt * 512:tt * 512 + 258],
                                    lhsT=zrow[:], rhs=orow[:, 0:258],
                                    start=True, stop=False, skip_group_check=True)
                            C.inc(ins, "T")
                        for half in range(2):
                            k_idx = 2 * s + half
                            bank = k_idx % 2
                            if half == 0:
                                C.wait(eng, "S", base_S + s if s >= 1 else sb_prev)
                            qsrc = (q1b if half == 0 else q2b)[qb]
                            ins = eng.matmul(
                                out=s3_ps[:, bank * 512:(bank + 1) * 512],
                                lhsT=k_sb[:, ktile * T + s * 128:ktile * T + (s + 1) * 128],
                                rhs=qsrc[:], start=True, stop=True)
                            tok_mm[(s, half)] = C.inc(ins, "T")
                        if s >= 1:
                            emit_pv(s - 1)
                    emit_pv(Eg - 1)
                blk.tensor(_)
                q_free[qb] = C.now("T")
                tok_pv_last = C.now("T")
                sbank_last = base_S + Eg

                def _(eng, Eg=Eg, nm0=nm0, base_S=base_S, base_V=base_V,
                      pml=pmask_last):
                    C.wait(eng, "V", pml)
                    for s in range(Eg):
                        C.wait(eng, "T", tok_mm[(s, 1)])
                        if s - 2 >= nm0:
                            C.wait(eng, "V", base_V + 2 * (s - 2 - nm0) + 2)
                        tok = C.inc(eng.activation(
                            out=pbuf[s % 2][:], in_=s3_ps[:, 0:1024],
                            func=AF.Exp, scale=0.125), "S")
                        assert tok == base_S + s + 1
                blk.scalar(_)

                def _(eng, Eg=Eg, nm0=nm0, base_S=base_S, base_V=base_V,
                      tok_mask=tok_mask):
                    C.wait(eng, "msk", tok_mask)
                    for s in range(nm0, Eg):
                        si = s - nm0
                        for half in range(2):
                            if half == 0:
                                C.wait(eng, "S", base_S + s + 1)
                                if s - 2 >= 0:
                                    C.wait(eng, "T", tok_pv.get(s - 2, 0))
                            src = pbuf[s % 2][:, half * 512:(half + 1) * 512]
                            tok = C.inc(eng.tensor_tensor(
                                out=src, in0=src,
                                in1=mask_sb[:, si * 512:(si + 1) * 512],
                                op=ALU.mult), "V")
                            assert tok == base_V + 2 * si + 1 + half
                blk.vector(_)
                pmask_last = base_V + 2 * NMASK

                pv_free_tok = [0]
                pv3 = pv4.ap().rearrange("p (t c) -> p t c", c=512)
                def _(eng, tok_pv_last=tok_pv_last):
                    C.wait(eng, "T", tok_pv_last)
                    t = C.inc(eng.reciprocal(out=cmb.ap()[:, 1024:1028].unsqueeze(2),
                                             in_=pv3[:, :, 128:129]), "V")
                    C.wait(eng, "V", t)
                    t = C.inc(eng.reciprocal(out=cmb.ap()[:, 1028:1032].unsqueeze(2),
                                             in_=pv3[:, :, 257:258]), "V")
                    C.wait(eng, "V", t)
                    t = C.inc(eng.tensor_scalar(out=cmb[:, 1032:1036], in0=cmb[:, 1028:1032],
                                                scalar1=lambc[:], scalar2=-1.0,
                                                op0=ALU.mult, op1=ALU.mult), "V")
                    C.wait(eng, "V", t)
                    A3 = cmb.ap()[:, 0:512].rearrange("p (t c) -> p t c", c=128)
                    B3 = cmb.ap()[:, 512:1024].rearrange("p (t c) -> p t c", c=128)
                    a1bc = cmb.ap()[:, 1024:1028].unsqueeze(2).broadcast_to([P, 4, 128])
                    a2bc = cmb.ap()[:, 1032:1036].unsqueeze(2).broadcast_to([P, 4, 128])
                    t = C.inc(eng.tensor_tensor(out=A3, in0=pv3[:, :, 0:128],
                                                in1=a1bc, op=ALU.mult), "V")
                    C.wait(eng, "V", t)
                    t = C.inc(eng.tensor_tensor(out=B3, in0=pv3[:, :, 129:257],
                                                in1=a2bc, op=ALU.mult), "V")
                    C.wait(eng, "V", t)
                    pv_free_tok[0] = t
                    t = C.inc(eng.tensor_tensor(out=cmb[:, 0:512], in0=cmb[:, 0:512],
                                                in1=cmb[:, 512:1024], op=ALU.add), "V")
                    C.wait(eng, "V", t)
                    t = C.inc(eng.tensor_tensor(out=cmb[:, 512:1024], in0=cmb[:, 0:512],
                                                in1=cmb[:, 0:512], op=ALU.mult), "V")
                    C.wait(eng, "V", t)
                    t = C.inc(eng.reduce_sum(out=cmb[:, 1036:1040], in_=B3, axis=AX.X), "V")
                    C.wait(eng, "V", t)
                    t = C.inc(eng.tensor_scalar(out=cmb[:, 1036:1040], in0=cmb[:, 1036:1040],
                                                scalar1=1.0 / VD, scalar2=None,
                                                op0=ALU.mult), "V")
                    C.wait(eng, "V", t)
                blk.vector(_)
                tok_cmb1 = C.now("V")
                pv_cons = ("V", pv_free_tok[0])

                def _(eng, tok_cmb1=tok_cmb1):
                    C.wait(eng, "V", tok_cmb1)
                    t = C.inc(eng.activation(out=cmb[:, 1028:1032], in_=cmb[:, 1036:1040],
                                             func=AF.Ln, bias=epsap[:]), "S")
                    C.wait(eng, "S", t)
                    C.inc(eng.activation(out=cmb[:, 1024:1028], in_=cmb[:, 1028:1032],
                                         func=AF.Exp, scale=-0.5), "S")
                blk.scalar(_)
                tok_rms = C.now("S")

                def _(eng, h=h, tok_rms=tok_rms):
                    C.wait(eng, "S", tok_rms)
                    A3 = cmb.ap()[:, 0:512].rearrange("p (t c) -> p t c", c=128)
                    B3 = cmb.ap()[:, 512:1024].rearrange("p (t c) -> p t c", c=128)
                    rbc = cmb.ap()[:, 1024:1028].unsqueeze(2).broadcast_to([P, 4, 128])
                    lnw3 = lnw.ap().unsqueeze(1).broadcast_to([P, 4, 128])
                    t = C.inc(eng.tensor_tensor(out=B3, in0=A3, in1=rbc, op=ALU.mult), "V")
                    C.wait(eng, "V", t)
                    C.wait(eng, "T", abf_freeT[h % 2])
                    C.inc(eng.tensor_tensor(out=abf4[h % 2].ap().rearrange(
                        "p (t c) -> p t c", c=128), in0=B3, in1=lnw3, op=ALU.mult), "V")
                blk.vector(_)
                tok_abf = C.now("V")

                def emit_attn_tr(eng, h=h, tok_abf=tok_abf):
                    C.wait(eng, "V", tok_abf)
                    C.wait(eng, "V", trv_consumed)
                    C.wait(eng, "S", yps_free2[0])
                    for tt in range(4):
                        ins = eng.transpose(out=trview[:, tt * 128:(tt + 1) * 128],
                                            in_=abf4[h % 2][:, tt * 128:(tt + 1) * 128],
                                            identity=ident[:])
                    C.inc(ins, "T")
                    abf_freeT[h % 2] = C.now("T")

                def emit_attn_ev(eng, h=h, tok_trp=None):
                    C.wait(eng, "T", tok_trp)
                    C.wait(eng, "T", afm_free)
                    C.inc(eng.tensor_copy(out=afm[:, h * 512:(h + 1) * 512],
                                          in_=trview[:, 0:512]), "V")

                pend_tr.append((emit_attn_tr, emit_attn_ev))
                if h == NPAIR - 1:
                    while pend_tr:
                        etr, eev = pend_tr.pop(0)
                        blk.tensor(etr)
                        tok_trp = C.now("T")
                        blk.vector(lambda eng, eev=eev, tok_trp=tok_trp: eev(
                            eng, tok_trp=tok_trp))
                        trv_consumed = C.now("V")
            tok_afm = C.now("V")
            mask_free = C.now("V")

            for tt in range(4):
                tile = g * 4 + tt

                def _(eng, tile=tile):
                    C.wait(eng, "V", xm_freeV)
                    C.incd(eng.dma_start(out=xo[:], in_=xown[tile * P:(tile + 1) * P, :]), "xo2")
                blk.sync(_)
                tok_xo = C.now("xo2")

                for dcol in range(4):
                    wops = shps if dcol % 2 == 0 else shps2
                    def _(eng, tt=tt, dcol=dcol, wops=wops, tok_afm=tok_afm,
                          ypf=yps_free2[dcol % 2]):
                        C.wait(eng, "V", tok_afm)
                        C.wait(eng, "wo2", tok_wo)
                        C.wait(eng, "S", ypf)
                        for fc in range(16):
                            ins = eng.matmul(
                                out=wops[:],
                                lhsT=afm[:, fc * 512 + tt * 128:fc * 512 + (tt + 1) * 128],
                                rhs=wo_sb[:, fc * D + dcol * 512:fc * D + (dcol + 1) * 512],
                                start=(fc == 0), stop=(fc == 15))
                        C.inc(ins, "T")
                    blk.tensor(_)
                    tok_y = C.now("T")

                    def _(eng, dcol=dcol, wops=wops, tok_y=tok_y):
                        C.wait(eng, "T", tok_y)
                        C.wait(eng, "V", xm_freeV)
                        C.inc(eng.copy(out=ysb[:, dcol * 512:(dcol + 1) * 512],
                                       in_=wops[:]), "S")
                    blk.scalar(_)
                    yps_free2[dcol % 2] = C.now("S")
                    yps_free = C.now("S")
                tok_ye = C.now("S")

                def _(eng, tok_ye=tok_ye):
                    C.wait(eng, "S", tok_ye)
                    for i in range(4):
                        ins = eng.bn_stats(out=bn6y[:, 6 * i:6 * i + 6],
                                           in_=ysb[:, 512 * i:512 * (i + 1)])
                    t = C.inc(ins, "V")
                    C.wait(eng, "V", t)
                    C.inc(eng.bn_aggr(out=mvy[:], in_=bn6y[:]), "V")
                blk.vector(_)
                tok_mvy = C.now("V")

                def _(eng, tok_mvy=tok_mvy):
                    C.wait(eng, "V", tok_mvy)
                    t = C.inc(eng.activation(out=rsy[:, 0:1], in_=mvy[:, 1:2],
                                             func=AF.Ln, bias=epsap[:]), "S")
                    C.wait(eng, "S", t)
                    C.inc(eng.activation(out=rsy[:, 1:2], in_=rsy[:, 0:1],
                                         func=AF.Exp, scale=-0.5), "S")
                blk.scalar(_)
                tok_rsy = C.now("S")

                def _(eng, tok_rsy=tok_rsy, tok_xo=tok_xo):
                    C.wait(eng, "S", tok_rsy)
                    C.wait(eng, "xo2", tok_xo)
                    C.wait(eng, "xmw", xm_freeDs)
                    t = C.inc(eng.tensor_scalar(out=xm[:], in0=ysb[:],
                                                scalar1=mvy[:, 0:1], scalar2=rsy[:, 1:2],
                                                op0=ALU.subtract, op1=ALU.mult), "V")
                    C.wait(eng, "V", t)
                    C.inc(eng.tensor_tensor(out=xm[:], in0=xm[:], in1=xo[:], op=ALU.add), "V")
                blk.vector(_)
                tok_xm = C.now("V")
                xm_freeV = tok_xm

                def _(eng, tile=tile, tok_xm=tok_xm):
                    C.wait(eng, "V", tok_xm)
                    C.incd(eng.dma_start(out=xmid_scr[tile], in_=xm[:]), "xmw")
                blk.sync(_)
                xm_freeDs = C.now("xmw")
            afm_free = C.now("T")

    es2.close()

    es3 = contextlib.ExitStack()
    s3 = lambda name, shape, dt: es3.enter_context(nc.sbuf_tensor(name, shape, dt))

    xfm2 = s3("p3_xfm2", [P, 16 * 1024], BF16)
    acc = s3("p3_acc", [P, 16 * 1024], F32)
    gT = s3("p3_gT", [16, 1024], BF16)
    m3sb = s3("p3_m3", [P, 8], F32)
    r3sb = s3("p3_r3", [P, 8], F32)
    centT_sb = s3("p3_cent", [P, 16 * NRE], BF16)
    eoh_sb = s3("p3_eoh", [16, NRE * P], BF16)
    b1_sb = s3("p3_b1", [P, NEXP * 11], F32)
    b2r_sb = s3("p3_b2r", [16, D], BF16)
    sb2_sb = s3("p3_sb2", [NSH, D], BF16)
    bias_sb = s3("p3_bias", [P, NRE], F32)

    with nc.Block("ph3pre") as blk:
        def _(eng):
            C.wait(eng, "xmw", C.now("xmw"))
            C.incd(eng.dma_start(out=centT_sb.ap().rearrange("p (c e) -> p c e", c=16),
                                 in_=centT.rearrange("(c p) e -> p c e", p=P)), "c3g")
            C.incd(eng.dma_start(out=eoh_sb[:], in_=eoh_in), "c3g")
            C.incd(eng.dma_start(out=b2r_sb[:], in_=b2r), "c3g")
            C.incd(eng.dma_start(out=sb2_sb[:], in_=sb2), "c3g")
        blk.gpsimd(_)

        def _(eng):
            C.wait(eng, "xmw", C.now("xmw"))
            C.incd(eng.dma_start(out=b1_sb[:], in_=b1all), "c3s")
            C.incd(eng.dma_start(out=bias_sb[:], in_=biasbc), "c3s")
        blk.sync(_)
    tok_p3c = (C.now("c3g"), C.now("c3s"))

    for g2 in range(2):
        esA = contextlib.ExitStack()
        sA = lambda name, shape, dt: esA.enter_context(nc.sbuf_tensor(name, shape, dt))
        pA = lambda name, shape, dt=F32: esA.enter_context(nc.psum_tensor(name, shape, dt))
        xmt = [sA(f"p3a_xm{g2}_{i}", [P, D], F32) for i in range(4)]
        x2bf = [sA(f"p3a_x2{g2}_{i}", [P, D], BF16) for i in range(2)]
        bn6b = [sA(f"p3a_bn6{g2}_{i}", [P, 24], F32) for i in range(4)]
        mvall = sA(f"p3a_mva{g2}", [P, 16], F32)
        gsc = sA(f"p3a_gsc{g2}", [P, 640], F32)
        gall = sA(f"p3a_gall{g2}", [P, 8 * 16], BF16)
        psT3 = [pA(f"p3a_psT{g2}_{i}", [P, 1024], BF16) for i in range(2)]
        g_ps = [pA(f"p3a_gps{g2}_{i}", [P, 16]) for i in range(2)]
        gt_ps = pA(f"p3a_gtps{g2}", [16, P], BF16)

        with nc.Block(f"ph3a_{g2}") as blk:
            xmt_free = [0, 0, 0, 0]
            psT3_free = [0, 0]
            x2_free = [0, 0]
            stat_toks = []
            for tt in range(8):
                tile = g2 * 8 + tt
                buf = tt % 4

                def _(eng, tile=tile, buf=buf, tt=tt):
                    if tt == 0:
                        C.wait(eng, "yo0", C.now("yo0"))
                        C.wait(eng, "yo1", C.now("yo1"))
                    C.wait(eng, "V", xmt_free[buf])
                    C.incd(eng.dma_start(out=xmt[buf][:], in_=xmid_scr[tile]), f"xma{buf}")
                blk.sync(_)
                tok_x = C.now(f"xma{buf}")

                def _(eng, tt=tt, buf=buf, tok_x=tok_x):
                    C.wait(eng, f"xma{buf}", tok_x)
                    for i in range(4):
                        ins = eng.bn_stats(out=bn6b[buf][:, 6 * i:6 * i + 6],
                                           in_=xmt[buf][:, 512 * i:512 * (i + 1)])
                    t = C.inc(ins, "V")
                    C.wait(eng, "V", t)
                    C.inc(eng.bn_aggr(out=mvall[:, tt * 2:tt * 2 + 2], in_=bn6b[buf][:]), "V")
                blk.vector(_)
                xmt_free[buf] = C.now("V")
                stat_toks.append(C.now("V"))

            def _(eng):
                C.wait(eng, "V", stat_toks[-1])
                vr = mvall.ap().rearrange("p (t two) -> p t two", two=2)[:, :, 1:2]
                t = C.inc(eng.activation(out=gsc[:, 512:520], in_=vr,
                                         func=AF.Ln, bias=epsap[:]), "S")
                C.wait(eng, "S", t)
                C.inc(eng.activation(out=r3sb[:, 0:8], in_=gsc[:, 512:520],
                                     func=AF.Exp, scale=-0.5), "S")
            blk.scalar(_)
            tok_r3 = C.now("S")

            def _(eng):
                C.wait(eng, "S", tok_r3)
                mcols = mvall.ap().rearrange("p (t two) -> p t two", two=2)[:, :, 0:1]
                C.inc(eng.tensor_copy(out=m3sb.ap().unsqueeze(2), in_=mcols), "V")
            blk.vector(_)
            tok_m3 = C.now("V")

            xmt_free2 = list(xmt_free)
            gexp_toks = []
            for tt in range(8):
                tile = g2 * 8 + tt
                buf = tt % 4
                b2f = tt % 2

                def _(eng, tile=tile, buf=buf, tt=tt):
                    C.wait(eng, "V", xmt_free2[buf])
                    C.incd(eng.dma_start(out=xmt[buf][:], in_=xmid_scr[tile]), f"xma{buf}")
                blk.sync(_)
                tok_x = C.now(f"xma{buf}")

                def _(eng, tt=tt, buf=buf, b2f=b2f, tok_x=tok_x):
                    C.wait(eng, f"xma{buf}", tok_x)
                    C.wait(eng, "V", tok_m3)
                    C.wait(eng, "T", x2_free[b2f])
                    C.inc(eng.tensor_scalar(out=x2bf[b2f][:], in0=xmt[buf][:],
                                            scalar1=m3sb[:, tt:tt + 1],
                                            scalar2=r3sb[:, tt:tt + 1],
                                            op0=ALU.subtract, op1=ALU.mult), "V")
                blk.vector(_)
                xmt_free2[buf] = C.now("V")
                tok_x2 = C.now("V")

                def _(eng, tt=tt, b2f=b2f, tok_x2=tok_x2):
                    C.wait(eng, "V", tok_x2)
                    C.wait(eng, "V", psT3_free[0] if b2f == 0 else psT3_free[1])
                    for c in range(8):
                        ins = eng.transpose(out=psT3[b2f][:, c * 128:(c + 1) * 128],
                                            in_=x2bf[b2f][:, c * 128:(c + 1) * 128],
                                            identity=ident[:])
                    C.inc(ins, "T")
                blk.tensor(_)
                tok_tr1 = C.now("T")

                def _(eng, tt=tt, b2f=b2f, tok_tr1=tok_tr1):
                    xfm23d = xfm2.ap().rearrange("p (c f) -> p c f", f=1024)
                    C.wait(eng, "T", tok_tr1)
                    C.inc(eng.tensor_copy(out=xfm23d[:, 0:8, tt * 128:(tt + 1) * 128],
                                          in_=psT3[b2f].ap().rearrange(
                                              "p (c f) -> p c f", f=128)), "V")
                blk.vector(_)
                tok_ev1 = C.now("V")

                def _(eng, tt=tt, b2f=b2f, tok_ev1=tok_ev1):
                    C.wait(eng, "V", tok_ev1)
                    for c in range(8):
                        ins = eng.transpose(out=psT3[b2f][:, c * 128:(c + 1) * 128],
                                            in_=x2bf[b2f][:, (8 + c) * 128:(9 + c) * 128],
                                            identity=ident[:])
                    C.inc(ins, "T")
                blk.tensor(_)
                tok_tr2 = C.now("T")
                x2_free[b2f] = tok_tr2

                def _(eng, tt=tt, b2f=b2f, tok_tr2=tok_tr2):
                    xfm23d = xfm2.ap().rearrange("p (c f) -> p c f", f=1024)
                    C.wait(eng, "T", tok_tr2)
                    C.inc(eng.tensor_copy(out=xfm23d[:, 8:16, tt * 128:(tt + 1) * 128],
                                          in_=psT3[b2f].ap().rearrange(
                                              "p (c f) -> p c f", f=128)), "V")
                blk.vector(_)
                psT3_free[b2f] = C.now("V")
                tok_ev2 = C.now("V")

                def _(eng, tt=tt, b2f=b2f, tok_ev2=tok_ev2):
                    C.wait(eng, "V", tok_ev2)
                    C.wait(eng, "c3g", tok_p3c[0])
                    C.wait(eng, "S", gexp_toks[tt - 2] if tt >= 2 else 0)
                    for dc in range(16):
                        ins = eng.matmul(out=g_ps[b2f][:],
                                         lhsT=xfm2[:, dc * 1024 + tt * 128:dc * 1024 + (tt + 1) * 128],
                                         rhs=centT_sb[:, dc * 16:(dc + 1) * 16],
                                         start=(dc == 0), stop=(dc == 15))
                    C.inc(ins, "T")
                blk.tensor(_)
                tok_gmm = C.now("T")

                def _(eng, tt=tt, b2f=b2f, tok_gmm=tok_gmm):
                    C.wait(eng, "T", tok_gmm)
                    C.inc(eng.copy(out=gsc[:, 128 + tt * 16:128 + (tt + 1) * 16],
                                   in_=g_ps[b2f][:]), "S")
                blk.scalar(_)
                gexp_toks.append(C.now("S"))

            def _(eng):
                C.wait(eng, "S", gexp_toks[-1])
                C.inc(eng.activation(out=gsc[:, 0:128], in_=gsc[:, 128:256],
                                     func=AF.Exp, scale=-1.0), "S")
            blk.scalar(_)
            tok_sg = C.now("S")

            def _(eng, tok_sg=tok_sg):
                sc3 = gsc.ap()[:, 0:128].rearrange("p (t e) -> p t e", e=16)
                msk3 = gsc.ap()[:, 256:384].rearrange("p (t e) -> p t e", e=16)
                gate3 = gsc.ap()[:, 384:512].rearrange("p (t e) -> p t e", e=16)
                bias3 = bias_sb.ap().unsqueeze(1).broadcast_to([P, 8, NRE])
                C.wait(eng, "S", tok_sg)
                C.wait(eng, "c3s", tok_p3c[1])
                t = C.inc(eng.tensor_scalar(out=gsc[:, 0:128], in0=gsc[:, 0:128],
                                            scalar1=1.0, scalar2=None, op0=ALU.add), "V")
                C.wait(eng, "V", t)
                t = C.inc(eng.reciprocal(out=gsc[:, 0:128], in_=gsc[:, 0:128]), "V")
                C.wait(eng, "V", t)
                t = C.inc(eng.tensor_tensor(out=sc3, in0=sc3, in1=bias3, op=ALU.add), "V")
                C.wait(eng, "V", t)
                t = C.inc(eng.memset(gsc[:, 384:512], 0.0), "V")
                C.wait(eng, "V", t)
                val3 = gsc.ap()[:, 128:256].rearrange("p (t e) -> p t e", e=16)
                for it in range(4):
                    t = C.inc(eng.reduce_max(out=gsc[:, 528:536], in_=sc3, axis=AX.X), "V")
                    C.wait(eng, "V", t)
                    m8bc = gsc.ap()[:, 528:536].unsqueeze(2).broadcast_to([P, 8, 16])
                    t = C.inc(eng.tensor_tensor(out=msk3, in0=sc3, in1=m8bc,
                                                op=ALU.is_equal), "V")
                    C.wait(eng, "V", t)
                    t = C.inc(eng.tensor_tensor(out=val3, in0=msk3, in1=m8bc,
                                                op=ALU.mult), "V")
                    C.wait(eng, "V", t)
                    t = C.inc(eng.tensor_tensor(out=gate3, in0=gate3,
                                                in1=val3, op=ALU.add), "V")
                    C.wait(eng, "V", t)
                    t = C.inc(eng.scalar_tensor_tensor(out=gsc[:, 0:128], in0=gsc[:, 256:384],
                                                       scalar=-1e30, in1=gsc[:, 0:128],
                                                       op0=ALU.mult, op1=ALU.add), "V")
                    C.wait(eng, "V", t)
                t = C.inc(eng.reduce_sum(out=gsc[:, 536:544], in_=gate3, axis=AX.X), "V")
                C.wait(eng, "V", t)
                t = C.inc(eng.tensor_scalar(out=gsc[:, 536:544], in0=gsc[:, 536:544],
                                            scalar1=1e-8, scalar2=None, op0=ALU.max), "V")
                C.wait(eng, "V", t)
                t = C.inc(eng.reciprocal(out=gsc[:, 544:552], in_=gsc[:, 536:544]), "V")
                C.wait(eng, "V", t)
                rg3 = gsc.ap()[:, 544:552].unsqueeze(2).broadcast_to([P, 8, 16])
                t = C.inc(eng.tensor_tensor(out=gall.ap().rearrange("p (t e) -> p t e", e=16),
                                            in0=gate3, in1=rg3, op=ALU.mult), "V")
                C.wait(eng, "V", t)
            blk.vector(_)
            tok_gate = C.now("V")

            gtps_freeV = 0
            for tt in range(8):
                def _(eng, tt=tt, tok_gate=tok_gate):
                    C.wait(eng, "V", tok_gate)
                    C.wait(eng, "V", gtps_freeV)
                    C.inc(eng.transpose(out=gt_ps[:],
                                        in_=gall[:, tt * 16:(tt + 1) * 16],
                                        identity=ident[:]), "T")
                blk.tensor(_)
                tok_gt = C.now("T")

                def _(eng, tt=tt, tok_gt=tok_gt):
                    C.wait(eng, "T", tok_gt)
                    C.inc(eng.tensor_copy(out=gT[:, tt * 128:(tt + 1) * 128],
                                          in_=gt_ps[:]), "V")
                blk.vector(_)
                gtps_freeV = C.now("V")
        tok_gT = C.now("V")
        tok_xfm2 = C.now("V")
        esA.close()

        esB = contextlib.ExitStack()
        sB = lambda name, shape, dt: esB.enter_context(nc.sbuf_tensor(name, shape, dt))
        pB = lambda name, shape, dt=F32: esB.enter_context(nc.psum_tensor(name, shape, dt))
        w1A = sB(f"p3b_w1A{g2}", [P, 16 * W1A], BF16)
        w1B = sB(f"p3b_w1B{g2}", [P, 16 * W1B], BF16)
        w2c = [sB(f"p3b_w2c{g2}_{i}", [P, 11 * 256], BF16) for i in range(2)]
        hg = sB(f"p3b_hg{g2}", [P, 11 * 1024], BF16)
        htmp = sB(f"p3b_htmp{g2}", [P, 1024], BF16)
        gbc_sb = sB(f"p3b_gbc{g2}", [P, 1024], BF16)
        h_ps = [pB(f"p3b_hps{g2}_{i}", [P, 1024]) for i in range(2)]
        y2_ps = [pB(f"p3b_yps{g2}_{i}", [P, 1024]) for i in range(2)]

        def w1_src(j, half):
            w = rw1[j] if j < NRE else sw1[j - NRE]
            lo = 0 if half == 0 else W1A
            width = W1A if half == 0 else W1B
            return w.rearrange("(c p) f -> p c f", p=P)[:, :, lo:lo + width]

        def w1_dst(half):
            buf = w1A if half == 0 else w1B
            width = W1A if half == 0 else W1B
            return buf.ap().rearrange("p (c f) -> p c f", f=width)

        with nc.Block(f"ph3b_{g2}") as blk:
            w1_free = [0, 0]
            w1_tok = {}
            w2_free = [0, 0]
            w2_tok = {}
            hps_free = [0, 0]
            yps3_free = [("V", 0), ("V", 0)]
            htmp_free = 0
            hg_free = 0
            gbc_freeV = 0

            def _(eng):
                C.wait(eng, "yo0", C.now("yo0"))
                C.wait(eng, "yo1", C.now("yo1"))
                C.incd(eng.dma_start(out=w1_dst(0), in_=w1_src(0, 0)), "w1A")
                w1_tok[(0, 0)] = C.now("w1A")
            blk.gpsimd(_)

            for j in range(NEXP):
                routed = j < NRE

                def _(eng, j=j):
                    C.wait(eng, "T", w1_free[1])
                    C.incd(eng.dma_start(out=w1_dst(1), in_=w1_src(j, 1)), "w1B")
                    w1_tok[(j, 1)] = C.now("w1B")
                blk.gpsimd(_)

                if routed:
                    def _(eng, j=j):
                        C.wait(eng, "V", tok_gT)
                        C.wait(eng, yps3_free[0][0], yps3_free[0][1])
                        ins = eng.matmul(out=y2_ps[0][:, 0:512],
                                         lhsT=eoh_sb[:, j * 128:(j + 1) * 128],
                                         rhs=gT[:, 0:512], start=True, stop=True)
                        C.inc(ins, "T")
                        ins = eng.matmul(out=y2_ps[0][:, 512:1024],
                                         lhsT=eoh_sb[:, j * 128:(j + 1) * 128],
                                         rhs=gT[:, 512:1024], start=True, stop=True)
                        C.inc(ins, "T")
                    blk.tensor(_)
                    tok_gbcm = C.now("T")

                    def _(eng, tok_gbcm=tok_gbcm):
                        C.wait(eng, "T", tok_gbcm - 1)
                        C.wait(eng, "V", gbc_freeV)
                        C.inc(eng.copy(out=gbc_sb[:, 0:512], in_=y2_ps[0][:, 0:512]), "S")
                        C.wait(eng, "T", tok_gbcm)
                        C.inc(eng.copy(out=gbc_sb[:, 512:1024], in_=y2_ps[0][:, 512:1024]), "S")
                    blk.scalar(_)
                    yps3_free[0] = ("S", C.now("S"))
                    tok_gbc = C.now("S")
                else:
                    tok_gbc = 0

                for fc in range(11):
                    half = 0 if fc < FC_W1A else 1
                    fcl = fc if fc < FC_W1A else fc - FC_W1A
                    width = W1A if half == 0 else W1B
                    wbuf = w1A if half == 0 else w1B
                    hp = fc % 2

                    def _(eng, j=j, fc=fc, half=half, fcl=fcl, width=width,
                          wbuf=wbuf, hp=hp):
                        C.wait(eng, "w1A" if half == 0 else "w1B", w1_tok[(j, half)])
                        C.wait(eng, "V", tok_xfm2)
                        C.wait(eng, "S", hps_free[hp])
                        for ts_ in range(2):
                            for dc in range(16):
                                ins = eng.matmul(
                                    out=h_ps[hp][:, ts_ * 512:(ts_ + 1) * 512],
                                    lhsT=wbuf[:, dc * width + fcl * 128:dc * width + (fcl + 1) * 128],
                                    rhs=xfm2[:, dc * 1024 + ts_ * 512:dc * 1024 + (ts_ + 1) * 512],
                                    start=(dc == 0), stop=(dc == 15))
                        C.inc(ins, "T")
                        if fc == 10:
                            w1_free[1] = C.now("T")
                        if fc == FC_W1A - 1:
                            w1_free[0] = C.now("T")
                    blk.tensor(_)
                    tok_h = C.now("T")

                    if fc == FC_W1A - 1 and j + 1 < NEXP:
                        def _(eng, j=j):
                            C.wait(eng, "T", w1_free[0])
                            C.incd(eng.dma_start(out=w1_dst(0), in_=w1_src(j + 1, 0)), "w1A")
                            w1_tok[(j + 1, 0)] = C.now("w1A")
                        blk.gpsimd(_)

                    def _(eng, j=j, fc=fc, hp=hp, routed=routed, tok_h=tok_h):
                        C.wait(eng, "T", tok_h)
                        if routed:
                            C.wait(eng, "V", htmp_free)
                            dst = htmp[:]
                        else:
                            C.wait(eng, "T", hg_free)
                            dst = hg[:, fc * 1024:(fc + 1) * 1024]
                        C.inc(eng.activation(out=dst, in_=h_ps[hp][:], func=AF.Relu,
                                             bias=b1_sb[:, j * 11 + fc:j * 11 + fc + 1]), "S")
                    blk.scalar(_)
                    hps_free[hp] = C.now("S")
                    tok_relu = C.now("S")

                    if routed:
                        def _(eng, fc=fc, tok_relu=tok_relu, tok_gbc=tok_gbc):
                            C.wait(eng, "S", tok_relu)
                            C.wait(eng, "S", tok_gbc)
                            C.wait(eng, "T", hg_free)
                            C.inc(eng.tensor_tensor(out=hg[:, fc * 1024:(fc + 1) * 1024],
                                                    in0=htmp[:], in1=gbc_sb[:],
                                                    op=ALU.mult), "V")
                        blk.vector(_)
                        htmp_free = C.now("V")
                tok_hg = C.now("V") if routed else C.now("S")
                hg_wait_stream = "V" if routed else "S"

                if j == 0:
                    def _(eng, j=j):
                        C.wait(eng, "T", w2_free[0])
                        w2 = rw2[j] if j < NRE else sw2[j - NRE]
                        C.incd(eng.dma_start(
                            out=w2c[0].ap().rearrange("p (c f) -> p c f", f=256),
                            in_=w2.rearrange("(c p) d -> p c d", p=P)[:, :, 0:256]), "w2c0")
                        w2_tok[0] = C.now("w2c0")
                    blk.gpsimd(_)

                for dc in range(16):
                    cb = dc % 2
                    cw = (dc // 2) % 2
                    dcl = dc % 2

                    def _(eng, j=j, dc=dc):
                        if dc % 2 == 0:
                            if dc + 2 < 16:
                                jn, cols = j, (dc + 2) * 128
                            elif j + 1 < NEXP:
                                jn, cols = j + 1, 0
                            else:
                                return
                            nb = (dc // 2 + 1) % 2
                            C.wait(eng, "T", w2_free[nb])
                            w2 = rw2[jn] if jn < NRE else sw2[jn - NRE]
                            C.incd(eng.dma_start(
                                out=w2c[nb].ap().rearrange("p (c f) -> p c f", f=256),
                                in_=w2.rearrange("(c p) d -> p c d", p=P)[
                                    :, :, cols:cols + 256]), f"w2c{nb}")
                            w2_tok[nb] = C.now(f"w2c{nb}")
                    blk.gpsimd(_)

                    def _(eng, j=j, dc=dc, cb=cb, cw=cw, dcl=dcl, tok_hg=tok_hg,
                          hws=hg_wait_stream):
                        C.wait(eng, hws, tok_hg)
                        C.wait(eng, f"w2c{cw}", w2_tok[cw])
                        C.wait(eng, yps3_free[cb][0], yps3_free[cb][1])
                        for ts_ in range(2):
                            for fc in range(11):
                                ins = eng.matmul(
                                    out=y2_ps[cb][:, ts_ * 512:(ts_ + 1) * 512],
                                    lhsT=w2c[cw][:, fc * 256 + dcl * 128:fc * 256 + (dcl + 1) * 128],
                                    rhs=hg[:, fc * 1024 + ts_ * 512:fc * 1024 + (ts_ + 1) * 512],
                                    start=(fc == 0), stop=False,
                                    skip_group_check=True)
                            if j == 0:
                                ins = eng.matmul(
                                    out=y2_ps[cb][:, ts_ * 512:(ts_ + 1) * 512],
                                    lhsT=b2r_sb[:, dc * 128:(dc + 1) * 128],
                                    rhs=gT[:, ts_ * 512:(ts_ + 1) * 512],
                                    start=False, stop=False, skip_group_check=True)
                                ins = eng.matmul(
                                    out=y2_ps[cb][:, ts_ * 512:(ts_ + 1) * 512],
                                    lhsT=sb2_sb[:, dc * 128:(dc + 1) * 128],
                                    rhs=ones2[:, 0:512],
                                    start=False, stop=(ts_ == 1), skip_group_check=True)
                        C.inc(ins, "T")
                        if dcl == 1:
                            w2_free[cw] = C.now("T")
                    blk.tensor(_)
                    tok_y2 = C.now("T")

                    def _(eng, j=j, dc=dc, cb=cb, tok_y2=tok_y2):
                        C.wait(eng, "T", tok_y2)
                        if j == 0:
                            C.inc(eng.tensor_copy(out=acc[:, dc * 1024:(dc + 1) * 1024],
                                                  in_=y2_ps[cb][:]), "V")
                        else:
                            C.inc(eng.tensor_tensor(out=acc[:, dc * 1024:(dc + 1) * 1024],
                                                    in0=y2_ps[cb][:],
                                                    in1=acc[:, dc * 1024:(dc + 1) * 1024],
                                                    op=ALU.add), "V")
                        yps3_free[cb] = ("V", C.now("V"))
                    blk.vector(_)
                hg_free = C.now("T")
                if routed:
                    gbc_freeV = tok_hg
        tok_acc = C.now("V")
        esB.close()

        esC = contextlib.ExitStack()
        sC = lambda name, shape, dt: esC.enter_context(nc.sbuf_tensor(name, shape, dt))
        pC = lambda name, shape, dt=F32: esC.enter_context(nc.psum_tensor(name, shape, dt))
        z_sb = [sC(f"p3c_z{g2}_{i}", [P, D], F32) for i in range(2)]
        xmc = [sC(f"p3c_xm{g2}_{i}", [P, D], F32) for i in range(2)]
        x2f = [sC(f"p3c_x2f{g2}_{i}", [P, D], F32) for i in range(2)]
        osb = [sC(f"p3c_o{g2}_{i}", [P, D], F32) for i in range(2)]
        bn6c = [sC(f"p3c_bn6{g2}_{i}", [P, 24], F32) for i in range(2)]
        mvc = [sC(f"p3c_mv{g2}_{i}", [P, 2], F32) for i in range(2)]
        rsc = [sC(f"p3c_rs{g2}_{i}", [P, 2], F32) for i in range(2)]
        trf = [pC(f"p3c_tr{g2}_{i}", [P, 512]) for i in range(4)]

        with nc.Block(f"ph3c_{g2}") as blk:
            xmc_free = [0, 0]
            osb_free = [0, 0]
            trf_free = [0, 0, 0, 0]
            z_freeV = [0, 0]
            for tt in range(8):
                tile = g2 * 8 + tt
                buf = tt % 2

                def _(eng, tile=tile, buf=buf):
                    C.wait(eng, "V", xmc_free[buf])
                    C.incd(eng.dma_start(out=xmc[buf][:], in_=xmid_scr[tile]), f"xmc{buf}")
                blk.sync(_)
                tok_x = C.now(f"xmc{buf}")

                ztoks = []
                for half in range(4):
                    tp = 2 * buf + half % 2

                    def _(eng, tt=tt, half=half, tp=tp):
                        C.wait(eng, "S", trf_free[tp])
                        for c in range(4):
                            dcc = half * 4 + c
                            ins = eng.transpose(
                                out=trf[tp][:, c * 128:(c + 1) * 128],
                                in_=acc[:, dcc * 1024 + tt * 128:dcc * 1024 + (tt + 1) * 128],
                                identity=identf[:])
                        C.inc(ins, "T")
                    blk.tensor(_)
                    tok_tr = C.now("T")

                    def _(eng, buf=buf, half=half, tp=tp, tok_tr=tok_tr):
                        C.wait(eng, "T", tok_tr)
                        C.wait(eng, "V", z_freeV[buf])
                        C.inc(eng.copy(out=z_sb[buf][:, half * 512:(half + 1) * 512],
                                       in_=trf[tp][:]), "S")
                    blk.scalar(_)
                    trf_free[tp] = C.now("S")
                tok_z = C.now("S")

                def _(eng, tt=tt, buf=buf, tok_z=tok_z, tok_x=tok_x):
                    C.wait(eng, "S", tok_z)
                    C.wait(eng, f"xmc{buf}", tok_x)
                    t = C.inc(eng.tensor_scalar(out=x2f[buf][:], in0=xmc[buf][:],
                                                scalar1=m3sb[:, tt:tt + 1],
                                                scalar2=r3sb[:, tt:tt + 1],
                                                op0=ALU.subtract, op1=ALU.mult), "V")
                    C.wait(eng, "V", t)
                    t = C.inc(eng.tensor_tensor(out=z_sb[buf][:], in0=z_sb[buf][:],
                                                in1=x2f[buf][:], op=ALU.add), "V")
                    C.wait(eng, "V", t)
                    for i in range(4):
                        ins = eng.bn_stats(out=bn6c[buf][:, 6 * i:6 * i + 6],
                                           in_=z_sb[buf][:, 512 * i:512 * (i + 1)])
                    t = C.inc(ins, "V")
                    C.wait(eng, "V", t)
                    C.inc(eng.bn_aggr(out=mvc[buf][:], in_=bn6c[buf][:]), "V")
                blk.vector(_)
                tok_mvz = C.now("V")

                def _(eng, buf=buf, tok_mvz=tok_mvz):
                    C.wait(eng, "V", tok_mvz)
                    t = C.inc(eng.activation(out=rsc[buf][:, 0:1], in_=mvc[buf][:, 1:2],
                                             func=AF.Ln, bias=epsap[:]), "S")
                    C.wait(eng, "S", t)
                    C.inc(eng.activation(out=rsc[buf][:, 1:2], in_=rsc[buf][:, 0:1],
                                         func=AF.Exp, scale=-0.5), "S")
                blk.scalar(_)
                tok_rsz = C.now("S")

                def _(eng, buf=buf, tok_rsz=tok_rsz):
                    C.wait(eng, "S", tok_rsz)
                    C.wait(eng, f"yo{buf}", osb_free[buf])
                    t = C.inc(eng.tensor_scalar(out=osb[buf][:], in0=z_sb[buf][:],
                                                scalar1=mvc[buf][:, 0:1],
                                                scalar2=rsc[buf][:, 1:2],
                                                op0=ALU.subtract, op1=ALU.mult), "V")
                    C.wait(eng, "V", t)
                    t = C.inc(eng.tensor_tensor(out=osb[buf][:], in0=osb[buf][:],
                                                in1=xmc[buf][:], op=ALU.add), "V")
                blk.vector(_)
                z_freeV[buf] = C.now("V")
                xmc_free[buf] = C.now("V")
                tok_o = C.now("V")

                def _(eng, tile=tile, buf=buf, tok_o=tok_o):
                    C.wait(eng, "V", tok_o)
                    C.incd(eng.dma_start(out=yout[tile * P:(tile + 1) * P, :],
                                         in_=osb[buf][:]), f"yo{buf}")
                blk.sync(_)
                osb_free[buf] = C.now(f"yo{buf}")
        esC.close()
    es3.close()

    with nc.Block("fin") as blk:
        if debug:
            def _(eng):
                for key in ("kst0", "kst1", "vst0", "vst1", "qst0", "qst1", "xmw"):
                    C.wait(eng, key, C.now(key))
                for i in range(16):
                    C.incd(eng.dma_start(out=dbg_xmid[i], in_=xmid_scr[i]), "dbg")
                for i in range(NPAIR):
                    C.incd(eng.dma_start(out=dbg_q[i], in_=q_scr[i]), "dbg")
                for i in range(4):
                    C.incd(eng.dma_start(out=dbg_k[i], in_=k_scr[i]), "dbg")
                for i in range(32):
                    C.incd(eng.dma_start(out=dbg_v[i], in_=v_scr[i]), "dbg")
                C.wait(eng, "dbg", C.now("dbg"))
            blk.gpsimd(_)

        def _(eng):
            C.wait(eng, "yo0", C.now("yo0"))
            C.wait(eng, "yo1", C.now("yo1"))
        blk.sync(_)

    es_glob.close()
    return nc



def _host_inputs(inputs):
    f32 = np.float32
    bf16 = ml_dtypes.bfloat16
    d = {}
    d["wqT"] = np.ascontiguousarray(inputs["L0_attn_Wq"].T)
    d["wkT"] = np.ascontiguousarray(inputs["L0_attn_Wk"].T)
    d["wvT"] = np.ascontiguousarray(inputs["L0_attn_Wv"].T)
    d["woT"] = np.ascontiguousarray(inputs["L0_attn_Wo"].T)
    d["centT"] = np.ascontiguousarray(inputs["L0_ffn_centroids"].T)
    d["rw1"] = np.ascontiguousarray(inputs["L0_ffn_routed_w1"])
    d["rw2"] = np.ascontiguousarray(inputs["L0_ffn_routed_w2"])
    d["sw1"] = np.ascontiguousarray(inputs["L0_ffn_shared_w1"])
    d["sw2"] = np.ascontiguousarray(inputs["L0_ffn_shared_w2"])
    b1 = np.concatenate([inputs["L0_ffn_routed_b1"], inputs["L0_ffn_shared_b1"]], 0)
    d["b1all"] = np.ascontiguousarray(
        b1.reshape(NEXP, 11, P).transpose(2, 0, 1).reshape(P, NEXP * 11)).astype(f32)
    d["b2r"] = np.ascontiguousarray(inputs["L0_ffn_routed_b2"])
    d["sb2"] = np.ascontiguousarray(inputs["L0_ffn_shared_b2"])
    d["biasbc"] = np.ascontiguousarray(
        np.broadcast_to(inputs["L0_ffn_bias"][None, :], (P, NRE))).astype(f32)
    d["lnwbc"] = np.ascontiguousarray(
        np.broadcast_to(inputs["L0_attn_ln_w"][None, :], (P, VD))).astype(f32)
    d["lamvec"] = np.concatenate(
        [inputs["L0_attn_lambda_q1"], inputs["L0_attn_lambda_k1"],
         inputs["L0_attn_lambda_q2"], inputs["L0_attn_lambda_k2"]]).reshape(1, 256).astype(f32)
    d["ident"] = np.eye(P, dtype=bf16)
    eoh = np.zeros((NRE, NRE * P), dtype=bf16)
    for j in range(NRE):
        eoh[j, j * P:(j + 1) * P] = 1.0
    d["eoh"] = eoh
    return d


def _masks_for(parity):
    astart = [0, 512, 3072, 3584] if parity == 0 else [1024, 1536, 2048, 2560]
    m = np.zeros((4, NMASK, P, 512), dtype=np.float32)
    r = np.arange(P)[:, None]
    q = np.arange(512)[None, :]
    for g in range(4):
        for si in range(NMASK):
            slot = ESLOT[g] - NMASK + si
            kabs = slot * 128 + r
            qabs = astart[g] + q
            m[g, si] = (kabs <= qabs)
    return m.astype(ml_dtypes.bfloat16)


def _own_idx(parity):
    chunks = (0, 3) if parity == 0 else (1, 2)
    return np.concatenate([np.arange(c * 1024, (c + 1) * 1024) for c in chunks])


def kernel(**inputs):
    x = inputs["L0_attn_x"]
    shared = _host_inputs(inputs)
    masks = [_masks_for(0), _masks_for(1)]

    nc = build_program()

    in_maps = []
    for c in range(NCHIP):
        b, parity = c // 2, c % 2
        m = dict(shared)
        m["xseq"] = np.ascontiguousarray(x[b])
        m["xown"] = np.ascontiguousarray(x[b][_own_idx(parity)])
        m["mask01"] = masks[parity]
        in_maps.append(m)

    res = run_bass_kernel_spmd(nc, in_maps, core_ids=list(range(NCHIP)))

    out = np.empty((4, T, D), dtype=np.float32)
    for c in range(NCHIP):
        b, parity = c // 2, c % 2
        out[b, _own_idx(parity)] = res.results[c]["yout"]
    return out


# version 10
# speedup vs baseline: 1.1216x; 1.1216x over previous
import math
import contextlib
import numpy as np
import ml_dtypes
import concourse.bass as bass
import concourse.mybir as mybir
from concourse.bass_utils import run_bass_kernel_spmd

F32 = mybir.dt.float32
BF16 = mybir.dt.bfloat16
AF = mybir.ActivationFunctionType
ALU = mybir.AluOpType
AX = mybir.AxisListType

P = 128
D = 2048
T = 4096
OWN = 2048
DFF = 1408
NRE = 16
NSH = 2
NEXP = NRE + NSH
NPAIR = 16
HD = 64
VD = 128
LAMBDA_INIT = 0.8 - 0.6 * math.exp(-0.3 * 12)
EPS = 1e-5
ESLOT = [12, 16, 28, 32]
NMASK = 12
NCHIP = 8

FC_W1A = 6
W1A = FC_W1A * P
W1B = DFF - W1A
FC_W1B = 5


class Ctx:

    NAMES = ["T", "V", "S", "G"]

    def __init__(self, nc, es):
        self.nc = nc
        self.es = es
        self.sem = {n: es.enter_context(nc.semaphore("sem_" + n)) for n in self.NAMES}
        self.cnt = {n: 0 for n in self.NAMES}

    def inc(self, instr, name, n=1):
        instr.then_inc(self.sem[name], n)
        self.cnt[name] += n
        return self.cnt[name]

    def incd(self, instr, key):
        if key not in self.sem:
            self.sem[key] = self.es.enter_context(self.nc.semaphore("d_" + key))
            self.cnt[key] = 0
        return self.inc(instr, key, 16)

    def wait(self, eng, name, val):
        if val > 0 and name in self.sem:
            eng.wait_ge(self.sem[name], val)

    def now(self, name):
        return self.cnt.get(name, 0)


def build_program(debug=False):
    nc = bass.Bass("TRN2", target_bir_lowering=False, debug=False)

    xseq = nc.dram_tensor("xseq", [T, D], F32, kind="ExternalInput").ap()
    xown = nc.dram_tensor("xown", [OWN, D], F32, kind="ExternalInput").ap()
    wqT = nc.dram_tensor("wqT", [D, D], F32, kind="ExternalInput").ap()
    wkT = nc.dram_tensor("wkT", [D, 512], F32, kind="ExternalInput").ap()
    wvT = nc.dram_tensor("wvT", [D, 512], F32, kind="ExternalInput").ap()
    woT = nc.dram_tensor("woT", [D, D], F32, kind="ExternalInput").ap()
    centT = nc.dram_tensor("centT", [D, NRE], F32, kind="ExternalInput").ap()
    rw1 = nc.dram_tensor("rw1", [NRE, D, DFF], F32, kind="ExternalInput").ap()
    rw2 = nc.dram_tensor("rw2", [NRE, DFF, D], F32, kind="ExternalInput").ap()
    sw1 = nc.dram_tensor("sw1", [NSH, D, DFF], F32, kind="ExternalInput").ap()
    sw2 = nc.dram_tensor("sw2", [NSH, DFF, D], F32, kind="ExternalInput").ap()
    b1all = nc.dram_tensor("b1all", [P, NEXP * 11], F32, kind="ExternalInput").ap()
    b2r = nc.dram_tensor("b2r", [NRE, D], F32, kind="ExternalInput").ap()
    sb2 = nc.dram_tensor("sb2", [NSH, D], F32, kind="ExternalInput").ap()
    biasbc = nc.dram_tensor("biasbc", [P, NRE], F32, kind="ExternalInput").ap()
    lnwbc = nc.dram_tensor("lnwbc", [P, VD], F32, kind="ExternalInput").ap()
    lamvec = nc.dram_tensor("lamvec", [1, 256], F32, kind="ExternalInput").ap()
    ident_in = nc.dram_tensor("ident", [P, P], BF16, kind="ExternalInput").ap()
    eoh_in = nc.dram_tensor("eoh", [NRE, NRE * P], BF16, kind="ExternalInput").ap()
    mask01 = nc.dram_tensor("mask01", [4, NMASK, P, 512], BF16, kind="ExternalInput").ap()

    yout = nc.dram_tensor("yout", [OWN, D], F32, kind="ExternalOutput").ap()

    q_scr = nc.dram_tensor("q_scr", [NPAIR, P, OWN], BF16, kind="Internal").ap()
    k_scr = nc.dram_tensor("k_scr", [4, P, T], BF16, kind="Internal").ap()
    v_scr = nc.dram_tensor("v_scr", [32, P, 4 * 129], BF16, kind="Internal").ap()
    xmid_scr = nc.dram_tensor("xmid_scr", [16, P, D], F32, kind="Internal").ap()
    if debug:
        dbg_xmid = nc.dram_tensor("dbg_xmid", [16, P, D], F32, kind="ExternalOutput").ap()
        dbg_q = nc.dram_tensor("dbg_q", [NPAIR, P, OWN], F32, kind="ExternalOutput").ap()
        dbg_k = nc.dram_tensor("dbg_k", [4, P, T], F32, kind="ExternalOutput").ap()
        dbg_v = nc.dram_tensor("dbg_v", [32, P, 4 * 129], F32, kind="ExternalOutput").ap()

    es_glob = contextlib.ExitStack()
    C = Ctx(nc, es_glob)
    sb = lambda name, shape, dt=F32: es_glob.enter_context(nc.sbuf_tensor(name, shape, dt))

    ident = sb("g_ident", [P, P], BF16)
    identf = sb("g_identf", [P, P], F32)
    lamv = sb("g_lamv", [1, 256], F32)
    lprod = sb("g_lprod", [1, 64], F32)
    lsc = sb("g_lsc", [1, 4], F32)
    lambc = sb("g_lambc", [P, 1], F32)
    lnw = sb("g_lnw", [P, VD], F32)
    epsap = sb("g_epsap", [P, 1], F32)
    zrow = sb("g_zrow", [1, P], BF16)
    orow = sb("g_orow", [1, 512], BF16)
    onesf = sb("g_onesf", [1, P], F32)
    ones2 = sb("g_ones2", [NSH, 512], BF16)

    with nc.Block("ph0"):
        pass

    with nc.Block("ph_const") as blk:
        def _(eng):
            tok = C.incd(eng.dma_start(out=ident[:], in_=ident_in), "c0")
            C.incd(eng.dma_start(out=lamv[:], in_=lamvec), "c0")
            C.incd(eng.dma_start(out=lnw[:], in_=lnwbc), "c0")
            return tok
        blk.sync(_)

        def _(eng):
            eng.memset(epsap[:], EPS)
            eng.memset(zrow[:], 0.0)
            eng.memset(orow[:], 1.0)
            eng.memset(onesf[:], 1.0)
            C.inc(eng.memset(ones2[:], 1.0), "G")
        blk.gpsimd(_)

        def _(eng):
            C.wait(eng, "c0", 48)
            C.inc(eng.tensor_copy(out=identf[:], in_=ident[:]), "V")
            t1 = C.inc(eng.tensor_tensor(out=lprod[:], in0=lamv[:, 0:64],
                                         in1=lamv[:, 64:128], op=ALU.mult), "V")
            C.wait(eng, "V", t1)
            t2 = C.inc(eng.reduce_sum(out=lsc[:, 0:1], in_=lprod[:], axis=AX.X), "V")
            C.wait(eng, "V", t2)
            t3 = C.inc(eng.tensor_tensor(out=lprod[:], in0=lamv[:, 128:192],
                                         in1=lamv[:, 192:256], op=ALU.mult), "V")
            C.wait(eng, "V", t3)
            C.inc(eng.reduce_sum(out=lsc[:, 1:2], in_=lprod[:], axis=AX.X), "V")
            C.inc(eng.tensor_scalar(out=lnw[:], in0=lnw[:], scalar1=1.0 - LAMBDA_INIT,
                                    scalar2=None, op0=ALU.mult), "V")
        blk.vector(_)

        def _(eng):
            C.wait(eng, "V", C.now("V"))
            t = C.inc(eng.activation(out=lsc[:, 2:3], in_=lsc[:, 0:1], func=AF.Exp), "S")
            C.wait(eng, "S", t)
            t = C.inc(eng.activation(out=lsc[:, 3:4], in_=lsc[:, 1:2], func=AF.Exp), "S")
            C.wait(eng, "S", t)
        blk.scalar(_)

        def _(eng):
            C.wait(eng, "S", C.now("S"))
            t = C.inc(eng.tensor_scalar(out=lsc[:, 0:1], in0=lsc[:, 2:3],
                                        scalar1=lsc[:, 3:4], scalar2=LAMBDA_INIT,
                                        op0=ALU.subtract, op1=ALU.add), "V")
            C.wait(eng, "V", t)
            t = C.inc(eng.tensor_copy(out=lsc[:, 1:2], in_=lsc[:, 0:1]), "V")
            C.wait(eng, "V", t)
        blk.vector(_)

        with nc.psum_tensor("ph0_bc", [P, 2], F32) as bc_ps:
            def _(eng):
                C.wait(eng, "V", C.now("V"))
                C.wait(eng, "G", C.now("G"))
                C.inc(eng.matmul(out=bc_ps[:], lhsT=onesf[:], rhs=lsc[:, 0:2],
                                 start=True, stop=True), "T")
            blk.tensor(_)

            def _(eng):
                C.wait(eng, "T", C.now("T"))
                C.inc(eng.tensor_copy(out=lambc[:], in_=bc_ps[:, 0:1]), "V")
            blk.vector(_)

    es1 = contextlib.ExitStack()
    s1 = lambda name, shape, dt: es1.enter_context(nc.sbuf_tensor(name, shape, dt))
    p1t = lambda name, shape, dt=F32: es1.enter_context(nc.psum_tensor(name, shape, dt))

    wq_sb = s1("p1_wq", [P, 16 * D], BF16)
    wk_sb = s1("p1_wk", [P, 16 * 512], BF16)
    wv_sb = s1("p1_wv", [P, 16 * 512], BF16)
    xt = [s1(f"p1_xt{i}", [P, D], F32) for i in range(4)]
    xlnbf = [s1(f"p1_xlnbf{i}", [P, D], BF16) for i in range(2)]
    xfm = s1("p1_xfm", [P, 16 * 512], BF16)
    bn6 = [s1(f"p1_bn6{i}", [P, 24], F32) for i in range(4)]
    mvg = s1("p1_mvg", [P, 8], F32)
    rstd4 = s1("p1_rstd4", [P, 4], F32)
    lnscr = s1("p1_lnscr", [P, 4], F32)
    kstage = [s1(f"p1_kst{i}", [P, 512], BF16) for i in range(2)]
    vstage = [s1(f"p1_vst{i}", [P, 4 * 129], BF16) for i in range(2)]
    qstage = [s1(f"p1_qst{i}", [P, 512], BF16) for i in range(2)]

    psT = [p1t(f"p1_psT{i}", [P, 1024], BF16) for i in range(2)]
    q_ps = [p1t(f"p1_qps{i}", [P, 512]) for i in range(2)]
    k_ps = q_ps
    v_ps = [p1t(f"p1_vps{i}", [P, 512]) for i in range(2)]

    with nc.Block("ph1") as blk:
        def _(eng):
            C.incd(eng.dma_start(out=wq_sb.ap().rearrange("p (c f) -> p c f", c=16),
                                 in_=wqT.rearrange("(c p) f -> p c f", p=P)), "w1x")
            C.incd(eng.dma_start(out=wk_sb.ap().rearrange("p (c f) -> p c f", c=16),
                                 in_=wkT.rearrange("(c p) f -> p c f", p=P)), "w1x")
            C.incd(eng.dma_start(out=wv_sb.ap().rearrange("p (c f) -> p c f", c=16),
                                 in_=wvT.rearrange("(c p) f -> p c f", p=P)), "w1x")
            for i in range(2):
                for kv in range(4):
                    ins = eng.memset(vstage[i][:, kv * 129 + 128:kv * 129 + 129], 1.0)
            C.inc(ins, "G")
        blk.gpsimd(_)
        tok_w = C.now("w1x")
        tok_vst = C.now("G")

        xt_free = [0, 0, 0, 0]
        xln_free = [0, 0]
        psT_free = [0, 0]
        kst_free = [0, 0]
        vst_free = [0, 0]
        qst_free = [0, 0]
        qkv_ps_free = [0, 0]
        xfm_free = 0

        def ln_group(src_tensor, g):
            stat_toks = []
            for t in range(4):
                tile = g * 4 + t
                buf = t

                def _(eng, tile=tile, buf=buf):
                    C.wait(eng, "V", xt_free[buf])
                    C.incd(eng.dma_start(out=xt[buf][:],
                                         in_=src_tensor[tile * P:(tile + 1) * P, :]),
                           f"xt{buf}")
                blk.sync(_)
                tok_x = C.now(f"xt{buf}")

                def _(eng, t=t, buf=buf, tok_x=tok_x):
                    C.wait(eng, f"xt{buf}", tok_x)
                    for i in range(4):
                        ins = eng.bn_stats(out=bn6[buf][:, 6 * i:6 * i + 6],
                                           in_=xt[buf][:, 512 * i:512 * (i + 1)])
                    tv = C.inc(ins, "V")
                    C.wait(eng, "V", tv)
                    C.inc(eng.bn_aggr(out=mvg[:, t * 2:t * 2 + 2], in_=bn6[buf][:]), "V")
                blk.vector(_)
                stat_toks.append(C.now("V"))

            def _(eng, tok=stat_toks[-1]):
                C.wait(eng, "V", tok)
                vr = mvg.ap().rearrange("p (t two) -> p t two", two=2)[:, :, 1:2]
                ts_ = C.inc(eng.activation(out=lnscr[:, 0:4], in_=vr,
                                           func=AF.Ln, bias=epsap[:]), "S")
                C.wait(eng, "S", ts_)
                C.inc(eng.activation(out=rstd4[:, 0:4], in_=lnscr[:, 0:4],
                                     func=AF.Exp, scale=-0.5), "S")
            blk.scalar(_)
            tok_rs = C.now("S")

            for t in range(4):
                buf = t

                def _(eng, t=t, buf=buf, tok_rs=tok_rs):
                    C.wait(eng, "S", tok_rs)
                    C.wait(eng, "T", xln_free[buf % 2])
                    C.inc(eng.tensor_scalar(out=xlnbf[buf % 2][:], in0=xt[buf][:],
                                            scalar1=mvg[:, t * 2:t * 2 + 1],
                                            scalar2=rstd4[:, t:t + 1],
                                            op0=ALU.subtract, op1=ALU.mult), "V")
                blk.vector(_)
                xt_free[buf] = C.now("V")
                tok_xln = C.now("V")

                def _(eng, buf=buf, tok_xln=tok_xln):
                    C.wait(eng, "V", tok_xln)
                    C.wait(eng, "V", psT_free[0])
                    for c in range(8):
                        ins = eng.transpose(out=psT[0][:, c * 128:(c + 1) * 128],
                                            in_=xlnbf[buf % 2][:, c * 128:(c + 1) * 128],
                                            identity=ident[:])
                    C.inc(ins, "T")
                    C.wait(eng, "V", psT_free[1])
                    for c in range(8):
                        ins = eng.transpose(out=psT[1][:, c * 128:(c + 1) * 128],
                                            in_=xlnbf[buf % 2][:, (8 + c) * 128:(9 + c) * 128],
                                            identity=ident[:])
                    C.inc(ins, "T")
                blk.tensor(_)
                xln_free[buf % 2] = C.now("T")
                tok_tr = C.now("T")

                def _(eng, t=t, tok_tr=tok_tr):
                    xfm3d = xfm.ap().rearrange("p (c f) -> p c f", f=512)
                    C.wait(eng, "T", tok_tr - 1)
                    C.wait(eng, "T", xfm_free)
                    dst0 = xfm3d[:, 0:8, t * 128:(t + 1) * 128]
                    C.inc(eng.tensor_copy(out=dst0, in_=psT[0].ap().rearrange(
                        "p (c f) -> p c f", f=128)), "V")
                    psT_free[0] = C.now("V")
                    C.wait(eng, "T", tok_tr)
                    dst1 = xfm3d[:, 8:16, t * 128:(t + 1) * 128]
                    C.inc(eng.tensor_copy(out=dst1, in_=psT[1].ap().rearrange(
                        "p (c f) -> p c f", f=128)), "V")
                    psT_free[1] = C.now("V")
                blk.vector(_)
            return C.now("V")

        for g in range(8):
            tok_xfm = ln_group(xseq, g)

            for fb in range(4):
                pb = fb % 2

                def _(eng, fb=fb, pb=pb):
                    C.wait(eng, "V", tok_xfm)
                    C.wait(eng, "w1x", tok_w)
                    C.wait(eng, "S", qkv_ps_free[pb])
                    for dc in range(16):
                        ins = eng.matmul(out=k_ps[pb][:],
                                         lhsT=wk_sb[:, dc * 512 + fb * 128:dc * 512 + (fb + 1) * 128],
                                         rhs=xfm[:, dc * 512:(dc + 1) * 512],
                                         start=(dc == 0), stop=(dc == 15))
                    C.inc(ins, "T")
                blk.tensor(_)
                tok_mm = C.now("T")

                def _(eng, fb=fb, pb=pb, tok_mm=tok_mm):
                    C.wait(eng, "T", tok_mm)
                    C.wait(eng, f"kst{pb}", kst_free[pb])
                    C.inc(eng.copy(out=kstage[pb][:], in_=k_ps[pb][:]), "S")
                blk.scalar(_)
                qkv_ps_free[pb] = C.now("S")
                tok_st = C.now("S")

                def _(eng, fb=fb, pb=pb, g=g, tok_st=tok_st):
                    C.wait(eng, "S", tok_st)
                    C.incd(eng.dma_start(out=k_scr[fb, :, g * 512:(g + 1) * 512],
                                         in_=kstage[pb][:]), f"kst{pb}")
                blk.sync(_)
                kst_free[pb] = C.now(f"kst{pb}")

            for t in range(4):
                pb = t % 2

                def _(eng, t=t, pb=pb):
                    C.wait(eng, "V", tok_xfm)
                    C.wait(eng, "S", qkv_ps_free[pb])
                    for dc in range(16):
                        ins = eng.matmul(out=v_ps[pb][:],
                                         lhsT=xfm[:, dc * 512 + t * 128:dc * 512 + (t + 1) * 128],
                                         rhs=wv_sb[:, dc * 512:(dc + 1) * 512],
                                         start=(dc == 0), stop=(dc == 15))
                    C.inc(ins, "T")
                blk.tensor(_)
                tok_mm = C.now("T")

                def _(eng, t=t, pb=pb, tok_mm=tok_mm):
                    C.wait(eng, "T", tok_mm)
                    C.wait(eng, f"vst{pb}", vst_free[pb])
                    C.wait(eng, "G", tok_vst)
                    for kv in range(4):
                        ins = eng.copy(out=vstage[pb][:, kv * 129:kv * 129 + 128],
                                       in_=v_ps[pb][:, kv * 128:(kv + 1) * 128])
                    C.inc(ins, "S")
                blk.scalar(_)
                qkv_ps_free[pb] = C.now("S")
                tok_st = C.now("S")

                def _(eng, t=t, pb=pb, g=g, tok_st=tok_st):
                    C.wait(eng, "S", tok_st)
                    C.incd(eng.dma_start(out=v_scr[g * 4 + t, :, :], in_=vstage[pb][:]), f"vst{pb}")
                blk.sync(_)
                vst_free[pb] = C.now(f"vst{pb}")

            xfm_free_local = C.now("T")
            xfm_free = xfm_free_local

        for g in range(4):
            tok_xfm = ln_group(xown, g)

            for fc in range(16):
                pb = fc % 2

                def _(eng, fc=fc, pb=pb):
                    C.wait(eng, "V", tok_xfm)
                    C.wait(eng, "S", qkv_ps_free[pb])
                    for dc in range(16):
                        ins = eng.matmul(out=q_ps[pb][:],
                                         lhsT=wq_sb[:, dc * D + fc * 128:dc * D + (fc + 1) * 128],
                                         rhs=xfm[:, dc * 512:(dc + 1) * 512],
                                         start=(dc == 0), stop=(dc == 15))
                    C.inc(ins, "T")
                blk.tensor(_)
                tok_mm = C.now("T")

                def _(eng, fc=fc, pb=pb, tok_mm=tok_mm):
                    C.wait(eng, "T", tok_mm)
                    C.wait(eng, f"qst{pb}", qst_free[pb])
                    C.inc(eng.copy(out=qstage[pb][:], in_=q_ps[pb][:]), "S")
                blk.scalar(_)
                qkv_ps_free[pb] = C.now("S")
                tok_st = C.now("S")

                def _(eng, fc=fc, pb=pb, g=g, tok_st=tok_st):
                    C.wait(eng, "S", tok_st)
                    C.incd(eng.dma_start(out=q_scr[fc, :, g * 512:(g + 1) * 512],
                                         in_=qstage[pb][:]), f"qst{pb}")
                blk.sync(_)
                qst_free[pb] = C.now(f"qst{pb}")

            xfm_free = C.now("T")

    es1.close()

    es2 = contextlib.ExitStack()
    s2 = lambda name, shape, dt: es2.enter_context(nc.sbuf_tensor(name, shape, dt))
    p2t = lambda name, shape, dt=F32: es2.enter_context(nc.psum_tensor(name, shape, dt))

    k_sb = s2("p2_k", [P, 4 * T], BF16)
    v_sb = s2("p2_v", [P, 4 * 32 * 129], BF16)
    wo_sb = s2("p2_wo", [P, 16 * D], BF16)
    afm = s2("p2_afm", [P, NPAIR * 512], BF16)
    mask_sb = s2("p2_mask", [P, NMASK * 512], BF16)
    q1b = [s2(f"p2_q1{i}", [P, 512], BF16) for i in range(4)]
    q2b = [s2(f"p2_q2{i}", [P, 512], BF16) for i in range(4)]
    pbuf = [s2(f"p2_p{i}", [P, 512], BF16) for i in range(4)]
    cmb = s2("p2_cmb", [P, 1040], F32)
    abf4 = [s2(f"p2_abf4{i}", [P, 512], BF16) for i in range(2)]
    ysb = s2("p2_ysb", [P, D], F32)
    xo = s2("p2_xo", [P, D], F32)
    xm = s2("p2_xm", [P, D], F32)
    bn6y = s2("p2_bn6y", [P, 24], F32)
    mvy = s2("p2_mvy", [P, 2], F32)
    rsy = s2("p2_rsy", [P, 2], F32)

    s3_ps = p2t("p2_s3", [P, 1024])
    shps2 = p2t("p2_sh2", [P, 512])
    pv4 = p2t("p2_pv4", [P, 2048])
    shps = p2t("p2_sh", [P, 512])
    trview = shps.ap().bitcast(BF16)

    with nc.Block("ph2") as blk:
        def _(eng):
            for key in ("kst0", "kst1", "vst0", "vst1", "qst0", "qst1"):
                C.wait(eng, key, C.now(key))
            C.incd(eng.dma_start(out=wo_sb.ap().rearrange("p (c f) -> p c f", c=16),
                                 in_=woT.rearrange("(c p) f -> p c f", p=P)), "wo2")
        blk.gpsimd(_)
        tok_wo = C.now("wo2")

        def _(eng):
            for key in ("kst0", "kst1", "vst0", "vst1", "qst0", "qst1"):
                C.wait(eng, key, C.now(key))
            for fb in range(4):
                C.incd(eng.dma_start(out=k_sb[:, fb * T:(fb + 1) * T], in_=k_scr[fb]), "kv2")
            for kv in range(4):
                C.incd(eng.dma_start(
                    out=v_sb.ap()[:, kv * 32 * 129:(kv + 1) * 32 * 129].rearrange(
                        "p (tc c) -> p tc c", c=129),
                    in_=v_scr.rearrange("tc p c -> p tc c")[:, :, kv * 129:(kv + 1) * 129]), "kv2")
        blk.sync(_)
        tok_kv = C.now("kv2")

        def _(eng):
            for key in ("kst0", "kst1", "vst0", "vst1", "qst0", "qst1"):
                C.wait(eng, key, C.now(key))
            for b in q1b + q2b:
                C.inc(eng.memset(b[:], 0.0), "V")
        blk.vector(_)
        tok_qz = C.now("V")

        q_free = [0, 0, 0, 0]
        pend_tr = []
        abf_freeT = [0, 0]
        sbank_last = [0, 0]
        pmask_last = 0
        pv_cons = ("V", 0)
        trv_consumed = 0
        afm_free = 0
        mask_free = 0
        yps_free = 0
        yps_free2 = [0, 0]
        xm_freeV = 0
        xm_freeDs = 0

        for g in range(4):
            Eg = ESLOT[g]
            nm0 = Eg - NMASK

            def _(eng, g=g):
                C.wait(eng, "V", mask_free)
                C.incd(eng.dma_start(
                    out=mask_sb.ap().rearrange("p (s c) -> p s c", c=512),
                    in_=mask01[g].rearrange("s p c -> p s c")), "msk")
            blk.sync(_)
            tok_mask = C.now("msk")

            def emit_qdma(h):
                qb = h % 4
                kt = h // 2
                base = (kt % 2) * 64

                def _(eng, h=h, qb=qb, base=base, g=g):
                    C.wait(eng, "T", q_free[qb])
                    if g == 0 and h < 4:
                        C.wait(eng, "V", tok_qz)
                    C.incd(eng.dma_start(out=q1b[qb][base:base + 64, :],
                                         in_=q_scr[h, 0:64, g * 512:(g + 1) * 512]), f"q1_{qb}")
                    C.incd(eng.dma_start(out=q2b[qb][base:base + 64, :],
                                         in_=q_scr[h, 64:128, g * 512:(g + 1) * 512]), f"q2_{qb}")
                blk.sync(_)
                return (C.now(f"q1_{qb}"), C.now(f"q2_{qb}"), qb)

            tok_q_next = emit_qdma(0)
            for h in range(NPAIR):
                kt = h // 2
                base = (kt % 2) * 64
                ktile = kt // 2
                vh = h // 4
                qb = h % 4

                tok_q = tok_q_next
                if h + 1 < NPAIR:
                    tok_q_next = emit_qdma(h + 1)

                if len(pend_tr) > 1:
                    etr, eev = pend_tr.pop(0)
                    blk.tensor(etr)
                    _tok_trp = C.now("T")
                    blk.vector(lambda eng, eev=eev, _tok_trp=_tok_trp: eev(
                        eng, tok_trp=_tok_trp))
                    trv_consumed = C.now("V")

                base_T, base_S, base_V = C.now("T"), C.now("S"), C.now("V")
                tok_mm = {}
                tok_pv = {}

                def _(eng, h=h, qb=qb, base=base, ktile=ktile, vh=vh, Eg=Eg,
                      nm0=nm0, tok_q=tok_q, base_S=base_S, base_V=base_V,
                      sb_prev=tuple(sbank_last), pv_cons=pv_cons):
                    C.wait(eng, f"q1_{tok_q[2]}", tok_q[0])
                    C.wait(eng, f"q2_{tok_q[2]}", tok_q[1])
                    C.wait(eng, "kv2", tok_kv)

                    def emit_pv(s):
                        masked = s >= nm0
                        if masked:
                            C.wait(eng, "V", base_V + 2 * (s - nm0) + 2)
                        else:
                            C.wait(eng, "S", base_S + 2 * s + 2)
                        p1 = pbuf[s % 2]
                        p2 = pbuf[2 + s % 2]
                        vcol = vh * 32 * 129 + s * 129
                        for tt in range(4):
                            ins = eng.matmul(out=pv4[:, tt * 512:tt * 512 + 129],
                                             lhsT=p1[:, tt * 128:(tt + 1) * 128],
                                             rhs=v_sb[:, vcol:vcol + 129],
                                             start=False, stop=False, skip_group_check=True)
                            ins = eng.matmul(out=pv4[:, tt * 512 + 129:tt * 512 + 258],
                                             lhsT=p2[:, tt * 128:(tt + 1) * 128],
                                             rhs=v_sb[:, vcol:vcol + 129],
                                             start=False, stop=False, skip_group_check=True)
                        tok_pv[s] = C.inc(ins, "T")

                    for s in range(Eg):
                        if s == 1:
                            C.wait(eng, pv_cons[0], pv_cons[1])
                            for tt in range(4):
                                ins = eng.matmul(
                                    out=pv4[:, tt * 512:tt * 512 + 258],
                                    lhsT=zrow[:], rhs=orow[:, 0:258],
                                    start=True, stop=False, skip_group_check=True)
                            C.inc(ins, "T")
                        for half in range(2):
                            k_idx = 2 * s + half
                            bank = k_idx % 2
                            if k_idx >= 2:
                                C.wait(eng, "S", base_S + k_idx - 1)
                            else:
                                C.wait(eng, "S", sb_prev[bank])
                            qsrc = (q1b if half == 0 else q2b)[qb]
                            ins = eng.matmul(
                                out=s3_ps[:, bank * 512:(bank + 1) * 512],
                                lhsT=k_sb[:, ktile * T + s * 128:ktile * T + (s + 1) * 128],
                                rhs=qsrc[:], start=True, stop=True)
                            tok_mm[(s, half)] = C.inc(ins, "T")
                        if s >= 1:
                            emit_pv(s - 1)
                    emit_pv(Eg - 1)
                blk.tensor(_)
                q_free[qb] = C.now("T")
                tok_pv_last = C.now("T")
                for _k in range(2 * Eg - 2, 2 * Eg):
                    sbank_last[_k % 2] = base_S + _k + 1

                def _(eng, Eg=Eg, nm0=nm0, base_S=base_S, base_V=base_V,
                      pml=pmask_last):
                    C.wait(eng, "V", pml)
                    for s in range(Eg):
                        for half in range(2):
                            k_idx = 2 * s + half
                            bank = k_idx % 2
                            C.wait(eng, "T", tok_mm[(s, half)])
                            if s - 2 >= nm0:
                                C.wait(eng, "V", base_V + 2 * (s - 2 - nm0) + 1 + half)
                            dst = (pbuf[s % 2] if half == 0 else pbuf[2 + s % 2])
                            tok = C.inc(eng.activation(
                                out=dst[:], in_=s3_ps[:, bank * 512:(bank + 1) * 512],
                                func=AF.Exp, scale=0.125), "S")
                            assert tok == base_S + k_idx + 1
                blk.scalar(_)

                def _(eng, Eg=Eg, nm0=nm0, base_S=base_S, base_V=base_V,
                      tok_mask=tok_mask):
                    C.wait(eng, "msk", tok_mask)
                    for s in range(nm0, Eg):
                        si = s - nm0
                        for half in range(2):
                            C.wait(eng, "S", base_S + 2 * s + 1 + half)
                            if s - 2 >= 0:
                                C.wait(eng, "T", tok_pv.get(s - 2, 0))
                            src = pbuf[s % 2] if half == 0 else pbuf[2 + s % 2]
                            tok = C.inc(eng.tensor_tensor(
                                out=src[:], in0=src[:],
                                in1=mask_sb[:, si * 512:(si + 1) * 512],
                                op=ALU.mult), "V")
                            assert tok == base_V + 2 * si + 1 + half
                blk.vector(_)
                pmask_last = base_V + 2 * NMASK

                pv_free_tok = [0]
                pv3 = pv4.ap().rearrange("p (t c) -> p t c", c=512)
                def _(eng, tok_pv_last=tok_pv_last):
                    C.wait(eng, "T", tok_pv_last)
                    t = C.inc(eng.reciprocal(out=cmb.ap()[:, 1024:1028].unsqueeze(2),
                                             in_=pv3[:, :, 128:129]), "V")
                    C.wait(eng, "V", t)
                    t = C.inc(eng.reciprocal(out=cmb.ap()[:, 1028:1032].unsqueeze(2),
                                             in_=pv3[:, :, 257:258]), "V")
                    C.wait(eng, "V", t)
                    t = C.inc(eng.tensor_scalar(out=cmb[:, 1032:1036], in0=cmb[:, 1028:1032],
                                                scalar1=lambc[:], scalar2=-1.0,
                                                op0=ALU.mult, op1=ALU.mult), "V")
                    C.wait(eng, "V", t)
                    A3 = cmb.ap()[:, 0:512].rearrange("p (t c) -> p t c", c=128)
                    B3 = cmb.ap()[:, 512:1024].rearrange("p (t c) -> p t c", c=128)
                    a1bc = cmb.ap()[:, 1024:1028].unsqueeze(2).broadcast_to([P, 4, 128])
                    a2bc = cmb.ap()[:, 1032:1036].unsqueeze(2).broadcast_to([P, 4, 128])
                    t = C.inc(eng.tensor_tensor(out=A3, in0=pv3[:, :, 0:128],
                                                in1=a1bc, op=ALU.mult), "V")
                    C.wait(eng, "V", t)
                    t = C.inc(eng.tensor_tensor(out=B3, in0=pv3[:, :, 129:257],
                                                in1=a2bc, op=ALU.mult), "V")
                    C.wait(eng, "V", t)
                    pv_free_tok[0] = t
                    t = C.inc(eng.tensor_tensor(out=cmb[:, 0:512], in0=cmb[:, 0:512],
                                                in1=cmb[:, 512:1024], op=ALU.add), "V")
                    C.wait(eng, "V", t)
                    t = C.inc(eng.tensor_tensor(out=cmb[:, 512:1024], in0=cmb[:, 0:512],
                                                in1=cmb[:, 0:512], op=ALU.mult), "V")
                    C.wait(eng, "V", t)
                    t = C.inc(eng.reduce_sum(out=cmb[:, 1036:1040], in_=B3, axis=AX.X), "V")
                    C.wait(eng, "V", t)
                    t = C.inc(eng.tensor_scalar(out=cmb[:, 1036:1040], in0=cmb[:, 1036:1040],
                                                scalar1=1.0 / VD, scalar2=None,
                                                op0=ALU.mult), "V")
                    C.wait(eng, "V", t)
                blk.vector(_)
                tok_cmb1 = C.now("V")
                pv_cons = ("V", pv_free_tok[0])

                def _(eng, tok_cmb1=tok_cmb1):
                    C.wait(eng, "V", tok_cmb1)
                    t = C.inc(eng.activation(out=cmb[:, 1028:1032], in_=cmb[:, 1036:1040],
                                             func=AF.Ln, bias=epsap[:]), "S")
                    C.wait(eng, "S", t)
                    C.inc(eng.activation(out=cmb[:, 1024:1028], in_=cmb[:, 1028:1032],
                                         func=AF.Exp, scale=-0.5), "S")
                blk.scalar(_)
                tok_rms = C.now("S")

                def _(eng, h=h, tok_rms=tok_rms):
                    C.wait(eng, "S", tok_rms)
                    A3 = cmb.ap()[:, 0:512].rearrange("p (t c) -> p t c", c=128)
                    B3 = cmb.ap()[:, 512:1024].rearrange("p (t c) -> p t c", c=128)
                    rbc = cmb.ap()[:, 1024:1028].unsqueeze(2).broadcast_to([P, 4, 128])
                    lnw3 = lnw.ap().unsqueeze(1).broadcast_to([P, 4, 128])
                    t = C.inc(eng.tensor_tensor(out=B3, in0=A3, in1=rbc, op=ALU.mult), "V")
                    C.wait(eng, "V", t)
                    C.wait(eng, "T", abf_freeT[h % 2])
                    C.inc(eng.tensor_tensor(out=abf4[h % 2].ap().rearrange(
                        "p (t c) -> p t c", c=128), in0=B3, in1=lnw3, op=ALU.mult), "V")
                blk.vector(_)
                tok_abf = C.now("V")

                def emit_attn_tr(eng, h=h, tok_abf=tok_abf):
                    C.wait(eng, "V", tok_abf)
                    C.wait(eng, "V", trv_consumed)
                    C.wait(eng, "V", yps_free2[0])
                    for tt in range(4):
                        ins = eng.transpose(out=trview[:, tt * 128:(tt + 1) * 128],
                                            in_=abf4[h % 2][:, tt * 128:(tt + 1) * 128],
                                            identity=ident[:])
                    C.inc(ins, "T")
                    abf_freeT[h % 2] = C.now("T")

                def emit_attn_ev(eng, h=h, tok_trp=None):
                    C.wait(eng, "T", tok_trp)
                    C.wait(eng, "T", afm_free)
                    C.inc(eng.tensor_copy(out=afm[:, h * 512:(h + 1) * 512],
                                          in_=trview[:, 0:512]), "V")

                pend_tr.append((emit_attn_tr, emit_attn_ev))
                if h == NPAIR - 1:
                    while pend_tr:
                        etr, eev = pend_tr.pop(0)
                        blk.tensor(etr)
                        tok_trp = C.now("T")
                        blk.vector(lambda eng, eev=eev, tok_trp=tok_trp: eev(
                            eng, tok_trp=tok_trp))
                        trv_consumed = C.now("V")
            tok_afm = C.now("V")
            mask_free = C.now("V")

            for tt in range(4):
                tile = g * 4 + tt

                def _(eng, tile=tile):
                    C.wait(eng, "V", xm_freeV)
                    C.incd(eng.dma_start(out=xo[:], in_=xown[tile * P:(tile + 1) * P, :]), "xo2")
                blk.sync(_)
                tok_xo = C.now("xo2")

                for dcol in range(4):
                    wops = shps if dcol % 2 == 0 else shps2
                    def _(eng, tt=tt, dcol=dcol, wops=wops, tok_afm=tok_afm,
                          ypf=yps_free2[dcol % 2]):
                        C.wait(eng, "V", tok_afm)
                        C.wait(eng, "wo2", tok_wo)
                        C.wait(eng, "V", ypf)
                        for fc in range(16):
                            ins = eng.matmul(
                                out=wops[:],
                                lhsT=afm[:, fc * 512 + tt * 128:fc * 512 + (tt + 1) * 128],
                                rhs=wo_sb[:, fc * D + dcol * 512:fc * D + (dcol + 1) * 512],
                                start=(fc == 0), stop=(fc == 15))
                        C.inc(ins, "T")
                    blk.tensor(_)
                    tok_y = C.now("T")

                    def _(eng, dcol=dcol, wops=wops, tok_y=tok_y):
                        C.wait(eng, "T", tok_y)
                        C.wait(eng, "V", xm_freeV)
                        C.inc(eng.tensor_copy(out=ysb[:, dcol * 512:(dcol + 1) * 512],
                                              in_=wops[:]), "V")
                    blk.vector(_)
                    yps_free2[dcol % 2] = C.now("V")
                    yps_free = C.now("V")
                tok_ye = C.now("V")

                def _(eng, tok_ye=tok_ye):
                    C.wait(eng, "V", tok_ye)
                    for i in range(4):
                        ins = eng.bn_stats(out=bn6y[:, 6 * i:6 * i + 6],
                                           in_=ysb[:, 512 * i:512 * (i + 1)])
                    t = C.inc(ins, "V")
                    C.wait(eng, "V", t)
                    C.inc(eng.bn_aggr(out=mvy[:], in_=bn6y[:]), "V")
                blk.vector(_)
                tok_mvy = C.now("V")

                def _(eng, tok_mvy=tok_mvy):
                    C.wait(eng, "V", tok_mvy)
                    t = C.inc(eng.activation(out=rsy[:, 0:1], in_=mvy[:, 1:2],
                                             func=AF.Ln, bias=epsap[:]), "S")
                    C.wait(eng, "S", t)
                    C.inc(eng.activation(out=rsy[:, 1:2], in_=rsy[:, 0:1],
                                         func=AF.Exp, scale=-0.5), "S")
                blk.scalar(_)
                tok_rsy = C.now("S")

                def _(eng, tok_rsy=tok_rsy, tok_xo=tok_xo):
                    C.wait(eng, "S", tok_rsy)
                    C.wait(eng, "xo2", tok_xo)
                    C.wait(eng, "xmw", xm_freeDs)
                    t = C.inc(eng.tensor_scalar(out=xm[:], in0=ysb[:],
                                                scalar1=mvy[:, 0:1], scalar2=rsy[:, 1:2],
                                                op0=ALU.subtract, op1=ALU.mult), "V")
                    C.wait(eng, "V", t)
                    C.inc(eng.tensor_tensor(out=xm[:], in0=xm[:], in1=xo[:], op=ALU.add), "V")
                blk.vector(_)
                tok_xm = C.now("V")
                xm_freeV = tok_xm

                def _(eng, tile=tile, tok_xm=tok_xm):
                    C.wait(eng, "V", tok_xm)
                    C.incd(eng.dma_start(out=xmid_scr[tile], in_=xm[:]), "xmw")
                blk.sync(_)
                xm_freeDs = C.now("xmw")
            afm_free = C.now("T")

    es2.close()

    es3 = contextlib.ExitStack()
    s3 = lambda name, shape, dt: es3.enter_context(nc.sbuf_tensor(name, shape, dt))

    xfm2 = s3("p3_xfm2", [P, 16 * 1024], BF16)
    acc = s3("p3_acc", [P, 16 * 1024], F32)
    gT = s3("p3_gT", [16, 1024], BF16)
    m3sb = s3("p3_m3", [P, 8], F32)
    r3sb = s3("p3_r3", [P, 8], F32)
    centT_sb = s3("p3_cent", [P, 16 * NRE], BF16)
    eoh_sb = s3("p3_eoh", [16, NRE * P], BF16)
    b1_sb = s3("p3_b1", [P, NEXP * 11], F32)
    b2r_sb = s3("p3_b2r", [16, D], BF16)
    sb2_sb = s3("p3_sb2", [NSH, D], BF16)
    bias_sb = s3("p3_bias", [P, NRE], F32)

    with nc.Block("ph3pre") as blk:
        def _(eng):
            C.wait(eng, "xmw", C.now("xmw"))
            C.incd(eng.dma_start(out=centT_sb.ap().rearrange("p (c e) -> p c e", c=16),
                                 in_=centT.rearrange("(c p) e -> p c e", p=P)), "c3g")
            C.incd(eng.dma_start(out=eoh_sb[:], in_=eoh_in), "c3g")
            C.incd(eng.dma_start(out=b2r_sb[:], in_=b2r), "c3g")
            C.incd(eng.dma_start(out=sb2_sb[:], in_=sb2), "c3g")
        blk.gpsimd(_)

        def _(eng):
            C.wait(eng, "xmw", C.now("xmw"))
            C.incd(eng.dma_start(out=b1_sb[:], in_=b1all), "c3s")
            C.incd(eng.dma_start(out=bias_sb[:], in_=biasbc), "c3s")
        blk.sync(_)
    tok_p3c = (C.now("c3g"), C.now("c3s"))

    for g2 in range(2):
        esA = contextlib.ExitStack()
        sA = lambda name, shape, dt: esA.enter_context(nc.sbuf_tensor(name, shape, dt))
        pA = lambda name, shape, dt=F32: esA.enter_context(nc.psum_tensor(name, shape, dt))
        xmt = [sA(f"p3a_xm{g2}_{i}", [P, D], F32) for i in range(4)]
        x2bf = [sA(f"p3a_x2{g2}_{i}", [P, D], BF16) for i in range(2)]
        bn6b = [sA(f"p3a_bn6{g2}_{i}", [P, 24], F32) for i in range(4)]
        mvall = sA(f"p3a_mva{g2}", [P, 16], F32)
        gsc = sA(f"p3a_gsc{g2}", [P, 640], F32)
        gall = sA(f"p3a_gall{g2}", [P, 8 * 16], BF16)
        psT3 = [pA(f"p3a_psT{g2}_{i}", [P, 1024], BF16) for i in range(2)]
        g_ps = [pA(f"p3a_gps{g2}_{i}", [P, 16]) for i in range(2)]
        gt_ps = pA(f"p3a_gtps{g2}", [16, P], BF16)

        with nc.Block(f"ph3a_{g2}") as blk:
            xmt_free = [0, 0, 0, 0]
            psT3_free = [0, 0]
            x2_free = [0, 0]
            stat_toks = []
            for tt in range(8):
                tile = g2 * 8 + tt
                buf = tt % 4

                def _(eng, tile=tile, buf=buf, tt=tt):
                    if tt == 0:
                        C.wait(eng, "yo0", C.now("yo0"))
                        C.wait(eng, "yo1", C.now("yo1"))
                    C.wait(eng, "V", xmt_free[buf])
                    C.incd(eng.dma_start(out=xmt[buf][:], in_=xmid_scr[tile]), f"xma{buf}")
                blk.sync(_)
                tok_x = C.now(f"xma{buf}")

                def _(eng, tt=tt, buf=buf, tok_x=tok_x):
                    C.wait(eng, f"xma{buf}", tok_x)
                    for i in range(4):
                        ins = eng.bn_stats(out=bn6b[buf][:, 6 * i:6 * i + 6],
                                           in_=xmt[buf][:, 512 * i:512 * (i + 1)])
                    t = C.inc(ins, "V")
                    C.wait(eng, "V", t)
                    C.inc(eng.bn_aggr(out=mvall[:, tt * 2:tt * 2 + 2], in_=bn6b[buf][:]), "V")
                blk.vector(_)
                xmt_free[buf] = C.now("V")
                stat_toks.append(C.now("V"))

            def _(eng):
                C.wait(eng, "V", stat_toks[-1])
                vr = mvall.ap().rearrange("p (t two) -> p t two", two=2)[:, :, 1:2]
                t = C.inc(eng.activation(out=gsc[:, 512:520], in_=vr,
                                         func=AF.Ln, bias=epsap[:]), "S")
                C.wait(eng, "S", t)
                C.inc(eng.activation(out=r3sb[:, 0:8], in_=gsc[:, 512:520],
                                     func=AF.Exp, scale=-0.5), "S")
            blk.scalar(_)
            tok_r3 = C.now("S")

            def _(eng):
                C.wait(eng, "S", tok_r3)
                mcols = mvall.ap().rearrange("p (t two) -> p t two", two=2)[:, :, 0:1]
                C.inc(eng.tensor_copy(out=m3sb.ap().unsqueeze(2), in_=mcols), "V")
            blk.vector(_)
            tok_m3 = C.now("V")

            xmt_free2 = list(xmt_free)
            gexp_toks = []
            for tt in range(8):
                tile = g2 * 8 + tt
                buf = tt % 4
                b2f = tt % 2

                def _(eng, tile=tile, buf=buf, tt=tt):
                    C.wait(eng, "V", xmt_free2[buf])
                    C.incd(eng.dma_start(out=xmt[buf][:], in_=xmid_scr[tile]), f"xma{buf}")
                blk.sync(_)
                tok_x = C.now(f"xma{buf}")

                def _(eng, tt=tt, buf=buf, b2f=b2f, tok_x=tok_x):
                    C.wait(eng, f"xma{buf}", tok_x)
                    C.wait(eng, "V", tok_m3)
                    C.wait(eng, "T", x2_free[b2f])
                    C.inc(eng.tensor_scalar(out=x2bf[b2f][:], in0=xmt[buf][:],
                                            scalar1=m3sb[:, tt:tt + 1],
                                            scalar2=r3sb[:, tt:tt + 1],
                                            op0=ALU.subtract, op1=ALU.mult), "V")
                blk.vector(_)
                xmt_free2[buf] = C.now("V")
                tok_x2 = C.now("V")

                def _(eng, tt=tt, b2f=b2f, tok_x2=tok_x2):
                    C.wait(eng, "V", tok_x2)
                    C.wait(eng, "V", psT3_free[0] if b2f == 0 else psT3_free[1])
                    for c in range(8):
                        ins = eng.transpose(out=psT3[b2f][:, c * 128:(c + 1) * 128],
                                            in_=x2bf[b2f][:, c * 128:(c + 1) * 128],
                                            identity=ident[:])
                    C.inc(ins, "T")
                blk.tensor(_)
                tok_tr1 = C.now("T")

                def _(eng, tt=tt, b2f=b2f, tok_tr1=tok_tr1):
                    xfm23d = xfm2.ap().rearrange("p (c f) -> p c f", f=1024)
                    C.wait(eng, "T", tok_tr1)
                    C.inc(eng.tensor_copy(out=xfm23d[:, 0:8, tt * 128:(tt + 1) * 128],
                                          in_=psT3[b2f].ap().rearrange(
                                              "p (c f) -> p c f", f=128)), "V")
                blk.vector(_)
                tok_ev1 = C.now("V")

                def _(eng, tt=tt, b2f=b2f, tok_ev1=tok_ev1):
                    C.wait(eng, "V", tok_ev1)
                    for c in range(8):
                        ins = eng.transpose(out=psT3[b2f][:, c * 128:(c + 1) * 128],
                                            in_=x2bf[b2f][:, (8 + c) * 128:(9 + c) * 128],
                                            identity=ident[:])
                    C.inc(ins, "T")
                blk.tensor(_)
                tok_tr2 = C.now("T")
                x2_free[b2f] = tok_tr2

                def _(eng, tt=tt, b2f=b2f, tok_tr2=tok_tr2):
                    xfm23d = xfm2.ap().rearrange("p (c f) -> p c f", f=1024)
                    C.wait(eng, "T", tok_tr2)
                    C.inc(eng.tensor_copy(out=xfm23d[:, 8:16, tt * 128:(tt + 1) * 128],
                                          in_=psT3[b2f].ap().rearrange(
                                              "p (c f) -> p c f", f=128)), "V")
                blk.vector(_)
                psT3_free[b2f] = C.now("V")
                tok_ev2 = C.now("V")

                def _(eng, tt=tt, b2f=b2f, tok_ev2=tok_ev2):
                    C.wait(eng, "V", tok_ev2)
                    C.wait(eng, "c3g", tok_p3c[0])
                    C.wait(eng, "S", gexp_toks[tt - 2] if tt >= 2 else 0)
                    for dc in range(16):
                        ins = eng.matmul(out=g_ps[b2f][:],
                                         lhsT=xfm2[:, dc * 1024 + tt * 128:dc * 1024 + (tt + 1) * 128],
                                         rhs=centT_sb[:, dc * 16:(dc + 1) * 16],
                                         start=(dc == 0), stop=(dc == 15))
                    C.inc(ins, "T")
                blk.tensor(_)
                tok_gmm = C.now("T")

                def _(eng, tt=tt, b2f=b2f, tok_gmm=tok_gmm):
                    C.wait(eng, "T", tok_gmm)
                    C.inc(eng.copy(out=gsc[:, 128 + tt * 16:128 + (tt + 1) * 16],
                                   in_=g_ps[b2f][:]), "S")
                blk.scalar(_)
                gexp_toks.append(C.now("S"))

            def _(eng):
                C.wait(eng, "S", gexp_toks[-1])
                C.inc(eng.activation(out=gsc[:, 0:128], in_=gsc[:, 128:256],
                                     func=AF.Exp, scale=-1.0), "S")
            blk.scalar(_)
            tok_sg = C.now("S")

            def _(eng, tok_sg=tok_sg):
                sc3 = gsc.ap()[:, 0:128].rearrange("p (t e) -> p t e", e=16)
                msk3 = gsc.ap()[:, 256:384].rearrange("p (t e) -> p t e", e=16)
                gate3 = gsc.ap()[:, 384:512].rearrange("p (t e) -> p t e", e=16)
                bias3 = bias_sb.ap().unsqueeze(1).broadcast_to([P, 8, NRE])
                C.wait(eng, "S", tok_sg)
                C.wait(eng, "c3s", tok_p3c[1])
                t = C.inc(eng.tensor_scalar(out=gsc[:, 0:128], in0=gsc[:, 0:128],
                                            scalar1=1.0, scalar2=None, op0=ALU.add), "V")
                C.wait(eng, "V", t)
                t = C.inc(eng.reciprocal(out=gsc[:, 0:128], in_=gsc[:, 0:128]), "V")
                C.wait(eng, "V", t)
                t = C.inc(eng.tensor_tensor(out=sc3, in0=sc3, in1=bias3, op=ALU.add), "V")
                C.wait(eng, "V", t)
                t = C.inc(eng.memset(gsc[:, 384:512], 0.0), "V")
                C.wait(eng, "V", t)
                val3 = gsc.ap()[:, 128:256].rearrange("p (t e) -> p t e", e=16)
                for it in range(4):
                    t = C.inc(eng.reduce_max(out=gsc[:, 528:536], in_=sc3, axis=AX.X), "V")
                    C.wait(eng, "V", t)
                    m8bc = gsc.ap()[:, 528:536].unsqueeze(2).broadcast_to([P, 8, 16])
                    t = C.inc(eng.tensor_tensor(out=msk3, in0=sc3, in1=m8bc,
                                                op=ALU.is_equal), "V")
                    C.wait(eng, "V", t)
                    t = C.inc(eng.tensor_tensor(out=val3, in0=msk3, in1=m8bc,
                                                op=ALU.mult), "V")
                    C.wait(eng, "V", t)
                    t = C.inc(eng.tensor_tensor(out=gate3, in0=gate3,
                                                in1=val3, op=ALU.add), "V")
                    C.wait(eng, "V", t)
                    t = C.inc(eng.scalar_tensor_tensor(out=gsc[:, 0:128], in0=gsc[:, 256:384],
                                                       scalar=-1e30, in1=gsc[:, 0:128],
                                                       op0=ALU.mult, op1=ALU.add), "V")
                    C.wait(eng, "V", t)
                t = C.inc(eng.reduce_sum(out=gsc[:, 536:544], in_=gate3, axis=AX.X), "V")
                C.wait(eng, "V", t)
                t = C.inc(eng.tensor_scalar(out=gsc[:, 536:544], in0=gsc[:, 536:544],
                                            scalar1=1e-8, scalar2=None, op0=ALU.max), "V")
                C.wait(eng, "V", t)
                t = C.inc(eng.reciprocal(out=gsc[:, 544:552], in_=gsc[:, 536:544]), "V")
                C.wait(eng, "V", t)
                rg3 = gsc.ap()[:, 544:552].unsqueeze(2).broadcast_to([P, 8, 16])
                t = C.inc(eng.tensor_tensor(out=gall.ap().rearrange("p (t e) -> p t e", e=16),
                                            in0=gate3, in1=rg3, op=ALU.mult), "V")
                C.wait(eng, "V", t)
            blk.vector(_)
            tok_gate = C.now("V")

            gtps_freeV = 0
            for tt in range(8):
                def _(eng, tt=tt, tok_gate=tok_gate):
                    C.wait(eng, "V", tok_gate)
                    C.wait(eng, "V", gtps_freeV)
                    C.inc(eng.transpose(out=gt_ps[:],
                                        in_=gall[:, tt * 16:(tt + 1) * 16],
                                        identity=ident[:]), "T")
                blk.tensor(_)
                tok_gt = C.now("T")

                def _(eng, tt=tt, tok_gt=tok_gt):
                    C.wait(eng, "T", tok_gt)
                    C.inc(eng.tensor_copy(out=gT[:, tt * 128:(tt + 1) * 128],
                                          in_=gt_ps[:]), "V")
                blk.vector(_)
                gtps_freeV = C.now("V")
        tok_gT = C.now("V")
        tok_xfm2 = C.now("V")
        esA.close()

        esB = contextlib.ExitStack()
        sB = lambda name, shape, dt: esB.enter_context(nc.sbuf_tensor(name, shape, dt))
        pB = lambda name, shape, dt=F32: esB.enter_context(nc.psum_tensor(name, shape, dt))
        w1A = sB(f"p3b_w1A{g2}", [P, 16 * W1A], BF16)
        w1B = sB(f"p3b_w1B{g2}", [P, 16 * W1B], BF16)
        w2c = [sB(f"p3b_w2c{g2}_{i}", [P, 11 * 256], BF16) for i in range(2)]
        hg = sB(f"p3b_hg{g2}", [P, 11 * 1024], BF16)
        htmp = sB(f"p3b_htmp{g2}", [P, 1024], BF16)
        gbc_sb = sB(f"p3b_gbc{g2}", [P, 1024], BF16)
        h_ps = [pB(f"p3b_hps{g2}_{i}", [P, 1024]) for i in range(2)]
        y2_ps = [pB(f"p3b_yps{g2}_{i}", [P, 1024]) for i in range(2)]

        def w1_src(j, half):
            w = rw1[j] if j < NRE else sw1[j - NRE]
            lo = 0 if half == 0 else W1A
            width = W1A if half == 0 else W1B
            return w.rearrange("(c p) f -> p c f", p=P)[:, :, lo:lo + width]

        def w1_dst(half):
            buf = w1A if half == 0 else w1B
            width = W1A if half == 0 else W1B
            return buf.ap().rearrange("p (c f) -> p c f", f=width)

        with nc.Block(f"ph3b_{g2}") as blk:
            w1_free = [0, 0]
            w1_tok = {}
            w2_free = [0, 0]
            w2_tok = {}
            hps_free = [0, 0]
            yps3_free = [("V", 0), ("V", 0)]
            htmp_free = 0
            hg_free = 0
            gbc_freeV = 0

            def _(eng):
                C.wait(eng, "yo0", C.now("yo0"))
                C.wait(eng, "yo1", C.now("yo1"))
                C.incd(eng.dma_start(out=w1_dst(0), in_=w1_src(0, 0)), "w1A")
                w1_tok[(0, 0)] = C.now("w1A")
            blk.gpsimd(_)

            for j in range(NEXP):
                routed = j < NRE

                def _(eng, j=j):
                    C.wait(eng, "T", w1_free[1])
                    C.incd(eng.dma_start(out=w1_dst(1), in_=w1_src(j, 1)), "w1B")
                    w1_tok[(j, 1)] = C.now("w1B")
                blk.gpsimd(_)

                if routed:
                    def _(eng, j=j):
                        C.wait(eng, "V", tok_gT)
                        C.wait(eng, yps3_free[0][0], yps3_free[0][1])
                        ins = eng.matmul(out=y2_ps[0][:, 0:512],
                                         lhsT=eoh_sb[:, j * 128:(j + 1) * 128],
                                         rhs=gT[:, 0:512], start=True, stop=True)
                        C.inc(ins, "T")
                        ins = eng.matmul(out=y2_ps[0][:, 512:1024],
                                         lhsT=eoh_sb[:, j * 128:(j + 1) * 128],
                                         rhs=gT[:, 512:1024], start=True, stop=True)
                        C.inc(ins, "T")
                    blk.tensor(_)
                    tok_gbcm = C.now("T")

                    def _(eng, tok_gbcm=tok_gbcm):
                        C.wait(eng, "T", tok_gbcm - 1)
                        C.wait(eng, "V", gbc_freeV)
                        C.inc(eng.copy(out=gbc_sb[:, 0:512], in_=y2_ps[0][:, 0:512]), "S")
                        C.wait(eng, "T", tok_gbcm)
                        C.inc(eng.copy(out=gbc_sb[:, 512:1024], in_=y2_ps[0][:, 512:1024]), "S")
                    blk.scalar(_)
                    yps3_free[0] = ("S", C.now("S"))
                    tok_gbc = C.now("S")
                else:
                    tok_gbc = 0

                for fc in range(11):
                    half = 0 if fc < FC_W1A else 1
                    fcl = fc if fc < FC_W1A else fc - FC_W1A
                    width = W1A if half == 0 else W1B
                    wbuf = w1A if half == 0 else w1B
                    hp = fc % 2

                    def _(eng, j=j, fc=fc, half=half, fcl=fcl, width=width,
                          wbuf=wbuf, hp=hp):
                        C.wait(eng, "w1A" if half == 0 else "w1B", w1_tok[(j, half)])
                        C.wait(eng, "V", tok_xfm2)
                        C.wait(eng, "S", hps_free[hp])
                        for ts_ in range(2):
                            for dc in range(16):
                                ins = eng.matmul(
                                    out=h_ps[hp][:, ts_ * 512:(ts_ + 1) * 512],
                                    lhsT=wbuf[:, dc * width + fcl * 128:dc * width + (fcl + 1) * 128],
                                    rhs=xfm2[:, dc * 1024 + ts_ * 512:dc * 1024 + (ts_ + 1) * 512],
                                    start=(dc == 0), stop=(dc == 15))
                        C.inc(ins, "T")
                        if fc == 10:
                            w1_free[1] = C.now("T")
                        if fc == FC_W1A - 1:
                            w1_free[0] = C.now("T")
                    blk.tensor(_)
                    tok_h = C.now("T")

                    if fc == FC_W1A - 1 and j + 1 < NEXP:
                        def _(eng, j=j):
                            C.wait(eng, "T", w1_free[0])
                            C.incd(eng.dma_start(out=w1_dst(0), in_=w1_src(j + 1, 0)), "w1A")
                            w1_tok[(j + 1, 0)] = C.now("w1A")
                        blk.gpsimd(_)

                    def _(eng, j=j, fc=fc, hp=hp, routed=routed, tok_h=tok_h):
                        C.wait(eng, "T", tok_h)
                        if routed:
                            C.wait(eng, "V", htmp_free)
                            dst = htmp[:]
                        else:
                            C.wait(eng, "T", hg_free)
                            dst = hg[:, fc * 1024:(fc + 1) * 1024]
                        C.inc(eng.activation(out=dst, in_=h_ps[hp][:], func=AF.Relu,
                                             bias=b1_sb[:, j * 11 + fc:j * 11 + fc + 1]), "S")
                    blk.scalar(_)
                    hps_free[hp] = C.now("S")
                    tok_relu = C.now("S")

                    if routed:
                        def _(eng, fc=fc, tok_relu=tok_relu, tok_gbc=tok_gbc):
                            C.wait(eng, "S", tok_relu)
                            C.wait(eng, "S", tok_gbc)
                            C.wait(eng, "T", hg_free)
                            C.inc(eng.tensor_tensor(out=hg[:, fc * 1024:(fc + 1) * 1024],
                                                    in0=htmp[:], in1=gbc_sb[:],
                                                    op=ALU.mult), "V")
                        blk.vector(_)
                        htmp_free = C.now("V")
                tok_hg = C.now("V") if routed else C.now("S")
                hg_wait_stream = "V" if routed else "S"

                if j == 0:
                    def _(eng, j=j):
                        C.wait(eng, "T", w2_free[0])
                        w2 = rw2[j] if j < NRE else sw2[j - NRE]
                        C.incd(eng.dma_start(
                            out=w2c[0].ap().rearrange("p (c f) -> p c f", f=256),
                            in_=w2.rearrange("(c p) d -> p c d", p=P)[:, :, 0:256]), "w2c0")
                        w2_tok[0] = C.now("w2c0")
                    blk.gpsimd(_)

                for dc in range(16):
                    cb = dc % 2
                    cw = (dc // 2) % 2
                    dcl = dc % 2

                    def _(eng, j=j, dc=dc):
                        if dc % 2 == 0:
                            if dc + 2 < 16:
                                jn, cols = j, (dc + 2) * 128
                            elif j + 1 < NEXP:
                                jn, cols = j + 1, 0
                            else:
                                return
                            nb = (dc // 2 + 1) % 2
                            C.wait(eng, "T", w2_free[nb])
                            w2 = rw2[jn] if jn < NRE else sw2[jn - NRE]
                            C.incd(eng.dma_start(
                                out=w2c[nb].ap().rearrange("p (c f) -> p c f", f=256),
                                in_=w2.rearrange("(c p) d -> p c d", p=P)[
                                    :, :, cols:cols + 256]), f"w2c{nb}")
                            w2_tok[nb] = C.now(f"w2c{nb}")
                    blk.gpsimd(_)

                    def _(eng, j=j, dc=dc, cb=cb, cw=cw, dcl=dcl, tok_hg=tok_hg,
                          hws=hg_wait_stream):
                        C.wait(eng, hws, tok_hg)
                        C.wait(eng, f"w2c{cw}", w2_tok[cw])
                        C.wait(eng, yps3_free[cb][0], yps3_free[cb][1])
                        for ts_ in range(2):
                            for fc in range(11):
                                ins = eng.matmul(
                                    out=y2_ps[cb][:, ts_ * 512:(ts_ + 1) * 512],
                                    lhsT=w2c[cw][:, fc * 256 + dcl * 128:fc * 256 + (dcl + 1) * 128],
                                    rhs=hg[:, fc * 1024 + ts_ * 512:fc * 1024 + (ts_ + 1) * 512],
                                    start=(fc == 0), stop=False,
                                    skip_group_check=True)
                            if j == 0:
                                ins = eng.matmul(
                                    out=y2_ps[cb][:, ts_ * 512:(ts_ + 1) * 512],
                                    lhsT=b2r_sb[:, dc * 128:(dc + 1) * 128],
                                    rhs=gT[:, ts_ * 512:(ts_ + 1) * 512],
                                    start=False, stop=False, skip_group_check=True)
                                ins = eng.matmul(
                                    out=y2_ps[cb][:, ts_ * 512:(ts_ + 1) * 512],
                                    lhsT=sb2_sb[:, dc * 128:(dc + 1) * 128],
                                    rhs=ones2[:, 0:512],
                                    start=False, stop=(ts_ == 1), skip_group_check=True)
                        C.inc(ins, "T")
                        if dcl == 1:
                            w2_free[cw] = C.now("T")
                    blk.tensor(_)
                    tok_y2 = C.now("T")

                    def _(eng, j=j, dc=dc, cb=cb, tok_y2=tok_y2):
                        C.wait(eng, "T", tok_y2)
                        if j == 0:
                            C.inc(eng.tensor_copy(out=acc[:, dc * 1024:(dc + 1) * 1024],
                                                  in_=y2_ps[cb][:]), "V")
                        else:
                            C.inc(eng.tensor_tensor(out=acc[:, dc * 1024:(dc + 1) * 1024],
                                                    in0=y2_ps[cb][:],
                                                    in1=acc[:, dc * 1024:(dc + 1) * 1024],
                                                    op=ALU.add), "V")
                        yps3_free[cb] = ("V", C.now("V"))
                    blk.vector(_)
                hg_free = C.now("T")
                if routed:
                    gbc_freeV = tok_hg
        tok_acc = C.now("V")
        esB.close()

        esC = contextlib.ExitStack()
        sC = lambda name, shape, dt: esC.enter_context(nc.sbuf_tensor(name, shape, dt))
        pC = lambda name, shape, dt=F32: esC.enter_context(nc.psum_tensor(name, shape, dt))
        z_sb = [sC(f"p3c_z{g2}_{i}", [P, D], F32) for i in range(2)]
        xmc = [sC(f"p3c_xm{g2}_{i}", [P, D], F32) for i in range(2)]
        x2f = [sC(f"p3c_x2f{g2}_{i}", [P, D], F32) for i in range(2)]
        osb = [sC(f"p3c_o{g2}_{i}", [P, D], F32) for i in range(2)]
        bn6c = [sC(f"p3c_bn6{g2}_{i}", [P, 24], F32) for i in range(2)]
        mvc = [sC(f"p3c_mv{g2}_{i}", [P, 2], F32) for i in range(2)]
        rsc = [sC(f"p3c_rs{g2}_{i}", [P, 2], F32) for i in range(2)]
        trf = [pC(f"p3c_tr{g2}_{i}", [P, 512]) for i in range(4)]

        with nc.Block(f"ph3c_{g2}") as blk:
            xmc_free = [0, 0]
            osb_free = [0, 0]
            trf_free = [0, 0, 0, 0]
            z_freeV = [0, 0]
            for tt in range(8):
                tile = g2 * 8 + tt
                buf = tt % 2

                def _(eng, tile=tile, buf=buf):
                    C.wait(eng, "V", xmc_free[buf])
                    C.incd(eng.dma_start(out=xmc[buf][:], in_=xmid_scr[tile]), f"xmc{buf}")
                blk.sync(_)
                tok_x = C.now(f"xmc{buf}")

                ztoks = []
                for half in range(4):
                    tp = 2 * buf + half % 2

                    def _(eng, tt=tt, half=half, tp=tp):
                        C.wait(eng, "S", trf_free[tp])
                        for c in range(4):
                            dcc = half * 4 + c
                            ins = eng.transpose(
                                out=trf[tp][:, c * 128:(c + 1) * 128],
                                in_=acc[:, dcc * 1024 + tt * 128:dcc * 1024 + (tt + 1) * 128],
                                identity=identf[:])
                        C.inc(ins, "T")
                    blk.tensor(_)
                    tok_tr = C.now("T")

                    def _(eng, buf=buf, half=half, tp=tp, tok_tr=tok_tr):
                        C.wait(eng, "T", tok_tr)
                        C.wait(eng, "V", z_freeV[buf])
                        C.inc(eng.copy(out=z_sb[buf][:, half * 512:(half + 1) * 512],
                                       in_=trf[tp][:]), "S")
                    blk.scalar(_)
                    trf_free[tp] = C.now("S")
                tok_z = C.now("S")

                def _(eng, tt=tt, buf=buf, tok_z=tok_z, tok_x=tok_x):
                    C.wait(eng, "S", tok_z)
                    C.wait(eng, f"xmc{buf}", tok_x)
                    t = C.inc(eng.tensor_scalar(out=x2f[buf][:], in0=xmc[buf][:],
                                                scalar1=m3sb[:, tt:tt + 1],
                                                scalar2=r3sb[:, tt:tt + 1],
                                                op0=ALU.subtract, op1=ALU.mult), "V")
                    C.wait(eng, "V", t)
                    t = C.inc(eng.tensor_tensor(out=z_sb[buf][:], in0=z_sb[buf][:],
                                                in1=x2f[buf][:], op=ALU.add), "V")
                    C.wait(eng, "V", t)
                    for i in range(4):
                        ins = eng.bn_stats(out=bn6c[buf][:, 6 * i:6 * i + 6],
                                           in_=z_sb[buf][:, 512 * i:512 * (i + 1)])
                    t = C.inc(ins, "V")
                    C.wait(eng, "V", t)
                    C.inc(eng.bn_aggr(out=mvc[buf][:], in_=bn6c[buf][:]), "V")
                blk.vector(_)
                tok_mvz = C.now("V")

                def _(eng, buf=buf, tok_mvz=tok_mvz):
                    C.wait(eng, "V", tok_mvz)
                    t = C.inc(eng.activation(out=rsc[buf][:, 0:1], in_=mvc[buf][:, 1:2],
                                             func=AF.Ln, bias=epsap[:]), "S")
                    C.wait(eng, "S", t)
                    C.inc(eng.activation(out=rsc[buf][:, 1:2], in_=rsc[buf][:, 0:1],
                                         func=AF.Exp, scale=-0.5), "S")
                blk.scalar(_)
                tok_rsz = C.now("S")

                def _(eng, buf=buf, tok_rsz=tok_rsz):
                    C.wait(eng, "S", tok_rsz)
                    C.wait(eng, f"yo{buf}", osb_free[buf])
                    t = C.inc(eng.tensor_scalar(out=osb[buf][:], in0=z_sb[buf][:],
                                                scalar1=mvc[buf][:, 0:1],
                                                scalar2=rsc[buf][:, 1:2],
                                                op0=ALU.subtract, op1=ALU.mult), "V")
                    C.wait(eng, "V", t)
                    t = C.inc(eng.tensor_tensor(out=osb[buf][:], in0=osb[buf][:],
                                                in1=xmc[buf][:], op=ALU.add), "V")
                blk.vector(_)
                z_freeV[buf] = C.now("V")
                xmc_free[buf] = C.now("V")
                tok_o = C.now("V")

                def _(eng, tile=tile, buf=buf, tok_o=tok_o):
                    C.wait(eng, "V", tok_o)
                    C.incd(eng.dma_start(out=yout[tile * P:(tile + 1) * P, :],
                                         in_=osb[buf][:]), f"yo{buf}")
                blk.sync(_)
                osb_free[buf] = C.now(f"yo{buf}")
        esC.close()
    es3.close()

    with nc.Block("fin") as blk:
        if debug:
            def _(eng):
                for key in ("kst0", "kst1", "vst0", "vst1", "qst0", "qst1", "xmw"):
                    C.wait(eng, key, C.now(key))
                for i in range(16):
                    C.incd(eng.dma_start(out=dbg_xmid[i], in_=xmid_scr[i]), "dbg")
                for i in range(NPAIR):
                    C.incd(eng.dma_start(out=dbg_q[i], in_=q_scr[i]), "dbg")
                for i in range(4):
                    C.incd(eng.dma_start(out=dbg_k[i], in_=k_scr[i]), "dbg")
                for i in range(32):
                    C.incd(eng.dma_start(out=dbg_v[i], in_=v_scr[i]), "dbg")
                C.wait(eng, "dbg", C.now("dbg"))
            blk.gpsimd(_)

        def _(eng):
            C.wait(eng, "yo0", C.now("yo0"))
            C.wait(eng, "yo1", C.now("yo1"))
        blk.sync(_)

    es_glob.close()
    return nc



def _host_inputs(inputs):
    f32 = np.float32
    bf16 = ml_dtypes.bfloat16
    d = {}
    d["wqT"] = np.ascontiguousarray(inputs["L0_attn_Wq"].T)
    d["wkT"] = np.ascontiguousarray(inputs["L0_attn_Wk"].T)
    d["wvT"] = np.ascontiguousarray(inputs["L0_attn_Wv"].T)
    d["woT"] = np.ascontiguousarray(inputs["L0_attn_Wo"].T)
    d["centT"] = np.ascontiguousarray(inputs["L0_ffn_centroids"].T)
    d["rw1"] = np.ascontiguousarray(inputs["L0_ffn_routed_w1"])
    d["rw2"] = np.ascontiguousarray(inputs["L0_ffn_routed_w2"])
    d["sw1"] = np.ascontiguousarray(inputs["L0_ffn_shared_w1"])
    d["sw2"] = np.ascontiguousarray(inputs["L0_ffn_shared_w2"])
    b1 = np.concatenate([inputs["L0_ffn_routed_b1"], inputs["L0_ffn_shared_b1"]], 0)
    d["b1all"] = np.ascontiguousarray(
        b1.reshape(NEXP, 11, P).transpose(2, 0, 1).reshape(P, NEXP * 11)).astype(f32)
    d["b2r"] = np.ascontiguousarray(inputs["L0_ffn_routed_b2"])
    d["sb2"] = np.ascontiguousarray(inputs["L0_ffn_shared_b2"])
    d["biasbc"] = np.ascontiguousarray(
        np.broadcast_to(inputs["L0_ffn_bias"][None, :], (P, NRE))).astype(f32)
    d["lnwbc"] = np.ascontiguousarray(
        np.broadcast_to(inputs["L0_attn_ln_w"][None, :], (P, VD))).astype(f32)
    d["lamvec"] = np.concatenate(
        [inputs["L0_attn_lambda_q1"], inputs["L0_attn_lambda_k1"],
         inputs["L0_attn_lambda_q2"], inputs["L0_attn_lambda_k2"]]).reshape(1, 256).astype(f32)
    d["ident"] = np.eye(P, dtype=bf16)
    eoh = np.zeros((NRE, NRE * P), dtype=bf16)
    for j in range(NRE):
        eoh[j, j * P:(j + 1) * P] = 1.0
    d["eoh"] = eoh
    return d


def _masks_for(parity):
    astart = [0, 512, 3072, 3584] if parity == 0 else [1024, 1536, 2048, 2560]
    m = np.zeros((4, NMASK, P, 512), dtype=np.float32)
    r = np.arange(P)[:, None]
    q = np.arange(512)[None, :]
    for g in range(4):
        for si in range(NMASK):
            slot = ESLOT[g] - NMASK + si
            kabs = slot * 128 + r
            qabs = astart[g] + q
            m[g, si] = (kabs <= qabs)
    return m.astype(ml_dtypes.bfloat16)


def _own_idx(parity):
    chunks = (0, 3) if parity == 0 else (1, 2)
    return np.concatenate([np.arange(c * 1024, (c + 1) * 1024) for c in chunks])


def kernel(**inputs):
    x = inputs["L0_attn_x"]
    shared = _host_inputs(inputs)
    masks = [_masks_for(0), _masks_for(1)]

    nc = build_program()

    in_maps = []
    for c in range(NCHIP):
        b, parity = c // 2, c % 2
        m = dict(shared)
        m["xseq"] = np.ascontiguousarray(x[b])
        m["xown"] = np.ascontiguousarray(x[b][_own_idx(parity)])
        m["mask01"] = masks[parity]
        in_maps.append(m)

    res = run_bass_kernel_spmd(nc, in_maps, core_ids=list(range(NCHIP)))

    out = np.empty((4, T, D), dtype=np.float32)
    for c in range(NCHIP):
        b, parity = c // 2, c % 2
        out[b, _own_idx(parity)] = res.results[c]["yout"]
    return out


# version 11 (final)
# speedup vs baseline: 1.1586x; 1.0329x over previous
import math
import contextlib
import numpy as np
import ml_dtypes
import concourse.bass as bass
import concourse.mybir as mybir
from concourse.bass_utils import run_bass_kernel_spmd

F32 = mybir.dt.float32
BF16 = mybir.dt.bfloat16
AF = mybir.ActivationFunctionType
ALU = mybir.AluOpType
AX = mybir.AxisListType

P = 128
D = 2048
T = 4096
OWN = 2048
DFF = 1408
NRE = 16
NSH = 2
NEXP = NRE + NSH
NPAIR = 16
HD = 64
VD = 128
LAMBDA_INIT = 0.8 - 0.6 * math.exp(-0.3 * 12)
EPS = 1e-5
ESLOT = [12, 16, 28, 32]
NMASK = 12
NCHIP = 8

FC_W1A = 6
W1A = FC_W1A * P
W1B = DFF - W1A
FC_W1B = 5


class Ctx:

    NAMES = ["T", "V", "S", "G"]

    def __init__(self, nc, es):
        self.nc = nc
        self.es = es
        self.sem = {n: es.enter_context(nc.semaphore("sem_" + n)) for n in self.NAMES}
        self.cnt = {n: 0 for n in self.NAMES}

    def inc(self, instr, name, n=1):
        instr.then_inc(self.sem[name], n)
        self.cnt[name] += n
        return self.cnt[name]

    def incd(self, instr, key):
        if key not in self.sem:
            self.sem[key] = self.es.enter_context(self.nc.semaphore("d_" + key))
            self.cnt[key] = 0
        return self.inc(instr, key, 16)

    def wait(self, eng, name, val):
        if val > 0 and name in self.sem:
            eng.wait_ge(self.sem[name], val)

    def now(self, name):
        return self.cnt.get(name, 0)


def build_program(debug=False):
    nc = bass.Bass("TRN2", target_bir_lowering=False, debug=False)

    xseq = nc.dram_tensor("xseq", [T, D], F32, kind="ExternalInput").ap()
    xown = nc.dram_tensor("xown", [OWN, D], F32, kind="ExternalInput").ap()
    wqT = nc.dram_tensor("wqT", [D, D], F32, kind="ExternalInput").ap()
    wkT = nc.dram_tensor("wkT", [D, 512], F32, kind="ExternalInput").ap()
    wvT = nc.dram_tensor("wvT", [D, 512], F32, kind="ExternalInput").ap()
    woT = nc.dram_tensor("woT", [D, D], F32, kind="ExternalInput").ap()
    centT = nc.dram_tensor("centT", [D, NRE], F32, kind="ExternalInput").ap()
    rw1 = nc.dram_tensor("rw1", [NRE, D, DFF], F32, kind="ExternalInput").ap()
    rw2 = nc.dram_tensor("rw2", [NRE, DFF, D], F32, kind="ExternalInput").ap()
    sw1 = nc.dram_tensor("sw1", [NSH, D, DFF], F32, kind="ExternalInput").ap()
    sw2 = nc.dram_tensor("sw2", [NSH, DFF, D], F32, kind="ExternalInput").ap()
    b1all = nc.dram_tensor("b1all", [P, NEXP * 11], F32, kind="ExternalInput").ap()
    b2r = nc.dram_tensor("b2r", [NRE, D], F32, kind="ExternalInput").ap()
    sb2 = nc.dram_tensor("sb2", [NSH, D], F32, kind="ExternalInput").ap()
    biasbc = nc.dram_tensor("biasbc", [P, NRE], F32, kind="ExternalInput").ap()
    lnwbc = nc.dram_tensor("lnwbc", [P, VD], F32, kind="ExternalInput").ap()
    lamvec = nc.dram_tensor("lamvec", [1, 256], F32, kind="ExternalInput").ap()
    ident_in = nc.dram_tensor("ident", [P, P], BF16, kind="ExternalInput").ap()
    eoh_in = nc.dram_tensor("eoh", [NRE, NRE * P], BF16, kind="ExternalInput").ap()
    mask01 = nc.dram_tensor("mask01", [4, NMASK, P, 512], BF16, kind="ExternalInput").ap()

    yout = nc.dram_tensor("yout", [OWN, D], F32, kind="ExternalOutput").ap()

    q_scr = nc.dram_tensor("q_scr", [NPAIR, P, OWN], BF16, kind="Internal").ap()
    k_scr = nc.dram_tensor("k_scr", [4, P, T], BF16, kind="Internal").ap()
    v_scr = nc.dram_tensor("v_scr", [32, P, 4 * 129], BF16, kind="Internal").ap()
    xmid_scr = nc.dram_tensor("xmid_scr", [16, P, D], F32, kind="Internal").ap()
    if debug:
        dbg_xmid = nc.dram_tensor("dbg_xmid", [16, P, D], F32, kind="ExternalOutput").ap()
        dbg_q = nc.dram_tensor("dbg_q", [NPAIR, P, OWN], F32, kind="ExternalOutput").ap()
        dbg_k = nc.dram_tensor("dbg_k", [4, P, T], F32, kind="ExternalOutput").ap()
        dbg_v = nc.dram_tensor("dbg_v", [32, P, 4 * 129], F32, kind="ExternalOutput").ap()

    es_glob = contextlib.ExitStack()
    C = Ctx(nc, es_glob)
    sb = lambda name, shape, dt=F32: es_glob.enter_context(nc.sbuf_tensor(name, shape, dt))

    ident = sb("g_ident", [P, P], BF16)
    identf = sb("g_identf", [P, P], F32)
    lamv = sb("g_lamv", [1, 256], F32)
    lprod = sb("g_lprod", [1, 64], F32)
    lsc = sb("g_lsc", [1, 4], F32)
    lambc = sb("g_lambc", [P, 1], F32)
    lnw = sb("g_lnw", [P, VD], F32)
    epsap = sb("g_epsap", [P, 1], F32)
    zrow = sb("g_zrow", [1, P], BF16)
    orow = sb("g_orow", [1, 512], BF16)
    onesf = sb("g_onesf", [1, P], F32)
    ones2 = sb("g_ones2", [NSH, 512], BF16)

    with nc.Block("ph0"):
        pass

    with nc.Block("ph_const") as blk:
        def _(eng):
            tok = C.incd(eng.dma_start(out=ident[:], in_=ident_in), "c0")
            C.incd(eng.dma_start(out=lamv[:], in_=lamvec), "c0")
            C.incd(eng.dma_start(out=lnw[:], in_=lnwbc), "c0")
            return tok
        blk.sync(_)

        def _(eng):
            eng.memset(epsap[:], EPS)
            eng.memset(zrow[:], 0.0)
            eng.memset(orow[:], 1.0)
            eng.memset(onesf[:], 1.0)
            C.inc(eng.memset(ones2[:], 1.0), "G")
        blk.gpsimd(_)

        def _(eng):
            C.wait(eng, "c0", 48)
            C.inc(eng.tensor_copy(out=identf[:], in_=ident[:]), "V")
            t1 = C.inc(eng.tensor_tensor(out=lprod[:], in0=lamv[:, 0:64],
                                         in1=lamv[:, 64:128], op=ALU.mult), "V")
            C.wait(eng, "V", t1)
            t2 = C.inc(eng.reduce_sum(out=lsc[:, 0:1], in_=lprod[:], axis=AX.X), "V")
            C.wait(eng, "V", t2)
            t3 = C.inc(eng.tensor_tensor(out=lprod[:], in0=lamv[:, 128:192],
                                         in1=lamv[:, 192:256], op=ALU.mult), "V")
            C.wait(eng, "V", t3)
            C.inc(eng.reduce_sum(out=lsc[:, 1:2], in_=lprod[:], axis=AX.X), "V")
            C.inc(eng.tensor_scalar(out=lnw[:], in0=lnw[:], scalar1=1.0 - LAMBDA_INIT,
                                    scalar2=None, op0=ALU.mult), "V")
        blk.vector(_)

        def _(eng):
            C.wait(eng, "V", C.now("V"))
            t = C.inc(eng.activation(out=lsc[:, 2:3], in_=lsc[:, 0:1], func=AF.Exp), "S")
            C.wait(eng, "S", t)
            t = C.inc(eng.activation(out=lsc[:, 3:4], in_=lsc[:, 1:2], func=AF.Exp), "S")
            C.wait(eng, "S", t)
        blk.scalar(_)

        def _(eng):
            C.wait(eng, "S", C.now("S"))
            t = C.inc(eng.tensor_scalar(out=lsc[:, 0:1], in0=lsc[:, 2:3],
                                        scalar1=lsc[:, 3:4], scalar2=LAMBDA_INIT,
                                        op0=ALU.subtract, op1=ALU.add), "V")
            C.wait(eng, "V", t)
            t = C.inc(eng.tensor_copy(out=lsc[:, 1:2], in_=lsc[:, 0:1]), "V")
            C.wait(eng, "V", t)
        blk.vector(_)

        with nc.psum_tensor("ph0_bc", [P, 2], F32) as bc_ps:
            def _(eng):
                C.wait(eng, "V", C.now("V"))
                C.wait(eng, "G", C.now("G"))
                C.inc(eng.matmul(out=bc_ps[:], lhsT=onesf[:], rhs=lsc[:, 0:2],
                                 start=True, stop=True), "T")
            blk.tensor(_)

            def _(eng):
                C.wait(eng, "T", C.now("T"))
                C.inc(eng.tensor_copy(out=lambc[:], in_=bc_ps[:, 0:1]), "V")
            blk.vector(_)

    es1 = contextlib.ExitStack()
    s1 = lambda name, shape, dt: es1.enter_context(nc.sbuf_tensor(name, shape, dt))
    p1t = lambda name, shape, dt=F32: es1.enter_context(nc.psum_tensor(name, shape, dt))

    wq_sb = s1("p1_wq", [P, 16 * D], BF16)
    wk_sb = s1("p1_wk", [P, 16 * 512], BF16)
    wv_sb = s1("p1_wv", [P, 16 * 512], BF16)
    xt = [s1(f"p1_xt{i}", [P, D], F32) for i in range(4)]
    xlnbf = [s1(f"p1_xlnbf{i}", [P, D], BF16) for i in range(2)]
    xfm = s1("p1_xfm", [P, 16 * 512], BF16)
    bn6 = [s1(f"p1_bn6{i}", [P, 24], F32) for i in range(4)]
    mvg = s1("p1_mvg", [P, 8], F32)
    rstd4 = s1("p1_rstd4", [P, 4], F32)
    lnscr = s1("p1_lnscr", [P, 4], F32)
    kstage = [s1(f"p1_kst{i}", [P, 512], BF16) for i in range(2)]
    vstage = [s1(f"p1_vst{i}", [P, 4 * 129], BF16) for i in range(2)]
    qstage = [s1(f"p1_qst{i}", [P, 512], BF16) for i in range(2)]

    psT = [p1t(f"p1_psT{i}", [P, 1024], BF16) for i in range(2)]
    q_ps = [p1t(f"p1_qps{i}", [P, 512]) for i in range(2)]
    k_ps = q_ps
    v_ps = [p1t(f"p1_vps{i}", [P, 512]) for i in range(2)]

    with nc.Block("ph1") as blk:
        def _(eng):
            C.incd(eng.dma_start(out=wq_sb.ap().rearrange("p (c f) -> p c f", c=16),
                                 in_=wqT.rearrange("(c p) f -> p c f", p=P)), "w1x")
            C.incd(eng.dma_start(out=wk_sb.ap().rearrange("p (c f) -> p c f", c=16),
                                 in_=wkT.rearrange("(c p) f -> p c f", p=P)), "w1x")
            C.incd(eng.dma_start(out=wv_sb.ap().rearrange("p (c f) -> p c f", c=16),
                                 in_=wvT.rearrange("(c p) f -> p c f", p=P)), "w1x")
            for i in range(2):
                for kv in range(4):
                    ins = eng.memset(vstage[i][:, kv * 129 + 128:kv * 129 + 129], 1.0)
            C.inc(ins, "G")
        blk.gpsimd(_)
        tok_w = C.now("w1x")
        tok_vst = C.now("G")

        xt_free = [0, 0, 0, 0]
        xln_free = [0, 0]
        psT_free = [0, 0]
        kst_free = [0, 0]
        vst_free = [0, 0]
        qst_free = [0, 0]
        qkv_ps_free = [0, 0]
        xfm_free = 0

        def ln_group(src_tensor, g):
            stat_toks = []
            for t in range(4):
                tile = g * 4 + t
                buf = t

                def _(eng, tile=tile, buf=buf):
                    C.wait(eng, "V", xt_free[buf])
                    C.incd(eng.dma_start(out=xt[buf][:],
                                         in_=src_tensor[tile * P:(tile + 1) * P, :]),
                           f"xt{buf}")
                blk.sync(_)
                tok_x = C.now(f"xt{buf}")

                def _(eng, t=t, buf=buf, tok_x=tok_x):
                    C.wait(eng, f"xt{buf}", tok_x)
                    for i in range(4):
                        ins = eng.bn_stats(out=bn6[buf][:, 6 * i:6 * i + 6],
                                           in_=xt[buf][:, 512 * i:512 * (i + 1)])
                    tv = C.inc(ins, "V")
                    C.wait(eng, "V", tv)
                    C.inc(eng.bn_aggr(out=mvg[:, t * 2:t * 2 + 2], in_=bn6[buf][:]), "V")
                blk.vector(_)
                stat_toks.append(C.now("V"))

            def _(eng, tok=stat_toks[-1]):
                C.wait(eng, "V", tok)
                vr = mvg.ap().rearrange("p (t two) -> p t two", two=2)[:, :, 1:2]
                ts_ = C.inc(eng.activation(out=lnscr[:, 0:4], in_=vr,
                                           func=AF.Ln, bias=epsap[:]), "S")
                C.wait(eng, "S", ts_)
                C.inc(eng.activation(out=rstd4[:, 0:4], in_=lnscr[:, 0:4],
                                     func=AF.Exp, scale=-0.5), "S")
            blk.scalar(_)
            tok_rs = C.now("S")

            for t in range(4):
                buf = t

                def _(eng, t=t, buf=buf, tok_rs=tok_rs):
                    C.wait(eng, "S", tok_rs)
                    C.wait(eng, "T", xln_free[buf % 2])
                    C.inc(eng.tensor_scalar(out=xlnbf[buf % 2][:], in0=xt[buf][:],
                                            scalar1=mvg[:, t * 2:t * 2 + 1],
                                            scalar2=rstd4[:, t:t + 1],
                                            op0=ALU.subtract, op1=ALU.mult), "V")
                blk.vector(_)
                xt_free[buf] = C.now("V")
                tok_xln = C.now("V")

                def _(eng, buf=buf, tok_xln=tok_xln):
                    C.wait(eng, "V", tok_xln)
                    C.wait(eng, "V", psT_free[0])
                    for c in range(8):
                        ins = eng.transpose(out=psT[0][:, c * 128:(c + 1) * 128],
                                            in_=xlnbf[buf % 2][:, c * 128:(c + 1) * 128],
                                            identity=ident[:])
                    C.inc(ins, "T")
                    C.wait(eng, "V", psT_free[1])
                    for c in range(8):
                        ins = eng.transpose(out=psT[1][:, c * 128:(c + 1) * 128],
                                            in_=xlnbf[buf % 2][:, (8 + c) * 128:(9 + c) * 128],
                                            identity=ident[:])
                    C.inc(ins, "T")
                blk.tensor(_)
                xln_free[buf % 2] = C.now("T")
                tok_tr = C.now("T")

                def _(eng, t=t, tok_tr=tok_tr):
                    xfm3d = xfm.ap().rearrange("p (c f) -> p c f", f=512)
                    C.wait(eng, "T", tok_tr - 1)
                    C.wait(eng, "T", xfm_free)
                    dst0 = xfm3d[:, 0:8, t * 128:(t + 1) * 128]
                    C.inc(eng.tensor_copy(out=dst0, in_=psT[0].ap().rearrange(
                        "p (c f) -> p c f", f=128)), "V")
                    psT_free[0] = C.now("V")
                    C.wait(eng, "T", tok_tr)
                    dst1 = xfm3d[:, 8:16, t * 128:(t + 1) * 128]
                    C.inc(eng.tensor_copy(out=dst1, in_=psT[1].ap().rearrange(
                        "p (c f) -> p c f", f=128)), "V")
                    psT_free[1] = C.now("V")
                blk.vector(_)
            return C.now("V")

        for g in range(8):
            tok_xfm = ln_group(xseq, g)

            for fb in range(4):
                pb = fb % 2

                def _(eng, fb=fb, pb=pb):
                    C.wait(eng, "V", tok_xfm)
                    C.wait(eng, "w1x", tok_w)
                    C.wait(eng, "S", qkv_ps_free[pb])
                    for dc in range(16):
                        ins = eng.matmul(out=k_ps[pb][:],
                                         lhsT=wk_sb[:, dc * 512 + fb * 128:dc * 512 + (fb + 1) * 128],
                                         rhs=xfm[:, dc * 512:(dc + 1) * 512],
                                         start=(dc == 0), stop=(dc == 15))
                    C.inc(ins, "T")
                blk.tensor(_)
                tok_mm = C.now("T")

                def _(eng, fb=fb, pb=pb, tok_mm=tok_mm):
                    C.wait(eng, "T", tok_mm)
                    C.wait(eng, f"kst{pb}", kst_free[pb])
                    C.inc(eng.copy(out=kstage[pb][:], in_=k_ps[pb][:]), "S")
                blk.scalar(_)
                qkv_ps_free[pb] = C.now("S")
                tok_st = C.now("S")

                def _(eng, fb=fb, pb=pb, g=g, tok_st=tok_st):
                    C.wait(eng, "S", tok_st)
                    C.incd(eng.dma_start(out=k_scr[fb, :, g * 512:(g + 1) * 512],
                                         in_=kstage[pb][:]), f"kst{pb}")
                blk.gpsimd(_)
                kst_free[pb] = C.now(f"kst{pb}")

            for t in range(4):
                pb = t % 2

                def _(eng, t=t, pb=pb):
                    C.wait(eng, "V", tok_xfm)
                    C.wait(eng, "S", qkv_ps_free[pb])
                    for dc in range(16):
                        ins = eng.matmul(out=v_ps[pb][:],
                                         lhsT=xfm[:, dc * 512 + t * 128:dc * 512 + (t + 1) * 128],
                                         rhs=wv_sb[:, dc * 512:(dc + 1) * 512],
                                         start=(dc == 0), stop=(dc == 15))
                    C.inc(ins, "T")
                blk.tensor(_)
                tok_mm = C.now("T")

                def _(eng, t=t, pb=pb, tok_mm=tok_mm):
                    C.wait(eng, "T", tok_mm)
                    C.wait(eng, f"vst{pb}", vst_free[pb])
                    C.wait(eng, "G", tok_vst)
                    for kv in range(4):
                        ins = eng.copy(out=vstage[pb][:, kv * 129:kv * 129 + 128],
                                       in_=v_ps[pb][:, kv * 128:(kv + 1) * 128])
                    C.inc(ins, "S")
                blk.scalar(_)
                qkv_ps_free[pb] = C.now("S")
                tok_st = C.now("S")

                def _(eng, t=t, pb=pb, g=g, tok_st=tok_st):
                    C.wait(eng, "S", tok_st)
                    C.incd(eng.dma_start(out=v_scr[g * 4 + t, :, :], in_=vstage[pb][:]), f"vst{pb}")
                blk.gpsimd(_)
                vst_free[pb] = C.now(f"vst{pb}")

            xfm_free_local = C.now("T")
            xfm_free = xfm_free_local

        for g in range(4):
            tok_xfm = ln_group(xown, g)

            for fc in range(16):
                pb = fc % 2

                def _(eng, fc=fc, pb=pb):
                    C.wait(eng, "V", tok_xfm)
                    C.wait(eng, "S", qkv_ps_free[pb])
                    for dc in range(16):
                        ins = eng.matmul(out=q_ps[pb][:],
                                         lhsT=wq_sb[:, dc * D + fc * 128:dc * D + (fc + 1) * 128],
                                         rhs=xfm[:, dc * 512:(dc + 1) * 512],
                                         start=(dc == 0), stop=(dc == 15))
                    C.inc(ins, "T")
                blk.tensor(_)
                tok_mm = C.now("T")

                def _(eng, fc=fc, pb=pb, tok_mm=tok_mm):
                    C.wait(eng, "T", tok_mm)
                    C.wait(eng, f"qst{pb}", qst_free[pb])
                    C.inc(eng.copy(out=qstage[pb][:], in_=q_ps[pb][:]), "S")
                blk.scalar(_)
                qkv_ps_free[pb] = C.now("S")
                tok_st = C.now("S")

                def _(eng, fc=fc, pb=pb, g=g, tok_st=tok_st):
                    C.wait(eng, "S", tok_st)
                    C.incd(eng.dma_start(out=q_scr[fc, :, g * 512:(g + 1) * 512],
                                         in_=qstage[pb][:]), f"qst{pb}")
                blk.gpsimd(_)
                qst_free[pb] = C.now(f"qst{pb}")

            xfm_free = C.now("T")

    es1.close()

    es2 = contextlib.ExitStack()
    s2 = lambda name, shape, dt: es2.enter_context(nc.sbuf_tensor(name, shape, dt))
    p2t = lambda name, shape, dt=F32: es2.enter_context(nc.psum_tensor(name, shape, dt))

    k_sb = s2("p2_k", [P, 4 * T], BF16)
    v_sb = s2("p2_v", [P, 4 * 32 * 129], BF16)
    wo_sb = s2("p2_wo", [P, 16 * D], BF16)
    afm = s2("p2_afm", [P, NPAIR * 512], BF16)
    mask_sb = s2("p2_mask", [P, NMASK * 512], BF16)
    q1b = [s2(f"p2_q1{i}", [P, 512], BF16) for i in range(4)]
    q2b = [s2(f"p2_q2{i}", [P, 512], BF16) for i in range(4)]
    pbuf = [s2(f"p2_p{i}", [P, 512], BF16) for i in range(4)]
    cmb = s2("p2_cmb", [P, 1040], F32)
    abf4 = [s2(f"p2_abf4{i}", [P, 512], BF16) for i in range(2)]
    ysb = s2("p2_ysb", [P, D], F32)
    xo = s2("p2_xo", [P, D], F32)
    xm = s2("p2_xm", [P, D], F32)
    bn6y = s2("p2_bn6y", [P, 24], F32)
    mvy = s2("p2_mvy", [P, 2], F32)
    rsy = s2("p2_rsy", [P, 2], F32)

    s3_ps = p2t("p2_s3", [P, 1536])
    pv4 = p2t("p2_pv4", [P, 2048])
    shps = p2t("p2_sh", [P, 512])
    trview = shps.ap().bitcast(BF16)

    with nc.Block("ph2") as blk:
        def _(eng):
            for key in ("kst0", "kst1", "vst0", "vst1", "qst0", "qst1"):
                C.wait(eng, key, C.now(key))
            C.incd(eng.dma_start(out=wo_sb.ap().rearrange("p (c f) -> p c f", c=16),
                                 in_=woT.rearrange("(c p) f -> p c f", p=P)), "wo2")
        blk.gpsimd(_)
        tok_wo = C.now("wo2")

        def _(eng):
            for key in ("kst0", "kst1", "vst0", "vst1", "qst0", "qst1"):
                C.wait(eng, key, C.now(key))
            for fb in range(4):
                C.incd(eng.dma_start(out=k_sb[:, fb * T:(fb + 1) * T], in_=k_scr[fb]), "kv2")
            for kv in range(4):
                C.incd(eng.dma_start(
                    out=v_sb.ap()[:, kv * 32 * 129:(kv + 1) * 32 * 129].rearrange(
                        "p (tc c) -> p tc c", c=129),
                    in_=v_scr.rearrange("tc p c -> p tc c")[:, :, kv * 129:(kv + 1) * 129]), "kv2")
        blk.sync(_)
        tok_kv = C.now("kv2")

        def _(eng):
            for key in ("kst0", "kst1", "vst0", "vst1", "qst0", "qst1"):
                C.wait(eng, key, C.now(key))
            for b in q1b + q2b:
                C.inc(eng.memset(b[:], 0.0), "V")
        blk.vector(_)
        tok_qz = C.now("V")

        q_free = [0, 0, 0, 0]
        pend_tr = []
        abf_freeT = [0, 0]
        sbank_last = [0, 0, 0]
        pmask_last = 0
        pv_cons = ("V", 0)
        trv_consumed = 0
        afm_free = 0
        mask_free = 0
        yps_free = 0
        yps_free2 = [0, 0]
        xm_freeV = 0
        xm_freeDs = 0

        for g in range(4):
            Eg = ESLOT[g]
            nm0 = Eg - NMASK

            def _(eng, g=g):
                C.wait(eng, "V", mask_free)
                C.incd(eng.dma_start(
                    out=mask_sb.ap().rearrange("p (s c) -> p s c", c=512),
                    in_=mask01[g].rearrange("s p c -> p s c")), "msk")
            blk.sync(_)
            tok_mask = C.now("msk")

            def emit_qdma(h):
                qb = h % 4
                kt = h // 2
                base = (kt % 2) * 64

                def _(eng, h=h, qb=qb, base=base, g=g):
                    C.wait(eng, "T", q_free[qb])
                    if g == 0 and h < 4:
                        C.wait(eng, "V", tok_qz)
                    C.incd(eng.dma_start(out=q1b[qb][base:base + 64, :],
                                         in_=q_scr[h, 0:64, g * 512:(g + 1) * 512]), f"q1_{qb}")
                    C.incd(eng.dma_start(out=q2b[qb][base:base + 64, :],
                                         in_=q_scr[h, 64:128, g * 512:(g + 1) * 512]), f"q2_{qb}")
                blk.sync(_)
                return (C.now(f"q1_{qb}"), C.now(f"q2_{qb}"), qb)

            tok_q_next = emit_qdma(0)
            for h in range(NPAIR):
                kt = h // 2
                base = (kt % 2) * 64
                ktile = kt // 2
                vh = h // 4
                qb = h % 4

                tok_q = tok_q_next
                if h + 1 < NPAIR:
                    tok_q_next = emit_qdma(h + 1)

                if len(pend_tr) > 1:
                    etr, eev = pend_tr.pop(0)
                    blk.tensor(etr)
                    _tok_trp = C.now("T")
                    blk.vector(lambda eng, eev=eev, _tok_trp=_tok_trp: eev(
                        eng, tok_trp=_tok_trp))
                    trv_consumed = C.now("V")

                base_T, base_S, base_V = C.now("T"), C.now("S"), C.now("V")
                tok_mm = {}
                tok_pv = {}

                def _(eng, h=h, qb=qb, base=base, ktile=ktile, vh=vh, Eg=Eg,
                      nm0=nm0, tok_q=tok_q, base_S=base_S, base_V=base_V,
                      sb_prev=tuple(sbank_last), pv_cons=pv_cons):
                    C.wait(eng, f"q1_{tok_q[2]}", tok_q[0])
                    C.wait(eng, f"q2_{tok_q[2]}", tok_q[1])
                    C.wait(eng, "kv2", tok_kv)

                    def emit_pv(s):
                        masked = s >= nm0
                        if masked:
                            C.wait(eng, "V", base_V + 2 * (s - nm0) + 2)
                        else:
                            C.wait(eng, "S", base_S + 2 * s + 2)
                        p1 = pbuf[s % 2]
                        p2 = pbuf[2 + s % 2]
                        vcol = vh * 32 * 129 + s * 129
                        for tt in range(4):
                            ins = eng.matmul(out=pv4[:, tt * 512:tt * 512 + 129],
                                             lhsT=p1[:, tt * 128:(tt + 1) * 128],
                                             rhs=v_sb[:, vcol:vcol + 129],
                                             start=False, stop=False, skip_group_check=True)
                            ins = eng.matmul(out=pv4[:, tt * 512 + 129:tt * 512 + 258],
                                             lhsT=p2[:, tt * 128:(tt + 1) * 128],
                                             rhs=v_sb[:, vcol:vcol + 129],
                                             start=False, stop=False, skip_group_check=True)
                        tok_pv[s] = C.inc(ins, "T")

                    for s in range(Eg):
                        if s == 1:
                            C.wait(eng, pv_cons[0], pv_cons[1])
                            for tt in range(4):
                                ins = eng.matmul(
                                    out=pv4[:, tt * 512:tt * 512 + 258],
                                    lhsT=zrow[:], rhs=orow[:, 0:258],
                                    start=True, stop=False, skip_group_check=True)
                            C.inc(ins, "T")
                        for half in range(2):
                            k_idx = 2 * s + half
                            bank = k_idx % 3
                            if k_idx >= 3:
                                C.wait(eng, "S", base_S + k_idx - 2)
                            else:
                                C.wait(eng, "S", sb_prev[bank])
                            qsrc = (q1b if half == 0 else q2b)[qb]
                            ins = eng.matmul(
                                out=s3_ps[:, bank * 512:(bank + 1) * 512],
                                lhsT=k_sb[:, ktile * T + s * 128:ktile * T + (s + 1) * 128],
                                rhs=qsrc[:], start=True, stop=True)
                            tok_mm[(s, half)] = C.inc(ins, "T")
                        if s >= 1:
                            emit_pv(s - 1)
                    emit_pv(Eg - 1)
                blk.tensor(_)
                q_free[qb] = C.now("T")
                tok_pv_last = C.now("T")
                for _k in range(2 * Eg - 3, 2 * Eg):
                    sbank_last[_k % 3] = base_S + _k + 1

                def _(eng, Eg=Eg, nm0=nm0, base_S=base_S, base_V=base_V,
                      pml=pmask_last):
                    C.wait(eng, "V", pml)
                    for s in range(Eg):
                        for half in range(2):
                            k_idx = 2 * s + half
                            bank = k_idx % 3
                            C.wait(eng, "T", tok_mm[(s, half)])
                            if s - 2 >= nm0:
                                C.wait(eng, "V", base_V + 2 * (s - 2 - nm0) + 1 + half)
                            dst = (pbuf[s % 2] if half == 0 else pbuf[2 + s % 2])
                            tok = C.inc(eng.activation(
                                out=dst[:], in_=s3_ps[:, bank * 512:(bank + 1) * 512],
                                func=AF.Exp, scale=0.125), "S")
                            assert tok == base_S + k_idx + 1
                blk.scalar(_)

                def _(eng, Eg=Eg, nm0=nm0, base_S=base_S, base_V=base_V,
                      tok_mask=tok_mask):
                    C.wait(eng, "msk", tok_mask)
                    for s in range(nm0, Eg):
                        si = s - nm0
                        for half in range(2):
                            C.wait(eng, "S", base_S + 2 * s + 1 + half)
                            if s - 2 >= 0:
                                C.wait(eng, "T", tok_pv.get(s - 2, 0))
                            src = pbuf[s % 2] if half == 0 else pbuf[2 + s % 2]
                            tok = C.inc(eng.tensor_tensor(
                                out=src[:], in0=src[:],
                                in1=mask_sb[:, si * 512:(si + 1) * 512],
                                op=ALU.mult), "V")
                            assert tok == base_V + 2 * si + 1 + half
                blk.vector(_)
                pmask_last = base_V + 2 * NMASK

                pv_free_tok = [0]
                pv3 = pv4.ap().rearrange("p (t c) -> p t c", c=512)
                def _(eng, tok_pv_last=tok_pv_last):
                    C.wait(eng, "T", tok_pv_last)
                    t = C.inc(eng.reciprocal(out=cmb.ap()[:, 1024:1028].unsqueeze(2),
                                             in_=pv3[:, :, 128:129]), "V")
                    C.wait(eng, "V", t)
                    t = C.inc(eng.reciprocal(out=cmb.ap()[:, 1028:1032].unsqueeze(2),
                                             in_=pv3[:, :, 257:258]), "V")
                    C.wait(eng, "V", t)
                    t = C.inc(eng.tensor_scalar(out=cmb[:, 1032:1036], in0=cmb[:, 1028:1032],
                                                scalar1=lambc[:], scalar2=-1.0,
                                                op0=ALU.mult, op1=ALU.mult), "V")
                    C.wait(eng, "V", t)
                    A3 = cmb.ap()[:, 0:512].rearrange("p (t c) -> p t c", c=128)
                    B3 = cmb.ap()[:, 512:1024].rearrange("p (t c) -> p t c", c=128)
                    a1bc = cmb.ap()[:, 1024:1028].unsqueeze(2).broadcast_to([P, 4, 128])
                    a2bc = cmb.ap()[:, 1032:1036].unsqueeze(2).broadcast_to([P, 4, 128])
                    t = C.inc(eng.tensor_tensor(out=A3, in0=pv3[:, :, 0:128],
                                                in1=a1bc, op=ALU.mult), "V")
                    C.wait(eng, "V", t)
                    t = C.inc(eng.tensor_tensor(out=B3, in0=pv3[:, :, 129:257],
                                                in1=a2bc, op=ALU.mult), "V")
                    C.wait(eng, "V", t)
                    pv_free_tok[0] = t
                    t = C.inc(eng.tensor_tensor(out=cmb[:, 0:512], in0=cmb[:, 0:512],
                                                in1=cmb[:, 512:1024], op=ALU.add), "V")
                    C.wait(eng, "V", t)
                    t = C.inc(eng.tensor_tensor(out=cmb[:, 512:1024], in0=cmb[:, 0:512],
                                                in1=cmb[:, 0:512], op=ALU.mult), "V")
                    C.wait(eng, "V", t)
                    t = C.inc(eng.reduce_sum(out=cmb[:, 1036:1040], in_=B3, axis=AX.X), "V")
                    C.wait(eng, "V", t)
                    t = C.inc(eng.tensor_scalar(out=cmb[:, 1036:1040], in0=cmb[:, 1036:1040],
                                                scalar1=1.0 / VD, scalar2=None,
                                                op0=ALU.mult), "V")
                    C.wait(eng, "V", t)
                blk.vector(_)
                tok_cmb1 = C.now("V")
                pv_cons = ("V", pv_free_tok[0])

                def _(eng, tok_cmb1=tok_cmb1):
                    C.wait(eng, "V", tok_cmb1)
                    t = C.inc(eng.activation(out=cmb[:, 1028:1032], in_=cmb[:, 1036:1040],
                                             func=AF.Ln, bias=epsap[:]), "S")
                    C.wait(eng, "S", t)
                    C.inc(eng.activation(out=cmb[:, 1024:1028], in_=cmb[:, 1028:1032],
                                         func=AF.Exp, scale=-0.5), "S")
                blk.scalar(_)
                tok_rms = C.now("S")

                def _(eng, h=h, tok_rms=tok_rms):
                    C.wait(eng, "S", tok_rms)
                    A3 = cmb.ap()[:, 0:512].rearrange("p (t c) -> p t c", c=128)
                    B3 = cmb.ap()[:, 512:1024].rearrange("p (t c) -> p t c", c=128)
                    rbc = cmb.ap()[:, 1024:1028].unsqueeze(2).broadcast_to([P, 4, 128])
                    lnw3 = lnw.ap().unsqueeze(1).broadcast_to([P, 4, 128])
                    t = C.inc(eng.tensor_tensor(out=B3, in0=A3, in1=rbc, op=ALU.mult), "V")
                    C.wait(eng, "V", t)
                    C.wait(eng, "T", abf_freeT[h % 2])
                    C.inc(eng.tensor_tensor(out=abf4[h % 2].ap().rearrange(
                        "p (t c) -> p t c", c=128), in0=B3, in1=lnw3, op=ALU.mult), "V")
                blk.vector(_)
                tok_abf = C.now("V")

                def emit_attn_tr(eng, h=h, tok_abf=tok_abf):
                    C.wait(eng, "V", tok_abf)
                    C.wait(eng, "V", trv_consumed)
                    C.wait(eng, "V", yps_free2[0])
                    for tt in range(4):
                        ins = eng.transpose(out=trview[:, tt * 128:(tt + 1) * 128],
                                            in_=abf4[h % 2][:, tt * 128:(tt + 1) * 128],
                                            identity=ident[:])
                    C.inc(ins, "T")
                    abf_freeT[h % 2] = C.now("T")

                def emit_attn_ev(eng, h=h, tok_trp=None):
                    C.wait(eng, "T", tok_trp)
                    C.wait(eng, "T", afm_free)
                    C.inc(eng.tensor_copy(out=afm[:, h * 512:(h + 1) * 512],
                                          in_=trview[:, 0:512]), "V")

                pend_tr.append((emit_attn_tr, emit_attn_ev))
                if h == NPAIR - 1:
                    while pend_tr:
                        etr, eev = pend_tr.pop(0)
                        blk.tensor(etr)
                        tok_trp = C.now("T")
                        blk.vector(lambda eng, eev=eev, tok_trp=tok_trp: eev(
                            eng, tok_trp=tok_trp))
                        trv_consumed = C.now("V")
            tok_afm = C.now("V")
            mask_free = C.now("V")

            for tt in range(4):
                tile = g * 4 + tt

                def _(eng, tile=tile):
                    C.wait(eng, "V", xm_freeV)
                    C.incd(eng.dma_start(out=xo[:], in_=xown[tile * P:(tile + 1) * P, :]), "xo2")
                blk.sync(_)
                tok_xo = C.now("xo2")

                for dcol in range(4):
                    wops = shps
                    def _(eng, tt=tt, dcol=dcol, wops=wops, tok_afm=tok_afm,
                          ypf=yps_free2[dcol % 2]):
                        C.wait(eng, "V", tok_afm)
                        C.wait(eng, "wo2", tok_wo)
                        C.wait(eng, "V", ypf)
                        for fc in range(16):
                            ins = eng.matmul(
                                out=wops[:],
                                lhsT=afm[:, fc * 512 + tt * 128:fc * 512 + (tt + 1) * 128],
                                rhs=wo_sb[:, fc * D + dcol * 512:fc * D + (dcol + 1) * 512],
                                start=(fc == 0), stop=(fc == 15))
                        C.inc(ins, "T")
                    blk.tensor(_)
                    tok_y = C.now("T")

                    def _(eng, dcol=dcol, wops=wops, tok_y=tok_y):
                        C.wait(eng, "T", tok_y)
                        C.wait(eng, "V", xm_freeV)
                        C.inc(eng.tensor_copy(out=ysb[:, dcol * 512:(dcol + 1) * 512],
                                              in_=wops[:]), "V")
                    blk.vector(_)
                    yps_free2[0] = yps_free2[1] = C.now("V")
                    yps_free = C.now("V")
                tok_ye = C.now("V")

                def _(eng, tok_ye=tok_ye):
                    C.wait(eng, "V", tok_ye)
                    for i in range(4):
                        ins = eng.bn_stats(out=bn6y[:, 6 * i:6 * i + 6],
                                           in_=ysb[:, 512 * i:512 * (i + 1)])
                    t = C.inc(ins, "V")
                    C.wait(eng, "V", t)
                    C.inc(eng.bn_aggr(out=mvy[:], in_=bn6y[:]), "V")
                blk.vector(_)
                tok_mvy = C.now("V")

                def _(eng, tok_mvy=tok_mvy):
                    C.wait(eng, "V", tok_mvy)
                    t = C.inc(eng.activation(out=rsy[:, 0:1], in_=mvy[:, 1:2],
                                             func=AF.Ln, bias=epsap[:]), "S")
                    C.wait(eng, "S", t)
                    C.inc(eng.activation(out=rsy[:, 1:2], in_=rsy[:, 0:1],
                                         func=AF.Exp, scale=-0.5), "S")
                blk.scalar(_)
                tok_rsy = C.now("S")

                def _(eng, tok_rsy=tok_rsy, tok_xo=tok_xo):
                    C.wait(eng, "S", tok_rsy)
                    C.wait(eng, "xo2", tok_xo)
                    C.wait(eng, "xmw", xm_freeDs)
                    t = C.inc(eng.tensor_scalar(out=xm[:], in0=ysb[:],
                                                scalar1=mvy[:, 0:1], scalar2=rsy[:, 1:2],
                                                op0=ALU.subtract, op1=ALU.mult), "V")
                    C.wait(eng, "V", t)
                    C.inc(eng.tensor_tensor(out=xm[:], in0=xm[:], in1=xo[:], op=ALU.add), "V")
                blk.vector(_)
                tok_xm = C.now("V")
                xm_freeV = tok_xm

                def _(eng, tile=tile, tok_xm=tok_xm):
                    C.wait(eng, "V", tok_xm)
                    C.incd(eng.dma_start(out=xmid_scr[tile], in_=xm[:]), "xmw")
                blk.sync(_)
                xm_freeDs = C.now("xmw")
            afm_free = C.now("T")

    es2.close()

    es3 = contextlib.ExitStack()
    s3 = lambda name, shape, dt: es3.enter_context(nc.sbuf_tensor(name, shape, dt))

    xfm2 = s3("p3_xfm2", [P, 16 * 1024], BF16)
    acc = s3("p3_acc", [P, 16 * 1024], F32)
    gT = s3("p3_gT", [16, 1024], BF16)
    m3sb = s3("p3_m3", [P, 8], F32)
    r3sb = s3("p3_r3", [P, 8], F32)
    centT_sb = s3("p3_cent", [P, 16 * NRE], BF16)
    eoh_sb = s3("p3_eoh", [16, NRE * P], BF16)
    b1_sb = s3("p3_b1", [P, NEXP * 11], F32)
    b2r_sb = s3("p3_b2r", [16, D], BF16)
    sb2_sb = s3("p3_sb2", [NSH, D], BF16)
    bias_sb = s3("p3_bias", [P, NRE], F32)

    with nc.Block("ph3pre") as blk:
        def _(eng):
            C.wait(eng, "xmw", C.now("xmw"))
            C.incd(eng.dma_start(out=centT_sb.ap().rearrange("p (c e) -> p c e", c=16),
                                 in_=centT.rearrange("(c p) e -> p c e", p=P)), "c3g")
            C.incd(eng.dma_start(out=eoh_sb[:], in_=eoh_in), "c3g")
            C.incd(eng.dma_start(out=b2r_sb[:], in_=b2r), "c3g")
            C.incd(eng.dma_start(out=sb2_sb[:], in_=sb2), "c3g")
        blk.gpsimd(_)

        def _(eng):
            C.wait(eng, "xmw", C.now("xmw"))
            C.incd(eng.dma_start(out=b1_sb[:], in_=b1all), "c3s")
            C.incd(eng.dma_start(out=bias_sb[:], in_=biasbc), "c3s")
        blk.sync(_)
    tok_p3c = (C.now("c3g"), C.now("c3s"))

    for g2 in range(2):
        esA = contextlib.ExitStack()
        sA = lambda name, shape, dt: esA.enter_context(nc.sbuf_tensor(name, shape, dt))
        pA = lambda name, shape, dt=F32: esA.enter_context(nc.psum_tensor(name, shape, dt))
        xmt = [sA(f"p3a_xm{g2}_{i}", [P, D], F32) for i in range(4)]
        x2bf = [sA(f"p3a_x2{g2}_{i}", [P, D], BF16) for i in range(2)]
        bn6b = [sA(f"p3a_bn6{g2}_{i}", [P, 24], F32) for i in range(4)]
        mvall = sA(f"p3a_mva{g2}", [P, 16], F32)
        gsc = sA(f"p3a_gsc{g2}", [P, 640], F32)
        gall = sA(f"p3a_gall{g2}", [P, 8 * 16], BF16)
        psT3 = [pA(f"p3a_psT{g2}_{i}", [P, 1024], BF16) for i in range(2)]
        g_ps = [pA(f"p3a_gps{g2}_{i}", [P, 16]) for i in range(2)]
        gt_ps = pA(f"p3a_gtps{g2}", [16, P], BF16)

        with nc.Block(f"ph3a_{g2}") as blk:
            xmt_free = [0, 0, 0, 0]
            psT3_free = [0, 0]
            x2_free = [0, 0]
            stat_toks = []
            for tt in range(8):
                tile = g2 * 8 + tt
                buf = tt % 4

                def _(eng, tile=tile, buf=buf, tt=tt):
                    if tt == 0:
                        C.wait(eng, "yo0", C.now("yo0"))
                        C.wait(eng, "yo1", C.now("yo1"))
                    C.wait(eng, "V", xmt_free[buf])
                    C.incd(eng.dma_start(out=xmt[buf][:], in_=xmid_scr[tile]), f"xma{buf}")
                blk.sync(_)
                tok_x = C.now(f"xma{buf}")

                def _(eng, tt=tt, buf=buf, tok_x=tok_x):
                    C.wait(eng, f"xma{buf}", tok_x)
                    for i in range(4):
                        ins = eng.bn_stats(out=bn6b[buf][:, 6 * i:6 * i + 6],
                                           in_=xmt[buf][:, 512 * i:512 * (i + 1)])
                    t = C.inc(ins, "V")
                    C.wait(eng, "V", t)
                    C.inc(eng.bn_aggr(out=mvall[:, tt * 2:tt * 2 + 2], in_=bn6b[buf][:]), "V")
                blk.vector(_)
                xmt_free[buf] = C.now("V")
                stat_toks.append(C.now("V"))

            def _(eng):
                C.wait(eng, "V", stat_toks[-1])
                vr = mvall.ap().rearrange("p (t two) -> p t two", two=2)[:, :, 1:2]
                t = C.inc(eng.activation(out=gsc[:, 512:520], in_=vr,
                                         func=AF.Ln, bias=epsap[:]), "S")
                C.wait(eng, "S", t)
                C.inc(eng.activation(out=r3sb[:, 0:8], in_=gsc[:, 512:520],
                                     func=AF.Exp, scale=-0.5), "S")
            blk.scalar(_)
            tok_r3 = C.now("S")

            def _(eng):
                C.wait(eng, "S", tok_r3)
                mcols = mvall.ap().rearrange("p (t two) -> p t two", two=2)[:, :, 0:1]
                C.inc(eng.tensor_copy(out=m3sb.ap().unsqueeze(2), in_=mcols), "V")
            blk.vector(_)
            tok_m3 = C.now("V")

            xmt_free2 = list(xmt_free)
            gexp_toks = []
            for tt in range(8):
                tile = g2 * 8 + tt
                buf = tt % 4
                b2f = tt % 2

                def _(eng, tile=tile, buf=buf, tt=tt):
                    C.wait(eng, "V", xmt_free2[buf])
                    C.incd(eng.dma_start(out=xmt[buf][:], in_=xmid_scr[tile]), f"xma{buf}")
                blk.sync(_)
                tok_x = C.now(f"xma{buf}")

                def _(eng, tt=tt, buf=buf, b2f=b2f, tok_x=tok_x):
                    C.wait(eng, f"xma{buf}", tok_x)
                    C.wait(eng, "V", tok_m3)
                    C.wait(eng, "T", x2_free[b2f])
                    C.inc(eng.tensor_scalar(out=x2bf[b2f][:], in0=xmt[buf][:],
                                            scalar1=m3sb[:, tt:tt + 1],
                                            scalar2=r3sb[:, tt:tt + 1],
                                            op0=ALU.subtract, op1=ALU.mult), "V")
                blk.vector(_)
                xmt_free2[buf] = C.now("V")
                tok_x2 = C.now("V")

                def _(eng, tt=tt, b2f=b2f, tok_x2=tok_x2):
                    C.wait(eng, "V", tok_x2)
                    C.wait(eng, "V", psT3_free[0] if b2f == 0 else psT3_free[1])
                    for c in range(8):
                        ins = eng.transpose(out=psT3[b2f][:, c * 128:(c + 1) * 128],
                                            in_=x2bf[b2f][:, c * 128:(c + 1) * 128],
                                            identity=ident[:])
                    C.inc(ins, "T")
                blk.tensor(_)
                tok_tr1 = C.now("T")

                def _(eng, tt=tt, b2f=b2f, tok_tr1=tok_tr1):
                    xfm23d = xfm2.ap().rearrange("p (c f) -> p c f", f=1024)
                    C.wait(eng, "T", tok_tr1)
                    C.inc(eng.tensor_copy(out=xfm23d[:, 0:8, tt * 128:(tt + 1) * 128],
                                          in_=psT3[b2f].ap().rearrange(
                                              "p (c f) -> p c f", f=128)), "V")
                blk.vector(_)
                tok_ev1 = C.now("V")

                def _(eng, tt=tt, b2f=b2f, tok_ev1=tok_ev1):
                    C.wait(eng, "V", tok_ev1)
                    for c in range(8):
                        ins = eng.transpose(out=psT3[b2f][:, c * 128:(c + 1) * 128],
                                            in_=x2bf[b2f][:, (8 + c) * 128:(9 + c) * 128],
                                            identity=ident[:])
                    C.inc(ins, "T")
                blk.tensor(_)
                tok_tr2 = C.now("T")
                x2_free[b2f] = tok_tr2

                def _(eng, tt=tt, b2f=b2f, tok_tr2=tok_tr2):
                    xfm23d = xfm2.ap().rearrange("p (c f) -> p c f", f=1024)
                    C.wait(eng, "T", tok_tr2)
                    C.inc(eng.tensor_copy(out=xfm23d[:, 8:16, tt * 128:(tt + 1) * 128],
                                          in_=psT3[b2f].ap().rearrange(
                                              "p (c f) -> p c f", f=128)), "V")
                blk.vector(_)
                psT3_free[b2f] = C.now("V")
                tok_ev2 = C.now("V")

                def _(eng, tt=tt, b2f=b2f, tok_ev2=tok_ev2):
                    C.wait(eng, "V", tok_ev2)
                    C.wait(eng, "c3g", tok_p3c[0])
                    C.wait(eng, "S", gexp_toks[tt - 2] if tt >= 2 else 0)
                    for dc in range(16):
                        ins = eng.matmul(out=g_ps[b2f][:],
                                         lhsT=xfm2[:, dc * 1024 + tt * 128:dc * 1024 + (tt + 1) * 128],
                                         rhs=centT_sb[:, dc * 16:(dc + 1) * 16],
                                         start=(dc == 0), stop=(dc == 15))
                    C.inc(ins, "T")
                blk.tensor(_)
                tok_gmm = C.now("T")

                def _(eng, tt=tt, b2f=b2f, tok_gmm=tok_gmm):
                    C.wait(eng, "T", tok_gmm)
                    C.inc(eng.copy(out=gsc[:, 128 + tt * 16:128 + (tt + 1) * 16],
                                   in_=g_ps[b2f][:]), "S")
                blk.scalar(_)
                gexp_toks.append(C.now("S"))

            def _(eng):
                C.wait(eng, "S", gexp_toks[-1])
                C.inc(eng.activation(out=gsc[:, 0:128], in_=gsc[:, 128:256],
                                     func=AF.Exp, scale=-1.0), "S")
            blk.scalar(_)
            tok_sg = C.now("S")

            def _(eng, tok_sg=tok_sg):
                sc3 = gsc.ap()[:, 0:128].rearrange("p (t e) -> p t e", e=16)
                msk3 = gsc.ap()[:, 256:384].rearrange("p (t e) -> p t e", e=16)
                gate3 = gsc.ap()[:, 384:512].rearrange("p (t e) -> p t e", e=16)
                bias3 = bias_sb.ap().unsqueeze(1).broadcast_to([P, 8, NRE])
                C.wait(eng, "S", tok_sg)
                C.wait(eng, "c3s", tok_p3c[1])
                t = C.inc(eng.tensor_scalar(out=gsc[:, 0:128], in0=gsc[:, 0:128],
                                            scalar1=1.0, scalar2=None, op0=ALU.add), "V")
                C.wait(eng, "V", t)
                t = C.inc(eng.reciprocal(out=gsc[:, 0:128], in_=gsc[:, 0:128]), "V")
                C.wait(eng, "V", t)
                t = C.inc(eng.tensor_tensor(out=sc3, in0=sc3, in1=bias3, op=ALU.add), "V")
                C.wait(eng, "V", t)
                t = C.inc(eng.memset(gsc[:, 384:512], 0.0), "V")
                C.wait(eng, "V", t)
                val3 = gsc.ap()[:, 128:256].rearrange("p (t e) -> p t e", e=16)
                for it in range(4):
                    t = C.inc(eng.reduce_max(out=gsc[:, 528:536], in_=sc3, axis=AX.X), "V")
                    C.wait(eng, "V", t)
                    m8bc = gsc.ap()[:, 528:536].unsqueeze(2).broadcast_to([P, 8, 16])
                    t = C.inc(eng.tensor_tensor(out=msk3, in0=sc3, in1=m8bc,
                                                op=ALU.is_equal), "V")
                    C.wait(eng, "V", t)
                    t = C.inc(eng.tensor_tensor(out=val3, in0=msk3, in1=m8bc,
                                                op=ALU.mult), "V")
                    C.wait(eng, "V", t)
                    t = C.inc(eng.tensor_tensor(out=gate3, in0=gate3,
                                                in1=val3, op=ALU.add), "V")
                    C.wait(eng, "V", t)
                    t = C.inc(eng.scalar_tensor_tensor(out=gsc[:, 0:128], in0=gsc[:, 256:384],
                                                       scalar=-1e30, in1=gsc[:, 0:128],
                                                       op0=ALU.mult, op1=ALU.add), "V")
                    C.wait(eng, "V", t)
                t = C.inc(eng.reduce_sum(out=gsc[:, 536:544], in_=gate3, axis=AX.X), "V")
                C.wait(eng, "V", t)
                t = C.inc(eng.tensor_scalar(out=gsc[:, 536:544], in0=gsc[:, 536:544],
                                            scalar1=1e-8, scalar2=None, op0=ALU.max), "V")
                C.wait(eng, "V", t)
                t = C.inc(eng.reciprocal(out=gsc[:, 544:552], in_=gsc[:, 536:544]), "V")
                C.wait(eng, "V", t)
                rg3 = gsc.ap()[:, 544:552].unsqueeze(2).broadcast_to([P, 8, 16])
                t = C.inc(eng.tensor_tensor(out=gall.ap().rearrange("p (t e) -> p t e", e=16),
                                            in0=gate3, in1=rg3, op=ALU.mult), "V")
                C.wait(eng, "V", t)
            blk.vector(_)
            tok_gate = C.now("V")

            gtps_freeV = 0
            for tt in range(8):
                def _(eng, tt=tt, tok_gate=tok_gate):
                    C.wait(eng, "V", tok_gate)
                    C.wait(eng, "V", gtps_freeV)
                    C.inc(eng.transpose(out=gt_ps[:],
                                        in_=gall[:, tt * 16:(tt + 1) * 16],
                                        identity=ident[:]), "T")
                blk.tensor(_)
                tok_gt = C.now("T")

                def _(eng, tt=tt, tok_gt=tok_gt):
                    C.wait(eng, "T", tok_gt)
                    C.inc(eng.tensor_copy(out=gT[:, tt * 128:(tt + 1) * 128],
                                          in_=gt_ps[:]), "V")
                blk.vector(_)
                gtps_freeV = C.now("V")
        tok_gT = C.now("V")
        tok_xfm2 = C.now("V")
        esA.close()

        esB = contextlib.ExitStack()
        sB = lambda name, shape, dt: esB.enter_context(nc.sbuf_tensor(name, shape, dt))
        pB = lambda name, shape, dt=F32: esB.enter_context(nc.psum_tensor(name, shape, dt))
        w1A = sB(f"p3b_w1A{g2}", [P, 16 * W1A], BF16)
        w1B = sB(f"p3b_w1B{g2}", [P, 16 * W1B], BF16)
        w2c = [sB(f"p3b_w2c{g2}_{i}", [P, 11 * 256], BF16) for i in range(2)]
        hg = sB(f"p3b_hg{g2}", [P, 11 * 1024], BF16)
        htmp = sB(f"p3b_htmp{g2}", [P, 1024], BF16)
        gbc_sb = sB(f"p3b_gbc{g2}", [P, 1024], BF16)
        h_ps = [pB(f"p3b_hps{g2}_{i}", [P, 1024]) for i in range(2)]
        y2_ps = [pB(f"p3b_yps{g2}_{i}", [P, 1024]) for i in range(2)]

        def w1_src(j, half):
            w = rw1[j] if j < NRE else sw1[j - NRE]
            lo = 0 if half == 0 else W1A
            width = W1A if half == 0 else W1B
            return w.rearrange("(c p) f -> p c f", p=P)[:, :, lo:lo + width]

        def w1_dst(half):
            buf = w1A if half == 0 else w1B
            width = W1A if half == 0 else W1B
            return buf.ap().rearrange("p (c f) -> p c f", f=width)

        with nc.Block(f"ph3b_{g2}") as blk:
            w1_free = [0, 0]
            w1_tok = {}
            w2_free = [0, 0]
            w2_tok = {}
            hps_free = [0, 0]
            yps3_free = [("V", 0), ("V", 0)]
            htmp_free = 0
            hg_free = 0
            gbc_freeV = 0

            def _(eng):
                C.wait(eng, "yo0", C.now("yo0"))
                C.wait(eng, "yo1", C.now("yo1"))
                C.incd(eng.dma_start(out=w1_dst(0), in_=w1_src(0, 0)), "w1A")
                w1_tok[(0, 0)] = C.now("w1A")
            blk.gpsimd(_)

            for j in range(NEXP):
                routed = j < NRE

                def _(eng, j=j):
                    C.wait(eng, "T", w1_free[1])
                    C.incd(eng.dma_start(out=w1_dst(1), in_=w1_src(j, 1)), "w1B")
                    w1_tok[(j, 1)] = C.now("w1B")
                blk.gpsimd(_)

                if routed:
                    def _(eng, j=j):
                        C.wait(eng, "V", tok_gT)
                        C.wait(eng, yps3_free[0][0], yps3_free[0][1])
                        ins = eng.matmul(out=y2_ps[0][:, 0:512],
                                         lhsT=eoh_sb[:, j * 128:(j + 1) * 128],
                                         rhs=gT[:, 0:512], start=True, stop=True)
                        C.inc(ins, "T")
                        ins = eng.matmul(out=y2_ps[0][:, 512:1024],
                                         lhsT=eoh_sb[:, j * 128:(j + 1) * 128],
                                         rhs=gT[:, 512:1024], start=True, stop=True)
                        C.inc(ins, "T")
                    blk.tensor(_)
                    tok_gbcm = C.now("T")

                    def _(eng, tok_gbcm=tok_gbcm):
                        C.wait(eng, "T", tok_gbcm - 1)
                        C.wait(eng, "V", gbc_freeV)
                        C.inc(eng.copy(out=gbc_sb[:, 0:512], in_=y2_ps[0][:, 0:512]), "S")
                        C.wait(eng, "T", tok_gbcm)
                        C.inc(eng.copy(out=gbc_sb[:, 512:1024], in_=y2_ps[0][:, 512:1024]), "S")
                    blk.scalar(_)
                    yps3_free[0] = ("S", C.now("S"))
                    tok_gbc = C.now("S")
                else:
                    tok_gbc = 0

                for fc in range(11):
                    half = 0 if fc < FC_W1A else 1
                    fcl = fc if fc < FC_W1A else fc - FC_W1A
                    width = W1A if half == 0 else W1B
                    wbuf = w1A if half == 0 else w1B
                    hp = fc % 2

                    def _(eng, j=j, fc=fc, half=half, fcl=fcl, width=width,
                          wbuf=wbuf, hp=hp):
                        C.wait(eng, "w1A" if half == 0 else "w1B", w1_tok[(j, half)])
                        C.wait(eng, "V", tok_xfm2)
                        C.wait(eng, "S", hps_free[hp])
                        for ts_ in range(2):
                            for dc in range(16):
                                ins = eng.matmul(
                                    out=h_ps[hp][:, ts_ * 512:(ts_ + 1) * 512],
                                    lhsT=wbuf[:, dc * width + fcl * 128:dc * width + (fcl + 1) * 128],
                                    rhs=xfm2[:, dc * 1024 + ts_ * 512:dc * 1024 + (ts_ + 1) * 512],
                                    start=(dc == 0), stop=(dc == 15))
                        C.inc(ins, "T")
                        if fc == 10:
                            w1_free[1] = C.now("T")
                        if fc == FC_W1A - 1:
                            w1_free[0] = C.now("T")
                    blk.tensor(_)
                    tok_h = C.now("T")

                    if fc == FC_W1A - 1 and j + 1 < NEXP:
                        def _(eng, j=j):
                            C.wait(eng, "T", w1_free[0])
                            C.incd(eng.dma_start(out=w1_dst(0), in_=w1_src(j + 1, 0)), "w1A")
                            w1_tok[(j + 1, 0)] = C.now("w1A")
                        blk.gpsimd(_)

                    def _(eng, j=j, fc=fc, hp=hp, routed=routed, tok_h=tok_h):
                        C.wait(eng, "T", tok_h)
                        if routed:
                            C.wait(eng, "V", htmp_free)
                            dst = htmp[:]
                        else:
                            C.wait(eng, "T", hg_free)
                            dst = hg[:, fc * 1024:(fc + 1) * 1024]
                        C.inc(eng.activation(out=dst, in_=h_ps[hp][:], func=AF.Relu,
                                             bias=b1_sb[:, j * 11 + fc:j * 11 + fc + 1]), "S")
                    blk.scalar(_)
                    hps_free[hp] = C.now("S")
                    tok_relu = C.now("S")

                    if routed:
                        def _(eng, fc=fc, tok_relu=tok_relu, tok_gbc=tok_gbc):
                            C.wait(eng, "S", tok_relu)
                            C.wait(eng, "S", tok_gbc)
                            C.wait(eng, "T", hg_free)
                            C.inc(eng.tensor_tensor(out=hg[:, fc * 1024:(fc + 1) * 1024],
                                                    in0=htmp[:], in1=gbc_sb[:],
                                                    op=ALU.mult), "V")
                        blk.vector(_)
                        htmp_free = C.now("V")
                tok_hg = C.now("V") if routed else C.now("S")
                hg_wait_stream = "V" if routed else "S"

                if j == 0:
                    def _(eng, j=j):
                        C.wait(eng, "T", w2_free[0])
                        w2 = rw2[j] if j < NRE else sw2[j - NRE]
                        C.incd(eng.dma_start(
                            out=w2c[0].ap().rearrange("p (c f) -> p c f", f=256),
                            in_=w2.rearrange("(c p) d -> p c d", p=P)[:, :, 0:256]), "w2c0")
                        w2_tok[0] = C.now("w2c0")
                    blk.gpsimd(_)

                for dc in range(16):
                    cb = dc % 2
                    cw = (dc // 2) % 2
                    dcl = dc % 2

                    def _(eng, j=j, dc=dc):
                        if dc % 2 == 0:
                            if dc + 2 < 16:
                                jn, cols = j, (dc + 2) * 128
                            elif j + 1 < NEXP:
                                jn, cols = j + 1, 0
                            else:
                                return
                            nb = (dc // 2 + 1) % 2
                            C.wait(eng, "T", w2_free[nb])
                            w2 = rw2[jn] if jn < NRE else sw2[jn - NRE]
                            C.incd(eng.dma_start(
                                out=w2c[nb].ap().rearrange("p (c f) -> p c f", f=256),
                                in_=w2.rearrange("(c p) d -> p c d", p=P)[
                                    :, :, cols:cols + 256]), f"w2c{nb}")
                            w2_tok[nb] = C.now(f"w2c{nb}")
                    blk.gpsimd(_)

                    def _(eng, j=j, dc=dc, cb=cb, cw=cw, dcl=dcl, tok_hg=tok_hg,
                          hws=hg_wait_stream):
                        C.wait(eng, hws, tok_hg)
                        C.wait(eng, f"w2c{cw}", w2_tok[cw])
                        C.wait(eng, yps3_free[cb][0], yps3_free[cb][1])
                        for ts_ in range(2):
                            for fc in range(11):
                                ins = eng.matmul(
                                    out=y2_ps[cb][:, ts_ * 512:(ts_ + 1) * 512],
                                    lhsT=w2c[cw][:, fc * 256 + dcl * 128:fc * 256 + (dcl + 1) * 128],
                                    rhs=hg[:, fc * 1024 + ts_ * 512:fc * 1024 + (ts_ + 1) * 512],
                                    start=(fc == 0), stop=False,
                                    skip_group_check=True)
                            if j == 0:
                                ins = eng.matmul(
                                    out=y2_ps[cb][:, ts_ * 512:(ts_ + 1) * 512],
                                    lhsT=b2r_sb[:, dc * 128:(dc + 1) * 128],
                                    rhs=gT[:, ts_ * 512:(ts_ + 1) * 512],
                                    start=False, stop=False, skip_group_check=True)
                                ins = eng.matmul(
                                    out=y2_ps[cb][:, ts_ * 512:(ts_ + 1) * 512],
                                    lhsT=sb2_sb[:, dc * 128:(dc + 1) * 128],
                                    rhs=ones2[:, 0:512],
                                    start=False, stop=(ts_ == 1), skip_group_check=True)
                        C.inc(ins, "T")
                        if dcl == 1:
                            w2_free[cw] = C.now("T")
                    blk.tensor(_)
                    tok_y2 = C.now("T")

                    def _(eng, j=j, dc=dc, cb=cb, tok_y2=tok_y2):
                        C.wait(eng, "T", tok_y2)
                        if j == 0:
                            C.inc(eng.tensor_copy(out=acc[:, dc * 1024:(dc + 1) * 1024],
                                                  in_=y2_ps[cb][:]), "V")
                        else:
                            C.inc(eng.tensor_tensor(out=acc[:, dc * 1024:(dc + 1) * 1024],
                                                    in0=y2_ps[cb][:],
                                                    in1=acc[:, dc * 1024:(dc + 1) * 1024],
                                                    op=ALU.add), "V")
                        yps3_free[cb] = ("V", C.now("V"))
                    blk.vector(_)
                hg_free = C.now("T")
                if routed:
                    gbc_freeV = tok_hg
        tok_acc = C.now("V")
        esB.close()

        esC = contextlib.ExitStack()
        sC = lambda name, shape, dt: esC.enter_context(nc.sbuf_tensor(name, shape, dt))
        pC = lambda name, shape, dt=F32: esC.enter_context(nc.psum_tensor(name, shape, dt))
        z_sb = [sC(f"p3c_z{g2}_{i}", [P, D], F32) for i in range(2)]
        xmc = [sC(f"p3c_xm{g2}_{i}", [P, D], F32) for i in range(2)]
        x2f = [sC(f"p3c_x2f{g2}_{i}", [P, D], F32) for i in range(2)]
        osb = [sC(f"p3c_o{g2}_{i}", [P, D], F32) for i in range(2)]
        bn6c = [sC(f"p3c_bn6{g2}_{i}", [P, 24], F32) for i in range(2)]
        mvc = [sC(f"p3c_mv{g2}_{i}", [P, 2], F32) for i in range(2)]
        rsc = [sC(f"p3c_rs{g2}_{i}", [P, 2], F32) for i in range(2)]
        trf = [pC(f"p3c_tr{g2}_{i}", [P, 512]) for i in range(4)]

        with nc.Block(f"ph3c_{g2}") as blk:
            xmc_free = [0, 0]
            osb_free = [0, 0]
            trf_free = [0, 0, 0, 0]
            z_freeV = [0, 0]
            for tt in range(8):
                tile = g2 * 8 + tt
                buf = tt % 2

                def _(eng, tile=tile, buf=buf):
                    C.wait(eng, "V", xmc_free[buf])
                    C.incd(eng.dma_start(out=xmc[buf][:], in_=xmid_scr[tile]), f"xmc{buf}")
                blk.sync(_)
                tok_x = C.now(f"xmc{buf}")

                ztoks = []
                for half in range(4):
                    tp = 2 * buf + half % 2

                    def _(eng, tt=tt, half=half, tp=tp):
                        C.wait(eng, "S", trf_free[tp])
                        for c in range(4):
                            dcc = half * 4 + c
                            ins = eng.transpose(
                                out=trf[tp][:, c * 128:(c + 1) * 128],
                                in_=acc[:, dcc * 1024 + tt * 128:dcc * 1024 + (tt + 1) * 128],
                                identity=identf[:])
                        C.inc(ins, "T")
                    blk.tensor(_)
                    tok_tr = C.now("T")

                    def _(eng, buf=buf, half=half, tp=tp, tok_tr=tok_tr):
                        C.wait(eng, "T", tok_tr)
                        C.wait(eng, "V", z_freeV[buf])
                        C.inc(eng.copy(out=z_sb[buf][:, half * 512:(half + 1) * 512],
                                       in_=trf[tp][:]), "S")
                    blk.scalar(_)
                    trf_free[tp] = C.now("S")
                tok_z = C.now("S")

                def _(eng, tt=tt, buf=buf, tok_z=tok_z, tok_x=tok_x):
                    C.wait(eng, "S", tok_z)
                    C.wait(eng, f"xmc{buf}", tok_x)
                    t = C.inc(eng.tensor_scalar(out=x2f[buf][:], in0=xmc[buf][:],
                                                scalar1=m3sb[:, tt:tt + 1],
                                                scalar2=r3sb[:, tt:tt + 1],
                                                op0=ALU.subtract, op1=ALU.mult), "V")
                    C.wait(eng, "V", t)
                    t = C.inc(eng.tensor_tensor(out=z_sb[buf][:], in0=z_sb[buf][:],
                                                in1=x2f[buf][:], op=ALU.add), "V")
                    C.wait(eng, "V", t)
                    for i in range(4):
                        ins = eng.bn_stats(out=bn6c[buf][:, 6 * i:6 * i + 6],
                                           in_=z_sb[buf][:, 512 * i:512 * (i + 1)])
                    t = C.inc(ins, "V")
                    C.wait(eng, "V", t)
                    C.inc(eng.bn_aggr(out=mvc[buf][:], in_=bn6c[buf][:]), "V")
                blk.vector(_)
                tok_mvz = C.now("V")

                def _(eng, buf=buf, tok_mvz=tok_mvz):
                    C.wait(eng, "V", tok_mvz)
                    t = C.inc(eng.activation(out=rsc[buf][:, 0:1], in_=mvc[buf][:, 1:2],
                                             func=AF.Ln, bias=epsap[:]), "S")
                    C.wait(eng, "S", t)
                    C.inc(eng.activation(out=rsc[buf][:, 1:2], in_=rsc[buf][:, 0:1],
                                         func=AF.Exp, scale=-0.5), "S")
                blk.scalar(_)
                tok_rsz = C.now("S")

                def _(eng, buf=buf, tok_rsz=tok_rsz):
                    C.wait(eng, "S", tok_rsz)
                    C.wait(eng, f"yo{buf}", osb_free[buf])
                    t = C.inc(eng.tensor_scalar(out=osb[buf][:], in0=z_sb[buf][:],
                                                scalar1=mvc[buf][:, 0:1],
                                                scalar2=rsc[buf][:, 1:2],
                                                op0=ALU.subtract, op1=ALU.mult), "V")
                    C.wait(eng, "V", t)
                    t = C.inc(eng.tensor_tensor(out=osb[buf][:], in0=osb[buf][:],
                                                in1=xmc[buf][:], op=ALU.add), "V")
                blk.vector(_)
                z_freeV[buf] = C.now("V")
                xmc_free[buf] = C.now("V")
                tok_o = C.now("V")

                def _(eng, tile=tile, buf=buf, tok_o=tok_o):
                    C.wait(eng, "V", tok_o)
                    C.incd(eng.dma_start(out=yout[tile * P:(tile + 1) * P, :],
                                         in_=osb[buf][:]), f"yo{buf}")
                blk.sync(_)
                osb_free[buf] = C.now(f"yo{buf}")
        esC.close()
    es3.close()

    with nc.Block("fin") as blk:
        if debug:
            def _(eng):
                for key in ("kst0", "kst1", "vst0", "vst1", "qst0", "qst1", "xmw"):
                    C.wait(eng, key, C.now(key))
                for i in range(16):
                    C.incd(eng.dma_start(out=dbg_xmid[i], in_=xmid_scr[i]), "dbg")
                for i in range(NPAIR):
                    C.incd(eng.dma_start(out=dbg_q[i], in_=q_scr[i]), "dbg")
                for i in range(4):
                    C.incd(eng.dma_start(out=dbg_k[i], in_=k_scr[i]), "dbg")
                for i in range(32):
                    C.incd(eng.dma_start(out=dbg_v[i], in_=v_scr[i]), "dbg")
                C.wait(eng, "dbg", C.now("dbg"))
            blk.gpsimd(_)

        def _(eng):
            C.wait(eng, "yo0", C.now("yo0"))
            C.wait(eng, "yo1", C.now("yo1"))
        blk.sync(_)

    es_glob.close()
    return nc



def _host_inputs(inputs):
    f32 = np.float32
    bf16 = ml_dtypes.bfloat16
    d = {}
    d["wqT"] = np.ascontiguousarray(inputs["L0_attn_Wq"].T)
    d["wkT"] = np.ascontiguousarray(inputs["L0_attn_Wk"].T)
    d["wvT"] = np.ascontiguousarray(inputs["L0_attn_Wv"].T)
    d["woT"] = np.ascontiguousarray(inputs["L0_attn_Wo"].T)
    d["centT"] = np.ascontiguousarray(inputs["L0_ffn_centroids"].T)
    d["rw1"] = np.ascontiguousarray(inputs["L0_ffn_routed_w1"])
    d["rw2"] = np.ascontiguousarray(inputs["L0_ffn_routed_w2"])
    d["sw1"] = np.ascontiguousarray(inputs["L0_ffn_shared_w1"])
    d["sw2"] = np.ascontiguousarray(inputs["L0_ffn_shared_w2"])
    b1 = np.concatenate([inputs["L0_ffn_routed_b1"], inputs["L0_ffn_shared_b1"]], 0)
    d["b1all"] = np.ascontiguousarray(
        b1.reshape(NEXP, 11, P).transpose(2, 0, 1).reshape(P, NEXP * 11)).astype(f32)
    d["b2r"] = np.ascontiguousarray(inputs["L0_ffn_routed_b2"])
    d["sb2"] = np.ascontiguousarray(inputs["L0_ffn_shared_b2"])
    d["biasbc"] = np.ascontiguousarray(
        np.broadcast_to(inputs["L0_ffn_bias"][None, :], (P, NRE))).astype(f32)
    d["lnwbc"] = np.ascontiguousarray(
        np.broadcast_to(inputs["L0_attn_ln_w"][None, :], (P, VD))).astype(f32)
    d["lamvec"] = np.concatenate(
        [inputs["L0_attn_lambda_q1"], inputs["L0_attn_lambda_k1"],
         inputs["L0_attn_lambda_q2"], inputs["L0_attn_lambda_k2"]]).reshape(1, 256).astype(f32)
    d["ident"] = np.eye(P, dtype=bf16)
    eoh = np.zeros((NRE, NRE * P), dtype=bf16)
    for j in range(NRE):
        eoh[j, j * P:(j + 1) * P] = 1.0
    d["eoh"] = eoh
    return d


def _masks_for(parity):
    astart = [0, 512, 3072, 3584] if parity == 0 else [1024, 1536, 2048, 2560]
    m = np.zeros((4, NMASK, P, 512), dtype=np.float32)
    r = np.arange(P)[:, None]
    q = np.arange(512)[None, :]
    for g in range(4):
        for si in range(NMASK):
            slot = ESLOT[g] - NMASK + si
            kabs = slot * 128 + r
            qabs = astart[g] + q
            m[g, si] = (kabs <= qabs)
    return m.astype(ml_dtypes.bfloat16)


def _own_idx(parity):
    chunks = (0, 3) if parity == 0 else (1, 2)
    return np.concatenate([np.arange(c * 1024, (c + 1) * 1024) for c in chunks])


def kernel(**inputs):
    x = inputs["L0_attn_x"]
    shared = _host_inputs(inputs)
    masks = [_masks_for(0), _masks_for(1)]

    nc = build_program()

    in_maps = []
    for c in range(NCHIP):
        b, parity = c // 2, c % 2
        m = dict(shared)
        m["xseq"] = np.ascontiguousarray(x[b])
        m["xown"] = np.ascontiguousarray(x[b][_own_idx(parity)])
        m["mask01"] = masks[parity]
        in_maps.append(m)

    res = run_bass_kernel_spmd(nc, in_maps, core_ids=list(range(NCHIP)))

    out = np.empty((4, T, D), dtype=np.float32)
    for c in range(NCHIP):
        b, parity = c // 2, c % 2
        out[b, _own_idx(parity)] = res.results[c]["yout"]
    return out
